# version 28
# baseline (speedup 1.0000x reference)
import math
import contextlib
import numpy as np
import ml_dtypes
import concourse.bass as bass
import concourse.mybir as mybir
from concourse.bass_utils import run_bass_kernel_spmd

F32 = mybir.dt.float32
F32R = mybir.dt.float32r
BF16 = mybir.dt.bfloat16
AF = mybir.ActivationFunctionType
ALU = mybir.AluOpType
AX = mybir.AxisListType
BF = ml_dtypes.bfloat16

B, T, D = 2, 4096, 2048
H, DQ, DV = 16, 128, 128
HFF = 8192
NW = T // 512
OWN = 1024
CAP = 30.0
CSC = float(np.float32(1.0 / (math.sqrt(DQ) * CAP)))
NEG = -1.0e30
LN_EPS = 1e-5
RMS_EPS = float(np.finfo(np.float32).eps)
MAGIC = 12582912.0
QCLIP = 1.4375

COMBOS = [(qg, kc) for qg in range(8) for kc in range(4 * qg + 4)]
NCOMB = len(COMBOS)
CIDX = {c: i for i, c in enumerate(COMBOS)}


def _host_consts(g):
    o = {}
    invf = 1.0 / (10000.0 ** (np.arange(0, DQ, 2) / DQ))
    pos = np.arange(T, dtype=np.float64)
    fr2 = np.repeat(pos[None, :] * invf[:, None], 2, axis=0)
    o["cos_t"] = np.cos(fr2).astype(np.float32)
    o["sin_t"] = np.sin(fr2).astype(np.float32)
    P = np.zeros((DQ, DQ), np.float32)
    for i in range(DQ // 2):
        P[2 * i, 2 * i + 1] = -1.0
        P[2 * i + 1, 2 * i] = 1.0
    o["rotT"] = np.ascontiguousarray(P.T.astype(BF))
    o["ident_bf"] = np.eye(128, dtype=np.float32).astype(BF)
    o["ident_f32"] = np.eye(128, dtype=np.float32)
    m = np.zeros((128, 4 * 512), np.float64)
    pp = np.arange(128)[:, None]
    dqi = np.arange(512)[None, :]
    for r in range(4):
        m[:, r * 512:(r + 1) * 512] = np.where(pp + r * 128 > dqi, NEG, 0.0)
    o["masks"] = m.astype(np.float32)
    slopes = [2.0 ** (-(i + 1) * 8.0 / H) for i in range(H)]
    bias = np.zeros((128, 4 * NCOMB), np.float64)
    rhs = np.zeros((1, 4 * 512), np.float64)
    p64 = np.arange(128, dtype=np.float64)
    d64 = np.arange(512, dtype=np.float64)
    for hl in range(4):
        s = slopes[4 * g + hl]
        for ci, (qg, kc) in enumerate(COMBOS):
            bias[:, hl * NCOMB + ci] = CSC * s * (kc * 128 + p64 - qg * 512 - 256)
        rhs[0, hl * 512:(hl + 1) * 512] = -CSC * s * (d64 - 256)
    o["alibias"] = bias.astype(np.float32)
    o["aliq"] = rhs.astype(np.float32)
    o["ones_r"] = np.ones((1, 128), np.float32)
    o["eps_ln"] = np.full((128, 1), LN_EPS, np.float32)
    o["eps_rms"] = np.full((128, 1), RMS_EPS, np.float32)
    o["magicb"] = np.full((128, 1), MAGIC, np.float32)
    o["magicnb"] = np.full((128, 1), -MAGIC, np.float32)
    o["ones_col"] = np.ones((128, 1), np.float32).astype(BF)
    return o


class EQ:
    def __init__(self):
        self.items = []

    def add(self, fn):
        self.items.append(fn)


class Sem:
    def __init__(self, es, nc, name):
        self.h = es.enter_context(nc.semaphore(name))
        self.v = 0

    def w(self, q, val=None):
        tv = self.v if val is None else val
        q.add(lambda e, s=self.h, t=tv: e.wait_ge(s, t))
        return tv


class DS:

    def __init__(self, es, nc, name, k=2):
        self.sems = [Sem(es, nc, f"{name}{i}") for i in range(k)]
        self.k = k
        self.n = 0

    def emit(self, q, build):
        s = self.sems[self.n % self.k]
        if s.v > 0:
            q.add(lambda e, h=s.h, t=s.v: e.wait_ge(h, t))
        s.v += 16
        q.add(lambda e, b=build, h=s.h: b(e).then_inc(h, 16))
        self.n += 1

    def wait_upto(self, q, n):
        for j in range(self.k):
            need = (n - j + self.k - 1) // self.k
            if need > 0:
                q.add(lambda e, h=self.sems[j].h, t=16 * need: e.wait_ge(h, t))

    def wait_all(self, q):
        self.wait_upto(q, self.n)


def _op(q, build, sem=None, n=1):
    if sem is None:
        q.add(lambda e, b=build: b(e))
        return None
    sem.v += n
    q.add(lambda e, b=build, s=sem.h, k=n: b(e).then_inc(s, k))
    return sem.v


def _opm(q, build, sems):
    pairs = []
    for sem, n in sems:
        sem.v += n
        pairs.append((sem.h, n))

    def runner(e, b=build, pairs=tuple(pairs)):
        ins = b(e)
        for h, k in pairs:
            ins = ins.then_inc(h, k)
        return ins
    q.add(runner)


def build_nc(dbg=(), ph=9):
    dbg = set(dbg)
    nc = bass.Bass("TRN2", target_bir_lowering=False, debug=False)

    x_in = nc.dram_tensor("x", [T, D], F32, kind="ExternalInput").ap()
    xT_in = nc.dram_tensor("xT", [D, T], F32, kind="ExternalInput").ap()
    xown_in = nc.dram_tensor("x_own", [OWN, D], F32, kind="ExternalInput").ap()
    wq_in = nc.dram_tensor("wq", [D, 512], F32, kind="ExternalInput").ap()
    wk_in = nc.dram_tensor("wk", [D, DQ], F32, kind="ExternalInput").ap()
    wv_in = nc.dram_tensor("wv", [D, DV], F32, kind="ExternalInput").ap()
    wo_in = nc.dram_tensor("wo", [512, D], F32, kind="ExternalInput").ap()
    bo_in = nc.dram_tensor("bo_t", [128, 16], F32, kind="ExternalInput").ap()
    wgu_in = nc.dram_tensor("wgu_sh", [2048, D], F32, kind="ExternalInput").ap()
    wp_in = nc.dram_tensor("wp_sh", [D, 1024], F32, kind="ExternalInput").ap()
    cos_in = nc.dram_tensor("cos_t", [128, T], F32, kind="ExternalInput").ap()
    sin_in = nc.dram_tensor("sin_t", [128, T], F32, kind="ExternalInput").ap()
    rotT_in = nc.dram_tensor("rotT", [128, 128], BF16, kind="ExternalInput").ap()
    identb_in = nc.dram_tensor("ident_bf", [128, 128], BF16, kind="ExternalInput").ap()
    identf_in = nc.dram_tensor("ident_f32", [128, 128], F32, kind="ExternalInput").ap()
    masks_in = nc.dram_tensor("masks", [128, 4 * 512], F32, kind="ExternalInput").ap()
    alib_in = nc.dram_tensor("alibias", [128, 4 * NCOMB], F32, kind="ExternalInput").ap()
    aliq_in = nc.dram_tensor("aliq", [1, 4 * 512], F32R, kind="ExternalInput").ap()
    onesr_in = nc.dram_tensor("ones_r", [1, 128], F32R, kind="ExternalInput").ap()
    onesc_in = nc.dram_tensor("ones_col", [128, 1], BF16, kind="ExternalInput").ap()
    epsln_in = nc.dram_tensor("eps_ln", [128, 1], F32, kind="ExternalInput").ap()
    epsrms_in = nc.dram_tensor("eps_rms", [128, 1], F32, kind="ExternalInput").ap()
    magicb_in = nc.dram_tensor("magicb", [128, 1], F32, kind="ExternalInput").ap()
    magicnb_in = nc.dram_tensor("magicnb", [128, 1], F32, kind="ExternalInput").ap()

    out_ext = nc.dram_tensor("out", [D, OWN], F32, kind="ExternalOutput").ap()
    dbg_t = {}
    for name, shape, dt in dbg:
        dbg_t[name] = nc.dram_tensor("dbg_" + name, list(shape), dt,
                                     kind="ExternalOutput").ap()

    gshT_d = nc.dram_tensor("gshT_d", [2048, 2048], BF16).ap()
    gagT_d = nc.dram_tensor("gagT_d", [8 * 2048, 2048], BF16, addr_space="Shared").ap()
    pshT_d = nc.dram_tensor("pshT_d", [1024, 2048], BF16).ap()
    pagT_d = nc.dram_tensor("pagT_d", [8 * 1024, 2048], BF16, addr_space="Shared").ap()
    attnp_d = nc.dram_tensor("attnp_d", [4 * 4 * 512, 1024], F32).ap()
    attnr_d = nc.dram_tensor("attnr_d", [D, OWN], F32).ap()
    h_d = nc.dram_tensor("h_d", [HFF, OWN], BF16).ap()

    ENGS = ("sync", "vector", "scalar", "tensor", "gpsimd")
    q = {e: EQ() for e in ENGS}
    SY, VE, SC, TE, GP = (q[e] for e in ENGS)

    def _flush(blk):
        @blk.sync
        def _(e):
            for fn in SY.items:
                fn(e)

        @blk.vector
        def _(e):
            for fn in VE.items:
                fn(e)

        @blk.scalar
        def _(e):
            for fn in SC.items:
                fn(e)

        @blk.tensor
        def _(e):
            for fn in TE.items:
                fn(e)

        @blk.gpsimd
        def _(e):
            for fn in GP.items:
                fn(e)

    es = contextlib.ExitStack()
    with es:
        S = {}
        for n_ in ["app", "trp", "tev",
                   "mmq", "qev", "rot", "rope", "vmm", "vev",
                   "lnA", "lnB", "lnC", "lnD", "lnE",
                   "qd1", "qdg", "qrc", "qtt", "qte", "qmg", "ccA", "ccB",
                   "ms", "mt", "mm_", "mp", "pvd", "mce", "mct", "mcf",
                   "wo1", "wo2", "rs",
                   "xoA", "xoB", "xoC", "xoD", "xoE", "xoap", "xotr", "xotv",
                   "ff1", "ffs", "ffh", "ffq", "ffo",
                   "rsA", "rsB", "rsC",
                   "f2e", "f2f", "f2t", "f2a", "f2x", "f2m"]:
            S[n_] = Sem(es, nc, n_)
        ldc = DS(es, nc, "ldc", 4)
        ldc2 = DS(es, nc, "ldc2", 3)
        ldw = DS(es, nc, "ldw", 2)
        ldx = DS(es, nc, "ldx", 2)
        ldcs = DS(es, nc, "ldcs", 2)
        qg1 = DS(es, nc, "qg1", 2)
        qg2 = DS(es, nc, "qg2", 2)
        sast = DS(es, nc, "sast", 4)
        xol = DS(es, nc, "xol", 4)
        tg1 = DS(es, nc, "tg1", 2)
        tg2 = DS(es, nc, "tg2", 2)
        hst = DS(es, nc, "hst", 2)
        hld = DS(es, nc, "hld", 4)
        tp1 = DS(es, nc, "tp1", 4)
        axl = DS(es, nc, "axl", 2)
        fin = DS(es, nc, "fin", 2)
        dbgs = DS(es, nc, "dbgs", 1)

        blk = es.enter_context(nc.Block())

        rotT = es.enter_context(nc.sbuf_tensor("rotT_s", [128, 128], BF16))
        identb = es.enter_context(nc.sbuf_tensor("identb_s", [128, 128], BF16))
        identf = es.enter_context(nc.sbuf_tensor("identf_s", [128, 128], F32))
        onesr = es.enter_context(nc.sbuf_tensor("onesr_s", [1, 128], F32R))
        onesc = es.enter_context(nc.sbuf_tensor("onesc_s", [128, 1], BF16))
        bo_t = es.enter_context(nc.sbuf_tensor("bo_t_s", [128, 16], F32))
        epsln = es.enter_context(nc.sbuf_tensor("epsln_s", [128, 1], F32))
        epsrms = es.enter_context(nc.sbuf_tensor("epsrms_s", [128, 1], F32))
        magicb = es.enter_context(nc.sbuf_tensor("magicb_s", [128, 1], F32))
        magicnb = es.enter_context(nc.sbuf_tensor("magicnb_s", [128, 1], F32))
        esQW = contextlib.ExitStack()
        qw = esQW.enter_context(nc.sbuf_tensor("qw", [128, 4 * D], BF16, side="right"))
        qd = esQW.enter_context(nc.sbuf_tensor("qd", [128, 3 * D], BF16, side="right"))
        qsum = esQW.enter_context(nc.sbuf_tensor("qsum", [128, 32], F32, side="right"))
        qs_bf = esQW.enter_context(nc.sbuf_tensor("qs_bf", [128, 8 * 32], BF16, side="right"))
        qrcp = esQW.enter_context(nc.sbuf_tensor("qrcp", [128, 8 * 32], F32, side="right"))
        qdT = esQW.enter_context(nc.sbuf_tensor("qdT", [128, 2048], BF16, side="right"))
        qmagf = esQW.enter_context(nc.sbuf_tensor("qmagf", [128, 2048], F32, side="right"))

        for dst, srct in [(rotT, rotT_in), (identb, identb_in), (identf, identf_in),
                          (onesr, onesr_in), (onesc, onesc_in), (bo_t, bo_in),
                          (epsln, epsln_in), (epsrms, epsrms_in),
                          (magicb, magicb_in), (magicnb, magicnb_in)]:
            ldc.emit(SY, lambda e, d=dst, s=srct: e.dma_start(out=d[:], in_=s))

        esAtt = contextlib.ExitStack()
        q_fm = esAtt.enter_context(nc.sbuf_tensor("q_fm", [128, 4 * T], BF16))
        k_fm = esAtt.enter_context(nc.sbuf_tensor("k_fm", [128, T], BF16))
        v_tm = esAtt.enter_context(nc.sbuf_tensor("v_tm", [128, 32 * (DV + 1)], BF16))

        esP1 = contextlib.ExitStack()
        xwin = esP1.enter_context(nc.sbuf_tensor("xwin", [128, 4 * D], F32))
        xn_tm = esP1.enter_context(nc.sbuf_tensor("xn_tm", [128, 2 * D], BF16))
        xnw = esP1.enter_context(nc.sbuf_tensor("xnw", [128, 2 * 16 * 512], BF16))
        wq_s = esP1.enter_context(nc.sbuf_tensor("wq_s", [128, 16 * 512], BF16))
        wk_s = esP1.enter_context(nc.sbuf_tensor("wk_s", [128, 16 * 128], BF16))
        wv_s = esP1.enter_context(nc.sbuf_tensor("wv_s", [128, 16 * 128], BF16))
        cosw = esP1.enter_context(nc.sbuf_tensor("cosw", [128, 2 * 512], F32))
        sinw = esP1.enter_context(nc.sbuf_tensor("sinw", [128, 2 * 512], F32))
        lnsum = esP1.enter_context(nc.sbuf_tensor("lnsum", [128, 4], F32))
        lnsq = esP1.enter_context(nc.sbuf_tensor("lnsq", [128, 4], F32))
        lnmean = esP1.enter_context(nc.sbuf_tensor("lnmean", [128, 4], F32))
        lnm2 = esP1.enter_context(nc.sbuf_tensor("lnm2", [128, 4], F32))
        lnvar = esP1.enter_context(nc.sbuf_tensor("lnvar", [128, 4], F32))
        lnrs = esP1.enter_context(nc.sbuf_tensor("lnrs", [128, 4], F32))
        lnnmr = esP1.enter_context(nc.sbuf_tensor("lnnmr", [128, 4], F32))
        lnjunk = esP1.enter_context(nc.sbuf_tensor("lnjunk", [128, D], BF16))
        qst = esP1.enter_context(nc.sbuf_tensor("qst", [128, 512], BF16))
        rt1 = esP1.enter_context(nc.sbuf_tensor("rt1", [128, 512], F32))
        rt2 = esP1.enter_context(nc.sbuf_tensor("rt2", [128, 512], F32))

        esP1p = contextlib.ExitStack()
        ps_tr = [esP1p.enter_context(nc.psum_tensor(f"ps_tr{i}", [128, 1024], BF16))
                 for i in range(2)]
        ps_qkv = esP1p.enter_context(nc.psum_tensor("ps_qkv", [128, 512], F32))
        ps_rot = esP1p.enter_context(nc.psum_tensor("ps_rot", [128, 512], F32))
        ps_v = [esP1p.enter_context(nc.psum_tensor(f"ps_v{i}", [128, 512], F32))
                for i in range(2)]
        ps_qt = [esP1p.enter_context(nc.psum_tensor(f"ps_qt{i}", [128, 1024], BF16))
                 for i in range(2)]

        ldw.emit(GP, lambda e: e.dma_start(
            out=wq_s.ap().rearrange("p (c m) -> p c m", c=16),
            in_=wq_in.rearrange("(c p) m -> p c m", p=128)))
        ldw.emit(GP, lambda e: e.dma_start(
            out=wk_s.ap().rearrange("p (c m) -> p c m", c=16),
            in_=wk_in.rearrange("(c p) m -> p c m", p=128)))
        ldw.emit(GP, lambda e: e.dma_start(
            out=wv_s.ap().rearrange("p (c m) -> p c m", c=16),
            in_=wv_in.rearrange("(c p) m -> p c m", p=128)))
        def FC(t):
            return 2 * (t + 1) if t < 16 else t + 17

        nqf = [0]

        def emit_qtrans(qt):
            nblk = 16 if qt < 16 else 8
            dp = qt % 3
            S["qdg"].w(TE, QDG_MUL(qt))
            for fi in range(nblk // 8):
                bank = nqf[0] % 2
                if nqf[0] >= 2:
                    S["qte"].w(TE, nqf[0] - 1)
                for j in range(8):
                    _op(TE, lambda e, fi=fi, j=j, bank=bank, dp=dp: e.transpose(
                        ps_qt[bank][:, j * 128:(j + 1) * 128],
                        qd[:, dp * D + (fi * 8 + j) * 128:dp * D + (fi * 8 + j + 1) * 128],
                        identb[:]), S["qtt"] if j == 7 else None, 1)
                nqf[0] += 1
            for fi in range(nblk // 8):
                nf = nqf[0] - nblk // 8 + fi
                bank = nf % 2
                S["qtt"].w(SC, nf + 1)
                if fi == 0 and qt >= 1:
                    qg2.wait_upto(SC, qt)
                _op(SC, lambda e, fi=fi, bank=bank: e.copy(
                    out=qdT[:, fi * 1024:(fi + 1) * 1024], in_=ps_qt[bank][:]),
                    S["qte"], 1)
            if qt < 16:
                dstp = gshT_d.rearrange("(j p) c -> p j c", p=128)[
                    :, :, qt * 128:(qt + 1) * 128]
                srcp = qdT.ap().rearrange("p (j c) -> p j c", c=128)
            else:
                dstp = pshT_d.rearrange("(j p) c -> p j c", p=128)[
                    :, :, (qt - 16) * 128:(qt - 16 + 1) * 128]
                srcp = qdT.ap()[:, 0:1024].rearrange("p (j c) -> p j c", c=128)
            S["qte"].w(SC, nqf[0])
            qg2.emit(SC, lambda e, dstp=dstp, srcp=srcp: e.dma_start(
                out=dstp, in_=srcp))

        def qsrc(qt):
            return (wgu_in[qt * 128:(qt + 1) * 128, :] if qt < 16
                    else wp_in[(qt - 16) * 128:(qt - 16 + 1) * 128, :])

        def emit_qred(qt):
            wid = 2048 if qt < 16 else 1024
            qp, s8 = qt % 4, qt % 8
            qg1.wait_upto(VE, qt + 1)
            _op(VE, lambda e, wid=wid, qp=qp: e.tensor_reduce(
                out=qsum[:, 0:wid // 64], in_=qw.ap()[:, qp * D:qp * D + wid].rearrange(
                    "p (g e) -> p g e", e=64),
                axis=AX.X, op=ALU.add, apply_absolute_value=True), S["qd1"], 1)
            S["qd1"].w(VE)
            if qt >= 8:
                S["qdg"].w(VE, QDG_MUL(qt - 8))
            _op(VE, lambda e, ng=wid // 64, s8=s8: e.tensor_scalar(
                out=qs_bf[:, s8 * 32:s8 * 32 + ng], in0=qsum[:, 0:ng], scalar1=1.0 / 64,
                scalar2=1e-8, op0=ALU.mult, op1=ALU.max), S["qd1"], 1)
            S["qd1"].w(VE)
            _op(VE, lambda e, ng=wid // 64, s8=s8: e.reciprocal(
                out=qrcp[:, s8 * 32:s8 * 32 + ng], in_=qs_bf[:, s8 * 32:s8 * 32 + ng]),
                S["qrc"], 1)

        def emit_qmul(mq):
            wid = 2048 if mq < 16 else 1024
            ng = wid // 64
            s8, dp = mq % 8, mq % 3
            S["qmg"].w(GP, 2 * mq + 2)
            _op(GP, lambda e, wid=wid, ng=ng, s8=s8, dp=dp: e.tensor_tensor(
                out=qd[:, dp * D:dp * D + wid].rearrange("p (g e) -> p g e", e=64),
                in0=qd[:, dp * D:dp * D + wid].rearrange("p (g e) -> p g e", e=64),
                in1=qs_bf[:, s8 * 32:s8 * 32 + ng].unsqueeze(2).broadcast_to([128, ng, 64]),
                op=ALU.mult), S["qdg"], 1)
            S["qdg"].w(GP)

        def QDG_CLIP(t):
            if t == 0:
                return 2
            return 50 if t == 16 else 3 * t + 1

        def QDG_MUL(t):
            if t == 15:
                return 48
            return 96 if t == 31 else 3 * t + 5

        def emit_qmagic(qt):
            wid = 2048 if qt < 16 else 1024
            dp = qt % 3
            S["qdg"].w(SC, QDG_CLIP(qt))
            _op(SC, lambda e, wid=wid, dp=dp: e.activation(
                out=qmagf[:, 0:wid], in_=qd[:, dp * D:dp * D + wid],
                func=AF.Identity, scale=1.0, bias=magicb[:]), S["qmg"], 1)
            S["qmg"].w(SC)
            _op(SC, lambda e, wid=wid, dp=dp: e.activation(
                out=qd[:, dp * D:dp * D + wid], in_=qmagf[:, 0:wid],
                func=AF.Identity, scale=1.0, bias=magicnb[:]), S["qmg"], 1)

        for qt in range(4):
            qg1.emit(GP, lambda e, s=qsrc(qt), qp=qt % 4: e.dma_start(
                out=qw[:, qp * D:qp * D + 2048], in_=s))
        for qt in range(32):
            wid = 2048 if qt < 16 else 1024
            ng = wid // 64
            qp, s8, dp = qt % 4, qt % 8, qt % 3
            qg1.wait_upto(GP, qt + 1)
            S["qrc"].w(GP, qt + 1)
            if qt >= 3:
                S["qtt"].w(GP, FC(qt - 3))
            _op(GP, lambda e, wid=wid, ng=ng, qp=qp, s8=s8, dp=dp: e.tensor_tensor(
                out=qd[:, dp * D:dp * D + wid].rearrange("p (g e) -> p g e", e=64),
                in0=qw.ap()[:, qp * D:qp * D + wid].rearrange("p (g e) -> p g e", e=64),
                in1=qrcp[:, s8 * 32:s8 * 32 + ng].unsqueeze(2).broadcast_to([128, ng, 64]),
                op=ALU.mult), S["qdg"], 1)
            S["qdg"].w(GP)
            if qt + 4 < 32:
                S["qd1"].w(GP, 2 * qt + 1)
                qg1.emit(GP, lambda e, s=qsrc(qt + 4), qp=qp, w2=(2048 if qt + 4 < 16 else 1024): e.dma_start(
                    out=qw[:, qp * D:qp * D + w2], in_=s))
            _op(GP, lambda e, wid=wid, dp=dp: e.tensor_scalar(
                out=qd[:, dp * D:dp * D + wid], in0=qd[:, dp * D:dp * D + wid],
                scalar1=QCLIP, scalar2=-QCLIP,
                op0=ALU.min, op1=ALU.max), S["qdg"], 1)
            S["qdg"].w(GP)
            if qt >= 1 and qt != 16:
                emit_qmul(qt - 1)
            if qt == 15:
                emit_qmul(15)
            if qt == 31:
                emit_qmul(31)
            if qt == 15:
                qg2.wait_upto(GP, 16)
                _op(GP, lambda e: e.collective_compute(
                    "AllGather", ALU.bypass, replica_groups=[list(range(8))],
                    ins=[gshT_d.opt()], outs=[gagT_d.opt()]), S["ccA"], 1)
        S["qd1"].w(GP, 64)
        ldw.emit(GP, lambda e: e.dma_start(
            out=wo_s.ap().rearrange("p (c m) -> p c m", c=4),
            in_=wo_in.rearrange("(c p) m -> p c m", p=128)))

        ldc.wait_all(VE)
        _op(VE, lambda e: e.memset(v_tm[:], 1.0), S["vev"], 1)

        for w in range(NW):
            wpar = w % 2
            if w >= 1:
                S["app"].w(SY, w * 4)
            for i in range(4):
                t = 4 * w + i
                ldx.emit(SY, lambda e, i=i, t=t: e.dma_start(
                    out=xwin[:, i * D:(i + 1) * D], in_=x_in[t * 128:(t + 1) * 128, :]))
            if w >= 2:
                S["rope"].w(SY, 15 * (w - 1))
            for srct, dstb in ((cos_in, cosw), (sin_in, sinw)):
                ldcs.emit(SY, lambda e, s=srct, d=dstb, w=w, wpar=wpar: e.dma_start(
                    out=d[:, wpar * 512:(wpar + 1) * 512],
                    in_=s[:, w * 512:(w + 1) * 512]))

            ldx.wait_upto(VE, 4 * (w + 1))
            if w >= 1:
                S["lnE"].w(VE, w)
            for i in range(4):
                _op(VE, lambda e, i=i: e.tensor_reduce(
                    out=lnsum[:, i:i + 1], in_=xwin[:, i * D:(i + 1) * D],
                    axis=AX.X, op=ALU.add), S["lnA"] if i == 3 else None, 1)
            S["lnA"].w(VE)
            _op(VE, lambda e: e.tensor_scalar(out=lnmean[:], in0=lnsum[:],
                scalar1=1.0 / D, scalar2=None, op0=ALU.mult), S["lnA"], 1)
            S["lnA"].w(VE)
            _op(VE, lambda e: e.tensor_tensor(out=lnm2[:], in0=lnmean[:],
                in1=lnmean[:], op=ALU.mult), S["lnA"], 1)
            ldx.wait_upto(SC, 4 * (w + 1))
            if w > 0:
                S["lnB"].w(SC)
            for i in range(4):
                if i > 0:
                    S["lnB"].w(SC)
                _op(SC, lambda e, i=i: e.activation(
                    out=lnjunk[:], in_=xwin[:, i * D:(i + 1) * D], func=AF.Square,
                    accum_out=lnsq[:, i:i + 1]), S["lnB"], 1)
            S["lnB"].w(VE)
            S["lnA"].w(VE)
            _op(VE, lambda e: e.scalar_tensor_tensor(
                out=lnvar[:], in0=lnsq[:], scalar=1.0 / D, in1=lnm2[:],
                op0=ALU.mult, op1=ALU.subtract), S["lnC"], 1)
            S["lnC"].w(SC)
            _op(SC, lambda e: e.activation(out=lnvar[:], in_=lnvar[:], func=AF.Ln,
                                           bias=epsln[:]), S["lnD"], 1)
            S["lnD"].w(SC)
            _op(SC, lambda e: e.activation(out=lnrs[:], in_=lnvar[:], func=AF.Exp,
                                           scale=-0.5), S["lnD"], 1)
            S["lnD"].w(VE)
            _op(VE, lambda e: e.scalar_tensor_tensor(
                out=lnnmr[:], in0=lnmean[:], scalar=-1.0, in1=lnrs[:],
                op0=ALU.mult, op1=ALU.mult), S["lnE"], 1)
            S["lnE"].w(SC)
            for i in range(4):
                ti = 4 * w + i
                if ti >= 2:
                    S["trp"].w(SC, (ti - 1) * 4)
                _opm(SC, lambda e, i=i: e.activation(
                    out=xn_tm[:, (i % 2) * D:(i % 2 + 1) * D],
                    in_=xwin[:, i * D:(i + 1) * D], func=AF.Identity,
                    scale=lnrs[:, i:i + 1], bias=lnnmr[:, i:i + 1]),
                    [(S["app"], 1)])

            if w == 0:
                ldc.wait_all(TE)
                ldw.wait_upto(TE, 3)
            for i in range(4):
                ti = 4 * w + i
                S["app"].w(TE, ti + 1)
                for grp in range(4):
                    gi = ti * 4 + grp
                    gpar = gi % 2
                    if gi >= 2:
                        S["tev"].w(TE, gi - 1)
                    for cc4 in range(4):
                        dc = grp * 4 + cc4
                        _op(TE, lambda e, i=i, dc=dc, gpar=gpar, cc4=cc4: e.transpose(
                            ps_tr[gpar][:, cc4 * 128:(cc4 + 1) * 128],
                            xn_tm[:, (i % 2) * D + dc * 128:(i % 2) * D + (dc + 1) * 128],
                            identb[:]),
                            S["trp"] if cc4 == 3 else None, 1)
            for i in range(4):
                for grp in range(4):
                    gi = (4 * w + i) * 4 + grp
                    gpar = gi % 2
                    S["trp"].w(VE, gi + 1)
                    _op(VE, lambda e, wpar=wpar, i=i, grp=grp, gpar=gpar: e.tensor_copy(
                        out=xnw.ap().rearrange("p (wn c m) -> p wn c m", wn=2, c=16)[
                            :, wpar, grp * 4:(grp + 1) * 4, i * 128:(i + 1) * 128],
                        in_=ps_tr[gpar][:, 0:512].rearrange("p (c m) -> p c m", c=4)),
                        S["tev"], 1)

            def xnw_sl(dc, wpar):
                base = wpar * 16 * 512 + dc * 512
                return xnw[:, base:base + 512]

            S["tev"].w(TE, (4 * w + 4) * 4)
            for ch in range(5):
                cidx = 5 * w + ch
                if cidx > 0:
                    S["qev"].w(TE, cidx)
                for kc in range(16):
                    if ch < 4:
                        lhs = wq_s[:, kc * 512 + ch * 128: kc * 512 + (ch + 1) * 128]
                    else:
                        lhs = wk_s[:, kc * 128:(kc + 1) * 128]
                    _op(TE, lambda e, lhs=lhs, kc=kc, wpar=wpar: e.matmul(
                        ps_qkv[:], lhs, xnw_sl(kc, wpar), start=(kc == 0), stop=(kc == 15)),
                        S["mmq"] if kc == 15 else None, 1)
                S["mmq"].w(SC, cidx + 1)
                if cidx > 0:
                    S["rope"].w(SC, 3 * cidx - 2)
                _op(SC, lambda e: e.copy(out=qst[:], in_=ps_qkv[:]), S["qev"], 1)
                S["qev"].w(TE, cidx + 1)
                if cidx > 0:
                    S["rope"].w(TE, 3 * cidx - 1)
                _op(TE, lambda e: e.matmul(ps_rot[:], rotT[:], qst[:],
                                           start=True, stop=True), S["rot"], 1)
                if ch == 0:
                    ldcs.wait_upto(VE, 2 * (w + 1))
                if cidx > 0:
                    S["rope"].w(VE, 3 * cidx)
                S["qev"].w(VE, cidx + 1)
                _op(VE, lambda e, wpar=wpar: e.tensor_tensor(
                    out=rt1[:], in0=qst[:], in1=cosw[:, wpar * 512:(wpar + 1) * 512],
                    op=ALU.mult), S["rope"], 1)
                S["rot"].w(VE, cidx + 1)
                _op(VE, lambda e, wpar=wpar: e.tensor_tensor(
                    out=rt2[:], in0=ps_rot[:], in1=sinw[:, wpar * 512:(wpar + 1) * 512],
                    op=ALU.mult), S["rope"], 1)
                S["rope"].w(VE)
                dst = (q_fm[:, ch * T + w * 512: ch * T + (w + 1) * 512] if ch < 4
                       else k_fm[:, w * 512:(w + 1) * 512])
                _op(VE, lambda e, dst=dst: e.tensor_tensor(
                    out=dst, in0=rt1[:], in1=rt2[:], op=ALU.add), S["rope"], 1)

            if w >= 1:
                for qq in range(4):
                    emit_qred(4 * (w - 1) + qq)

            for sub in range(4):
                vi = 4 * w + sub
                if vi >= 2:
                    S["vev"].w(TE, vi)
                for kc in range(16):
                    _op(TE, lambda e, sub=sub, kc=kc, wpar=wpar, vp=vi % 2: e.matmul(
                        ps_v[vp][:, 0:128], xnw_sl(kc, wpar)[:, sub * 128:(sub + 1) * 128],
                        wv_s[:, kc * 128:(kc + 1) * 128],
                        start=(kc == 0), stop=(kc == 15)),
                        S["vmm"] if kc == 15 else None, 1)
                S["vmm"].w(SC, vi + 1)
                if vi == 0:
                    S["vev"].w(SC, 1)
                _op(SC, lambda e, vi=vi, vp=vi % 2: e.copy(
                    out=v_tm[:, vi * (DV + 1):vi * (DV + 1) + DV], in_=ps_v[vp][:, 0:128]),
                    S["vev"], 1)

            if w >= 2:
                for qtr in range(4 * (w - 2), 4 * (w - 1)):
                    emit_qmagic(qtr)
                    emit_qtrans(qtr)
        for qt in range(28, 32):
            emit_qred(qt)
        for qtr in range(24, 32):
            emit_qmagic(qtr)
            emit_qtrans(qtr)

        def w_fb1(q):
            S["rope"].w(q, 120)
            S["vmm"].w(q, 32)
            S["vev"].w(q, 33)

        for nm, buf in [("q_fm", q_fm), ("k_fm", k_fm), ("v_tm", v_tm)]:
            if nm in dbg_t:
                w_fb1(SY)
                dbgs.emit(SY, lambda e, nm=nm, buf=buf: e.dma_start(
                    out=dbg_t[nm], in_=buf[:]))

        esP1p.close()
        esP1.close()

        if ph <= 1:
            dbgs.wait_all(SY)
            _flush(blk)
            esAtt.close()
            return nc

        esQW.close()
        esP2 = contextlib.ExitStack()
        tbuf = esP2.enter_context(nc.sbuf_tensor("tbuf", [128, 4 * 512], F32))
        pbuf = esP2.enter_context(nc.sbuf_tensor("pbuf", [128, 4 * 512], BF16))
        ctmp = esP2.enter_context(nc.sbuf_tensor("ctmp", [128, 4 * 128], BF16))
        crcp = esP2.enter_context(nc.sbuf_tensor("crcp", [128, 4], F32))
        ast = esP2.enter_context(nc.sbuf_tensor("ast", [128, 4 * 512], F32))
        xox = esP2.enter_context(nc.sbuf_tensor("xox", [128, 4 * 512], F32))
        ctx_fmT = esP2.enter_context(nc.sbuf_tensor("ctx_fmT", [128, 4 * T], BF16))
        masks = esP2.enter_context(nc.sbuf_tensor("masks_s", [128, 4 * 512], F32))
        alibias = esP2.enter_context(nc.sbuf_tensor("alibias_s", [128, 4 * NCOMB], F32))
        aliq = esP2.enter_context(nc.sbuf_tensor("aliq_s", [1, 4 * 512], F32R))
        qcomb = esP2.enter_context(nc.sbuf_tensor("qcomb", [128, 20 * 512], F32))
        t2buf = esP2.enter_context(nc.sbuf_tensor("t2buf", [128, 2 * 512], F32))
        wo_s = esP2.enter_context(nc.sbuf_tensor("wo_s", [128, 4 * D], BF16, side="right"))

        w_fb1(SY)
        for dst, srct in [(masks, masks_in), (alibias, alib_in), (aliq, aliq_in)]:
            ldc2.emit(SY, lambda e, d=dst, s=srct: e.dma_start(out=d[:], in_=s))

        esP2p = contextlib.ExitStack()
        ps_s = [esP2p.enter_context(nc.psum_tensor(f"ps_s{i}", [128, 512], F32))
                for i in range(3)]
        ps_c = [[esP2p.enter_context(nc.psum_tensor(f"ps_c{i}{j}", [128, 512], F32))
                 for j in range(2)] for i in range(2)]
        ps_ct = esP2p.enter_context(nc.psum_tensor("ps_ct", [128, 1024], BF16))

        ITERS = [(h, qg, kc) for h in range(4) for qg, kc in COMBOS]
        NIT = len(ITERS)
        GEND = {}
        for i_, (h_, qg_, kc_) in enumerate(ITERS):
            GEND[h_ * 8 + qg_] = i_

        def emit_pv(pidx):
            ph, pqg, pkc = ITERS[pidx]
            ppar = pidx % 4
            gi = ph * 8 + pqg
            gpi = gi % 2
            S["mp"].w(TE, pidx + 1)
            if pkc == 0 and gi >= 2:
                S["mcf"].w(TE, gi - 1)
            last = pkc == 4 * pqg + 3
            for sub in range(4):
                _op(TE, lambda e, ppar=ppar, sub=sub, pkc=pkc, gpi=gpi,
                    st=(pkc == 0 and sub % 2 == 0),
                    sp=(last and sub % 2 == 1): e.matmul(
                    ps_c[gpi][sub // 2][:, (sub % 2) * (DV + 1):
                                        (sub % 2) * (DV + 1) + DV + 1],
                    pbuf[:, ppar * 512 + sub * 128: ppar * 512 + (sub + 1) * 128],
                    v_tm[:, pkc * (DV + 1):(pkc + 1) * (DV + 1)],
                    start=st, stop=sp),
                    S["pvd"] if sub == 3 else None, 1)

        def emit_group_finalize(gi):
            h_, qg_ = gi // 8, gi % 8
            gpi = gi % 2
            S["pvd"].w(VE, GEND[gi] + 1)
            S["mce"].w(VE)
            for sub in range(4):
                _op(VE, lambda e, gpi=gpi, sub=sub: e.reciprocal(
                    out=crcp[:, sub:sub + 1],
                    in_=ps_c[gpi][sub // 2][:, (sub % 2) * (DV + 1) + DV:
                                            (sub % 2) * (DV + 1) + DV + 1]),
                    S["mce"] if sub == 3 else None, 1)
            S["mce"].w(VE)
            if gi >= 1:
                S["mct"].w(VE, 4 * gi)
            for sub in range(4):
                _op(VE, lambda e, gpi=gpi, sub=sub: e.tensor_scalar(
                    out=ctmp[:, sub * 128:(sub + 1) * 128],
                    in0=ps_c[gpi][sub // 2][:, (sub % 2) * (DV + 1):
                                            (sub % 2) * (DV + 1) + DV],
                    scalar1=crcp[:, sub:sub + 1], scalar2=None, op0=ALU.mult),
                    S["mce"], 1)
            S["mce"].w(TE)
            if gi >= 1:
                S["mcf"].w(TE, gi)
            for sub in range(4):
                _op(TE, lambda e, sub=sub: e.transpose(
                    ps_ct[:, sub * 128:(sub + 1) * 128],
                    ctmp[:, sub * 128:(sub + 1) * 128], identb[:]), S["mct"], 1)
            S["mct"].w(VE, 4 * (gi + 1))
            _op(VE, lambda e, h_=h_, qg_=qg_: e.tensor_copy(
                out=ctx_fmT[:, h_ * T + qg_ * 512: h_ * T + (qg_ + 1) * 512],
                in_=ps_ct[:, 0:512]), S["mcf"], 1)

        S["rope"].w(TE, 120)
        S["vev"].w(TE, 33)
        ldc2.wait_upto(TE, 3)
        ldc2.wait_upto(SC, 3)
        ldc2.wait_upto(VE, 3)
        for h in range(4):
            _op(TE, lambda e, h=h: e.matmul(
                ps_s[0][:], onesr[:], aliq[:, h * 512:(h + 1) * 512],
                start=True, stop=True), S["ms"], 1)
            S["ms"].w(SC, h + 1)
            _op(SC, lambda e, h=h: e.copy(
                out=qcomb[:, (h * 5 + 4) * 512:(h * 5 + 5) * 512], in_=ps_s[0][:]),
                S["mt"], 1)
            S["mt"].w(TE, h + 1)
        S["mt"].w(VE, 4)
        for h in range(4):
            for r in range(4):
                _op(VE, lambda e, h=h, r=r: e.tensor_tensor(
                    out=qcomb[:, (h * 5 + r) * 512:(h * 5 + r + 1) * 512],
                    in0=qcomb[:, (h * 5 + 4) * 512:(h * 5 + 5) * 512],
                    in1=masks[:, r * 512:(r + 1) * 512], op=ALU.add),
                    S["mm_"] if (h == 3 and r == 3) else None, 1)
        S["mm_"].w(VE, 1)
        MS0, MT0 = S["ms"].v, S["mt"].v
        S["mt"].w(TE, MT0)
        for idx, (h, qg, kc) in enumerate(ITERS):
            par = idx % 3
            pt = idx % 4
            t2 = idx % 2
            pp = idx % 4
            ci = CIDX[(qg, kc)]
            gi = h * 8 + qg
            if idx >= 3:
                S["mm_"].w(TE, idx - 1)
            _op(TE, lambda e, kc=kc, h=h, qg=qg, par=par: e.matmul(
                ps_s[par][:], k_fm[:, kc * 128:(kc + 1) * 128],
                q_fm[:, h * T + qg * 512:h * T + (qg + 1) * 512],
                start=True, stop=True), S["ms"], 1)
            if idx >= 3:
                emit_pv(idx - 3)
            if kc == 3 and gi >= 1:
                emit_group_finalize(gi - 1)
            sel = (kc - 4 * qg) if kc >= 4 * qg else 4
            S["ms"].w(VE, MS0 + idx + 1)
            if idx >= 4:
                S["mt"].w(VE, MT0 + idx - 3)
            _op(VE, lambda e, par=par, pt=pt, h=h, sel=sel: e.scalar_tensor_tensor(
                out=tbuf[:, pt * 512:(pt + 1) * 512], in0=ps_s[par][:],
                scalar=CSC, in1=qcomb[:, (h * 5 + sel) * 512:(h * 5 + sel + 1) * 512],
                op0=ALU.mult, op1=ALU.add), S["mm_"], 1)
            S["mm_"].w(SC, idx + 2)
            _op(SC, lambda e, pt=pt, t2=t2, h=h, ci=ci: e.activation(
                out=t2buf[:, t2 * 512:(t2 + 1) * 512],
                in_=tbuf[:, pt * 512:(pt + 1) * 512],
                func=AF.Tanh, scale=1.0,
                bias=alibias[:, h * NCOMB + ci:h * NCOMB + ci + 1]), S["mt"], 1)
            S["mt"].w(SC, MT0 + idx + 1)
            if idx >= 4:
                S["pvd"].w(SC, idx - 3)
            _op(SC, lambda e, t2=t2, pp=pp: e.activation(
                out=pbuf[:, pp * 512:(pp + 1) * 512],
                in_=t2buf[:, t2 * 512:(t2 + 1) * 512], func=AF.Exp, scale=CAP),
                S["mp"], 1)
        emit_pv(NIT - 3)
        emit_pv(NIT - 2)
        emit_pv(NIT - 1)
        emit_group_finalize(31)

        if "ctx" in dbg_t:
            S["mcf"].w(SY, 32)
            dbgs.emit(SY, lambda e: e.dma_start(out=dbg_t["ctx"], in_=ctx_fmT[:]))

        if ph <= 2:
            S["mcf"].w(SY, 32)
            S["ccA"].w(SY, 1)
            S["ccB"].w(SY, 1)
            dbgs.wait_all(SY)
            _flush(blk)
            esP2p.close(); esP2.close(); esAtt.close()
            return nc

        S["mcf"].w(TE, 32)
        ldw.wait_upto(TE, 4)
        n3 = 0
        for dk in range(16):
            for tg in range(8):
                par = n3 % 2
                sp4 = n3 % 4
                if n3 >= 4:
                    S["wo2"].w(SY, n3 - 3)
                xol.emit(SY, lambda e, dk=dk, tg=tg, sp4=sp4: e.dma_start(
                    out=xox[:, sp4 * 512:(sp4 + 1) * 512],
                    in_=xT_in[dk * 128:(dk + 1) * 128, tg * 512:(tg + 1) * 512]))
                if n3 >= 2:
                    S["wo2"].w(TE, n3 - 1)
                for hh in range(4):
                    _op(TE, lambda e, dk=dk, tg=tg, hh=hh, par=par: e.matmul(
                        ps_s[par][:],
                        wo_s[:, hh * D + dk * 128:hh * D + (dk + 1) * 128],
                        ctx_fmT[:, hh * T + tg * 512:hh * T + (tg + 1) * 512],
                        start=(hh == 0), stop=(hh == 3)),
                        S["wo1"] if hh == 3 else None, 1)
                S["wo1"].w(VE, n3 + 1)
                xol.wait_upto(VE, n3 + 1)
                if n3 >= 4:
                    sast.wait_upto(VE, n3 - 3)
                _op(VE, lambda e, par=par, sp4=sp4: e.scalar_tensor_tensor(
                    out=ast[:, sp4 * 512:(sp4 + 1) * 512],
                    in0=xox[:, sp4 * 512:(sp4 + 1) * 512], scalar=0.25,
                    in1=ps_s[par][:], op0=ALU.mult, op1=ALU.add),
                    S["wo2"], 1)
                S["wo2"].w(SC, n3 + 1)
                prow = (dk // 4) * 2048 + (tg // 2) * 512 + (dk % 4) * 128
                pcol = (tg % 2) * 512
                sast.emit(SC, lambda e, prow=prow, pcol=pcol, sp4=sp4: e.dma_start(
                    out=attnp_d[prow:prow + 128, pcol:pcol + 512],
                    in_=ast[:, sp4 * 512:(sp4 + 1) * 512]))
                n3 += 1
        qg2.wait_upto(GP, 32)
        _op(GP, lambda e: e.collective_compute(
            "AllGather", ALU.bypass, replica_groups=[list(range(8))],
            ins=[pshT_d.opt()], outs=[pagT_d.opt()]), S["ccB"], 1)
        for p in range(4):
            sast.wait_upto(GP, 32 * (p + 1))
            _op(GP, lambda e, p=p: e.collective_compute(
                "ReduceScatter", ALU.add, replica_groups=[[0, 1, 2, 3], [4, 5, 6, 7]],
                ins=[attnp_d[p * 2048:(p + 1) * 2048, :].opt()],
                outs=[attnr_d[p * 512:(p + 1) * 512, :].opt()]), S["rs"], 1)

        def w_fb2(q):
            S["mcf"].w(q, 32)
            S["wo1"].w(q, 128)
            S["wo2"].w(q, 128)

        if "attnr" in dbg_t:
            S["rs"].w(SY, 4)
            dbgs.emit(SY, lambda e: e.dma_start(out=dbg_t["attnr"], in_=attnr_d))

        if ph <= 3:
            S["rs"].w(SY, 4)
            dbgs.wait_all(SY)
            _flush(blk)
            esP2p.close(); esP2.close(); esAtt.close()
            return nc

        esP2p.close()
        esP2.close()
        esAtt.close()
        for q in (SY, SC, VE, GP):
            qg2.wait_all(q)

        esFFa = contextlib.ExitStack()
        xn_own = esFFa.enter_context(nc.sbuf_tensor("xn_own", [128, 16 * 1024], BF16))
        rstd_fm = esFFa.enter_context(nc.sbuf_tensor("rstd_fm", [128, 2 * 512], F32))
        rstt = esFFa.enter_context(nc.sbuf_tensor("rstt", [1, 2 * 512], F32))
        rsttr = esFFa.enter_context(nc.sbuf_tensor("rsttr", [1, 2 * 512], F32R))

        esX = contextlib.ExitStack()
        xw2 = esX.enter_context(nc.sbuf_tensor("xw2", [128, 4 * D], F32))
        xt2 = esX.enter_context(nc.sbuf_tensor("xt2", [128, 2 * D], BF16))
        l2sum = esX.enter_context(nc.sbuf_tensor("l2sum", [128, 4], F32))
        l2sq = esX.enter_context(nc.sbuf_tensor("l2sq", [128, 4], F32))
        l2mean = esX.enter_context(nc.sbuf_tensor("l2mean", [128, 4], F32))
        l2m2 = esX.enter_context(nc.sbuf_tensor("l2m2", [128, 4], F32))
        l2var = esX.enter_context(nc.sbuf_tensor("l2var", [128, 4], F32))
        l2rs = esX.enter_context(nc.sbuf_tensor("l2rs", [128, 4], F32))
        l2nmr = esX.enter_context(nc.sbuf_tensor("l2nmr", [128, 4], F32))
        l2junk = esX.enter_context(nc.sbuf_tensor("l2junk", [128, D], BF16))
        esXp = contextlib.ExitStack()
        ps_x2 = [esXp.enter_context(nc.psum_tensor(f"ps_x2{i}", [128, 1024], BF16))
                 for i in range(2)]

        w_fb2(SY)
        sast.wait_all(SY)
        w_fb2(VE)
        w_fb2(SC)
        for w in range(2):
            if w >= 1:
                S["xoap"].w(SY, w * 4)
            for i in range(4):
                t = 4 * w + i
                ldx.emit(SY, lambda e, i=i, t=t: e.dma_start(
                    out=xw2[:, i * D:(i + 1) * D], in_=xown_in[t * 128:(t + 1) * 128, :]))
            ldx.wait_upto(VE, 32 + 4 * (w + 1))
            if w >= 1:
                S["xoE"].w(VE, w)
            for i in range(4):
                _op(VE, lambda e, i=i: e.tensor_reduce(
                    out=l2sum[:, i:i + 1], in_=xw2[:, i * D:(i + 1) * D],
                    axis=AX.X, op=ALU.add), S["xoA"] if i == 3 else None, 1)
            S["xoA"].w(VE)
            _op(VE, lambda e: e.tensor_scalar(out=l2mean[:], in0=l2sum[:],
                scalar1=1.0 / D, scalar2=None, op0=ALU.mult), S["xoA"], 1)
            S["xoA"].w(VE)
            _op(VE, lambda e: e.tensor_tensor(out=l2m2[:], in0=l2mean[:],
                in1=l2mean[:], op=ALU.mult), S["xoA"], 1)
            ldx.wait_upto(SC, 32 + 4 * (w + 1))
            if w > 0:
                S["xoB"].w(SC)
            for i in range(4):
                if i > 0:
                    S["xoB"].w(SC)
                _op(SC, lambda e, i=i: e.activation(
                    out=l2junk[:], in_=xw2[:, i * D:(i + 1) * D], func=AF.Square,
                    accum_out=l2sq[:, i:i + 1]), S["xoB"], 1)
            S["xoB"].w(VE)
            S["xoA"].w(VE)
            _op(VE, lambda e: e.scalar_tensor_tensor(
                out=l2var[:], in0=l2sq[:], scalar=1.0 / D, in1=l2m2[:],
                op0=ALU.mult, op1=ALU.subtract), S["xoC"], 1)
            S["xoC"].w(SC)
            _op(SC, lambda e: e.activation(out=l2var[:], in_=l2var[:], func=AF.Ln,
                                           bias=epsln[:]), S["xoD"], 1)
            S["xoD"].w(SC)
            _op(SC, lambda e: e.activation(out=l2rs[:], in_=l2var[:], func=AF.Exp,
                                           scale=-0.5), S["xoD"], 1)
            S["xoD"].w(VE)
            _op(VE, lambda e: e.scalar_tensor_tensor(
                out=l2nmr[:], in0=l2mean[:], scalar=-1.0, in1=l2rs[:],
                op0=ALU.mult, op1=ALU.mult), S["xoE"], 1)
            S["xoE"].w(SC)
            for i in range(4):
                ti = 4 * w + i
                if ti >= 2:
                    S["xotr"].w(SC, (ti - 1) * 4)
                _op(SC, lambda e, i=i: e.activation(
                    out=xt2[:, (i % 2) * D:(i % 2 + 1) * D],
                    in_=xw2[:, i * D:(i + 1) * D], func=AF.Identity,
                    scale=l2rs[:, i:i + 1], bias=l2nmr[:, i:i + 1]), S["xoap"], 1)
            for i in range(4):
                ti = 4 * w + i
                S["xoap"].w(TE, ti + 1)
                for grp in range(4):
                    gi2 = ti * 4 + grp
                    gpar = gi2 % 2
                    if gi2 >= 2:
                        S["xotv"].w(TE, gi2 - 1)
                    for cc4 in range(4):
                        dc = grp * 4 + cc4
                        _op(TE, lambda e, i=i, dc=dc, gpar=gpar, cc4=cc4: e.transpose(
                            ps_x2[gpar][:, cc4 * 128:(cc4 + 1) * 128],
                            xt2[:, (i % 2) * D + dc * 128:(i % 2) * D + (dc + 1) * 128],
                            identb[:]), S["xotr"] if cc4 == 3 else None, 1)
            for i in range(4):
                ti = 4 * w + i
                for grp in range(4):
                    gi2 = ti * 4 + grp
                    gpar = gi2 % 2
                    S["xotr"].w(VE, gi2 + 1)
                    _op(VE, lambda e, ti=ti, grp=grp, gpar=gpar: e.tensor_copy(
                        out=xn_own.ap().rearrange("p (c m) -> p c m", c=16)[
                            :, grp * 4:(grp + 1) * 4, ti * 128:(ti + 1) * 128],
                        in_=ps_x2[gpar][:, 0:512].rearrange("p (c m) -> p c m", c=4)),
                        S["xotv"], 1)
        XO_TV = S["xotv"].v

        if "xn_own" in dbg_t:
            S["xotv"].w(SY, XO_TV)
            dbgs.emit(SY, lambda e: e.dma_start(out=dbg_t["xn_own"], in_=xn_own[:]))

        def w_fb3(q):
            S["xoap"].w(q, 8)
            S["xotr"].w(q, 32)
            S["xotv"].w(q, 32)

        esXp.close()
        esX.close()

        esF1 = contextlib.ExitStack()
        wgt_g = esF1.enter_context(nc.sbuf_tensor("wgt_g", [128, 2 * 16 * 1024], BF16))
        wgt_u = esF1.enter_context(nc.sbuf_tensor("wgt_u", [128, 2 * 16 * 1024], BF16))
        sgbuf = esF1.enter_context(nc.sbuf_tensor("sgbuf", [128, 2 * 512], F32))
        hstg = esF1.enter_context(nc.sbuf_tensor("hstg", [128, 2 * 512], BF16))
        h2stg = esF1.enter_context(nc.sbuf_tensor("h2stg", [128, 2 * 512], BF16))
        esF1p = contextlib.ExitStack()
        ps_g = [esF1p.enter_context(nc.psum_tensor(f"ps_g{i}", [128, 512], F32))
                for i in range(2)]
        ps_u = [esF1p.enter_context(nc.psum_tensor(f"ps_u{i}", [128, 512], F32))
                for i in range(2)]
        ps_r2 = [esF1p.enter_context(nc.psum_tensor(f"ps_r2{i}", [128, 512], F32))
                 for i in range(2)]

        NBLK = 8

        def emit_treads(bi):
            r, hb = bi // 2, bi % 2
            bpar = bi % 2
            if bi >= 2:
                S["ff1"].w(SY, (bi - 1) * 32)
                S["ff1"].w(SC, (bi - 1) * 32)
            for dc in range(16):
                tg1.emit(SY, lambda e, bpar=bpar, dc=dc, r=r, hb=hb: e.dma_start(
                    out=wgt_g[:, (bpar * 16 + dc) * 1024:(bpar * 16 + dc + 1) * 1024],
                    in_=gagT_d[r * 2048 + dc * 128:r * 2048 + (dc + 1) * 128,
                               hb * 1024:(hb + 1) * 1024]))
                tg2.emit(SC, lambda e, bpar=bpar, dc=dc, r=r, hb=hb: e.dma_start(
                    out=wgt_u[:, (bpar * 16 + dc) * 1024:(bpar * 16 + dc + 1) * 1024],
                    in_=gagT_d[(r + 4) * 2048 + dc * 128:(r + 4) * 2048 + (dc + 1) * 128,
                               hb * 1024:(hb + 1) * 1024]))

        w_fb3(SY)
        w_fb3(SC)
        S["ccA"].w(SY, 1)
        S["ccA"].w(SC, 1)
        emit_treads(0)
        emit_treads(1)

        def emit_r2mm(mj):
            mjg, mth = mj // 2, mj % 2
            S["ffq"].w(TE, mj + 1)
            _op(TE, lambda e, par=mj % 2, th=mth,
                st=(mjg == 0), sp=(mjg == 63): e.matmul(
                ps_r2[th][0:1, 0:512], onesc[:],
                h2stg[:, par * 512:(par + 1) * 512],
                start=st, stop=sp), S["ffo"], 1)

        nj = 0
        for bi in range(NBLK):
            r, hb = bi // 2, bi % 2
            bpar = bi % 2
            tg1.wait_upto(TE, (bi + 1) * 16)
            tg2.wait_upto(TE, (bi + 1) * 16)
            if bi == 0:
                S["xotv"].w(TE, XO_TV)
            for jj in range(8):
                jg = r * 16 + hb * 8 + jj
                for th in range(2):
                    par = nj % 2
                    if nj >= 2:
                        S["ffs"].w(TE, nj - 1)
                        S["ffh"].w(TE, nj - 1)
                    for dc in range(16):
                        _op(TE, lambda e, bpar=bpar, dc=dc, jj=jj, th=th, par=par: e.matmul(
                            ps_g[par][:],
                            wgt_g[:, (bpar * 16 + dc) * 1024 + jj * 128:
                                  (bpar * 16 + dc) * 1024 + (jj + 1) * 128],
                            xn_own[:, dc * 1024 + th * 512:dc * 1024 + (th + 1) * 512],
                            start=(dc == 0), stop=(dc == 15)),
                            S["ff1"] if dc == 15 else None, 1)
                    for dc in range(16):
                        _op(TE, lambda e, bpar=bpar, dc=dc, jj=jj, th=th, par=par: e.matmul(
                            ps_u[par][:],
                            wgt_u[:, (bpar * 16 + dc) * 1024 + jj * 128:
                                  (bpar * 16 + dc) * 1024 + (jj + 1) * 128],
                            xn_own[:, dc * 1024 + th * 512:dc * 1024 + (th + 1) * 512],
                            start=(dc == 0), stop=(dc == 15)),
                            S["ff1"] if dc == 15 else None, 1)
                    S["ff1"].w(SC, 2 * nj + 1)
                    if nj >= 2:
                        S["ffh"].w(SC, nj - 1)
                    _op(SC, lambda e, par=par: e.activation(
                        out=sgbuf[:, par * 512:(par + 1) * 512], in_=ps_g[par][:],
                        func=AF.Silu), S["ffs"], 1)
                    S["ffs"].w(VE, nj + 1)
                    S["ff1"].w(VE, 2 * nj + 2)
                    if nj >= 2:
                        S["ffq"].w(VE, nj - 1)
                        hst.wait_upto(VE, nj - 1)
                    _op(VE, lambda e, par=par: e.tensor_tensor(
                        out=hstg[:, par * 512:(par + 1) * 512], in0=ps_u[par][:],
                        in1=sgbuf[:, par * 512:(par + 1) * 512], op=ALU.mult),
                        S["ffh"], 1)
                    S["ffh"].w(SC, nj + 1)
                    if nj >= 2:
                        S["ffo"].w(SC, nj - 1)
                    _op(SC, lambda e, par=par: e.activation(
                        out=h2stg[:, par * 512:(par + 1) * 512],
                        in_=hstg[:, par * 512:(par + 1) * 512], func=AF.Square),
                        S["ffq"], 1)
                    if nj >= 1:
                        emit_r2mm(nj - 1)
                    S["ffh"].w(SC, nj + 1)
                    hst.emit(SC, lambda e, jg=jg, th=th, par=par: e.dma_start(
                        out=h_d[jg * 128:(jg + 1) * 128, th * 512:(th + 1) * 512],
                        in_=hstg[:, par * 512:(par + 1) * 512]))
                    nj += 1
            if bi + 2 < NBLK:
                emit_treads(bi + 2)
        emit_r2mm(127)

        S["ffo"].w(VE, 128)
        for th in range(2):
            _op(VE, lambda e, th=th: e.tensor_scalar(
                out=rstt[:, th * 512:(th + 1) * 512], in0=ps_r2[th][0:1, 0:512],
                scalar1=1.0 / HFF, scalar2=None, op0=ALU.mult), S["rsA"], 1)
        S["rsA"].w(SC, 2)
        _op(SC, lambda e: e.activation(out=rstt[:], in_=rstt[:], func=AF.Ln,
                                       bias=epsrms[0:1, 0:1]), S["rsB"], 1)
        S["rsB"].w(SC)
        _op(SC, lambda e: e.activation(out=rstt[:], in_=rstt[:], func=AF.Exp,
                                       scale=-0.5), S["rsB"], 1)
        S["rsB"].w(VE)
        _op(VE, lambda e: e.tensor_copy(out=rsttr[:], in_=rstt[:]), S["rsB"], 1)
        if "h" in dbg_t:
            hst.wait_all(SY)
            dbgs.emit(SY, lambda e: e.dma_start(out=dbg_t["h"], in_=h_d))

        if ph <= 4:
            S["rsB"].w(SY, 3)
            S["rs"].w(SY, 4)
            dbgs.wait_all(SY)
            _flush(blk)
            esF1p.close(); esF1.close(); esFFa.close()
            return nc

        esF1p.close()
        esF1.close()

        esF2 = contextlib.ExitStack()
        h_buf = esF2.enter_context(nc.sbuf_tensor("h_buf", [128, 64 * 1024], BF16))
        wpt = esF2.enter_context(nc.sbuf_tensor("wpt", [128, 6 * 512], BF16))
        fstg = esF2.enter_context(nc.sbuf_tensor("fstg", [128, 2 * 512], F32))
        outstg = esF2.enter_context(nc.sbuf_tensor("outstg", [128, 4 * 512], F32))
        axa = esF2.enter_context(nc.sbuf_tensor("axa", [128, 4 * 512], F32))
        axx = esF2.enter_context(nc.sbuf_tensor("axx", [128, 4 * 512], F32))
        esF2p = contextlib.ExitStack()
        ps_rb = [esF2p.enter_context(nc.psum_tensor(f"ps_rb{i}", [128, 512], F32))
                 for i in range(2)]
        ps_o = [esF2p.enter_context(nc.psum_tensor(f"ps_o{i}", [128, 512], F32))
                for i in range(4)]

        S["rsB"].w(TE, 3)
        for th in range(2):
            _op(TE, lambda e, th=th: e.matmul(
                ps_rb[th][:], onesr[:], rsttr[:, th * 512:(th + 1) * 512],
                start=True, stop=True), S["rsC"], 1)
        S["rsC"].w(SC, 2)
        for th in range(2):
            _op(SC, lambda e, th=th: e.copy(
                out=rstd_fm[:, th * 512:(th + 1) * 512], in_=ps_rb[th][:]), S["rsC"], 1)
        RSC_ALL = S["rsC"].v

        S["ff1"].w(SY, 256)
        S["ffo"].w(SY, 128)
        for j2 in range(64):
            hst.wait_upto(SY, 2 * (j2 + 1))
            hld.emit(SY, lambda e, j2=j2: e.dma_start(
                out=h_buf[:, j2 * 1024:(j2 + 1) * 1024],
                in_=h_d[j2 * 128:(j2 + 1) * 128, :]))
        S["ccB"].w(SY, 1)
        S["rs"].w(SY, 4)
        nsw_l = 0

        def emit_out_stores(si):
            th_, dqi_ = si // 4, si % 4
            for dc in range(4):
                fin.emit(SY, lambda e, th=th_, dqi=dqi_, dc=dc: e.dma_start(
                    out=out_ext[dqi * 512 + dc * 128:dqi * 512 + (dc + 1) * 128,
                                th * 512:(th + 1) * 512],
                    in_=outstg[:, dc * 512:(dc + 1) * 512]))

        for th in range(2):
            for dqi in range(4):
                si = th * 4 + dqi
                for j2 in range(64):
                    par = nsw_l % 6
                    if nsw_l >= 6:
                        S["f2f"].w(SY, nsw_l - 5)
                    tp1.emit(SY, lambda e, j2=j2, dqi=dqi, par=par: e.dma_start(
                        out=wpt[:, par * 512:(par + 1) * 512],
                        in_=pagT_d[j2 * 128:(j2 + 1) * 128, dqi * 512:(dqi + 1) * 512]))
                    nsw_l += 1
                if si >= 1:
                    S["f2a"].w(SY, si * 4)
                    emit_out_stores(si - 1)
                for dc in range(4):
                    axl.emit(SY, lambda e, th=th, dqi=dqi, dc=dc: e.dma_start(
                        out=axa[:, dc * 512:(dc + 1) * 512],
                        in_=attnr_d[dqi * 512 + dc * 128:dqi * 512 + (dc + 1) * 128,
                                    th * 512:(th + 1) * 512]))
        S["f2a"].w(SY, 32)
        emit_out_stores(7)

        nsw = 0
        nev = 0
        ntr = 0
        for th in range(2):
            for dqi in range(4):
                si = th * 4 + dqi
                if si == 0:
                    S["rsC"].w(TE, RSC_ALL)
                for j2 in range(64):
                    par = nsw % 6
                    tp1.wait_upto(TE, nsw + 1)
                    if si == 0:
                        hld.wait_upto(TE, j2 + 1)
                    for dc in range(4):
                        if si > 0 and j2 == 0:
                            S["f2m"].w(TE, (si - 1) * 4 + dc + 1)
                        _op(TE, lambda e, par=par, dc=dc, j2=j2, th=th: e.matmul(
                            ps_o[dc][:],
                            wpt[:, par * 512 + dc * 128:par * 512 + (dc + 1) * 128],
                            h_buf[:, j2 * 1024 + th * 512:j2 * 1024 + (th + 1) * 512],
                            start=(j2 == 0), stop=(j2 == 63)),
                            S["f2f"] if dc == 3 else None, 1)
                    nsw += 1
                for dc in range(4):
                    fpar = dc % 2
                    S["f2f"].w(VE, nsw)
                    if si >= 1 and dc == 0:
                        fin.wait_upto(VE, si * 4)
                    _op(VE, lambda e, dc=dc, th=th, fpar=fpar: e.tensor_tensor(
                        out=fstg[:, fpar * 512:(fpar + 1) * 512], in0=ps_o[dc][:],
                        in1=rstd_fm[:, th * 512:(th + 1) * 512], op=ALU.mult),
                        S["f2m"], 1)
                    S["f2m"].w(VE)
                    _op(VE, lambda e, dqi=dqi, dc=dc, fpar=fpar: e.tensor_scalar(
                        out=fstg[:, fpar * 512:(fpar + 1) * 512],
                        in0=fstg[:, fpar * 512:(fpar + 1) * 512],
                        scalar1=bo_t[:, dqi * 4 + dc:dqi * 4 + dc + 1], scalar2=None,
                        op0=ALU.add), S["f2e"], 1)
                    S["f2e"].w(VE)
                    axl.wait_upto(VE, (si + 1) * 4)
                    _op(VE, lambda e, dc=dc, fpar=fpar: e.tensor_tensor(
                        out=outstg[:, dc * 512:(dc + 1) * 512],
                        in0=fstg[:, fpar * 512:(fpar + 1) * 512],
                        in1=axa[:, dc * 512:(dc + 1) * 512],
                        op=ALU.add), S["f2a"], 1)
                    nev += 1
        fin.wait_all(SY)
        dbgs.wait_all(SY)

        esF2p.close()
        esF2.close()
        esFFa.close()

        _flush(blk)

    return nc


def kernel(L0_attn_hidden_states, L0_attn_Wq, L0_attn_Wk, L0_attn_Wv,
           L0_attn_Wo, L0_attn_bo, L0_ffn_w_gate_up, L0_ffn_w_proj):
    x = np.ascontiguousarray(np.asarray(L0_attn_hidden_states, dtype=np.float32))
    Wq = np.ascontiguousarray(np.asarray(L0_attn_Wq, dtype=np.float32))
    Wk = np.ascontiguousarray(np.asarray(L0_attn_Wk, dtype=np.float32))
    Wv = np.ascontiguousarray(np.asarray(L0_attn_Wv, dtype=np.float32))
    Wo = np.ascontiguousarray(np.asarray(L0_attn_Wo, dtype=np.float32))
    bo = np.ascontiguousarray(np.asarray(L0_attn_bo, dtype=np.float32))
    wgu = np.ascontiguousarray(np.asarray(L0_ffn_w_gate_up, dtype=np.float32))
    wp = np.ascontiguousarray(np.asarray(L0_ffn_w_proj, dtype=np.float32))

    nc = build_nc()
    in_maps = []
    bo_tile = np.ascontiguousarray(bo.reshape(16, 128).T)
    for c in range(8):
        b, g = c // 4, c % 4
        m = dict(_host_consts(g))
        m["x"] = x[b]
        m["xT"] = np.ascontiguousarray(x[b].T)
        m["x_own"] = np.ascontiguousarray(x[b, g * OWN:(g + 1) * OWN])
        m["wq"] = np.ascontiguousarray(Wq[:, g * 512:(g + 1) * 512])
        m["wk"] = np.ascontiguousarray(Wk[:, g * 128:(g + 1) * 128])
        m["wv"] = np.ascontiguousarray(Wv[:, g * 128:(g + 1) * 128])
        m["wo"] = np.ascontiguousarray(Wo[g * 512:(g + 1) * 512, :])
        m["bo_t"] = bo_tile
        m["wgu_sh"] = np.ascontiguousarray(wgu[c * 2048:(c + 1) * 2048, :])
        m["wp_sh"] = np.ascontiguousarray(wp[:, c * 1024:(c + 1) * 1024])
        in_maps.append(m)

    res = run_bass_kernel_spmd(nc, in_maps, core_ids=list(range(8)))
    out = np.empty((B, T, D), np.float32)
    for c in range(8):
        b, g = c // 4, c % 4
        out[b, g * OWN:(g + 1) * OWN] = res.results[c]["out"].T
    return out


if __name__ == "__main__":
    print("building program...")
    nc = build_nc()
    print("instructions:", nc.n_instructions())


# version 29
# speedup vs baseline: 1.0325x; 1.0325x over previous
import math
import contextlib
import numpy as np
import ml_dtypes
import concourse.bass as bass
import concourse.mybir as mybir
from concourse.bass_utils import run_bass_kernel_spmd

F32 = mybir.dt.float32
F32R = mybir.dt.float32r
BF16 = mybir.dt.bfloat16
AF = mybir.ActivationFunctionType
ALU = mybir.AluOpType
AX = mybir.AxisListType
BF = ml_dtypes.bfloat16

B, T, D = 2, 4096, 2048
H, DQ, DV = 16, 128, 128
HFF = 8192
NW = T // 512
OWN = 1024
CAP = 30.0
CSC = float(np.float32(1.0 / (math.sqrt(DQ) * CAP)))
NEG = -1.0e30
LN_EPS = 1e-5
RMS_EPS = float(np.finfo(np.float32).eps)
MAGIC = 12582912.0
QCLIP = 1.4375

COMBOS = [(qg, kc) for qg in range(8) for kc in range(4 * qg + 4)]
NCOMB = len(COMBOS)
CIDX = {c: i for i, c in enumerate(COMBOS)}


def _host_consts(g):
    o = {}
    invf = 1.0 / (10000.0 ** (np.arange(0, DQ, 2) / DQ))
    pos = np.arange(T, dtype=np.float64)
    fr2 = np.repeat(pos[None, :] * invf[:, None], 2, axis=0)
    o["cos_t"] = np.cos(fr2).astype(np.float32)
    o["sin_t"] = np.sin(fr2).astype(np.float32)
    P = np.zeros((DQ, DQ), np.float32)
    for i in range(DQ // 2):
        P[2 * i, 2 * i + 1] = -1.0
        P[2 * i + 1, 2 * i] = 1.0
    o["rotT"] = np.ascontiguousarray(P.T.astype(BF))
    o["ident_bf"] = np.eye(128, dtype=np.float32).astype(BF)
    o["ident_f32"] = np.eye(128, dtype=np.float32)
    m = np.zeros((128, 4 * 512), np.float64)
    pp = np.arange(128)[:, None]
    dqi = np.arange(512)[None, :]
    for r in range(4):
        m[:, r * 512:(r + 1) * 512] = np.where(pp + r * 128 > dqi, NEG, 0.0)
    o["masks"] = m.astype(np.float32)
    slopes = [2.0 ** (-(i + 1) * 8.0 / H) for i in range(H)]
    bias = np.zeros((128, 4 * NCOMB), np.float64)
    rhs = np.zeros((1, 4 * 512), np.float64)
    p64 = np.arange(128, dtype=np.float64)
    d64 = np.arange(512, dtype=np.float64)
    for hl in range(4):
        s = slopes[4 * g + hl]
        for ci, (qg, kc) in enumerate(COMBOS):
            bias[:, hl * NCOMB + ci] = CSC * s * (kc * 128 + p64 - qg * 512 - 256)
        rhs[0, hl * 512:(hl + 1) * 512] = -CSC * s * (d64 - 256)
    o["alibias"] = bias.astype(np.float32)
    o["aliq"] = rhs.astype(np.float32)
    o["ones_r"] = np.ones((1, 128), np.float32)
    o["eps_ln"] = np.full((128, 1), LN_EPS, np.float32)
    o["eps_rms"] = np.full((128, 1), RMS_EPS, np.float32)
    o["magicb"] = np.full((128, 1), MAGIC, np.float32)
    o["magicnb"] = np.full((128, 1), -MAGIC, np.float32)
    o["ones_col"] = np.ones((128, 1), np.float32).astype(BF)
    return o


class EQ:
    def __init__(self):
        self.items = []

    def add(self, fn):
        self.items.append(fn)


class Sem:
    def __init__(self, es, nc, name):
        self.h = es.enter_context(nc.semaphore(name))
        self.v = 0

    def w(self, q, val=None):
        tv = self.v if val is None else val
        q.add(lambda e, s=self.h, t=tv: e.wait_ge(s, t))
        return tv


class DS:

    def __init__(self, es, nc, name, k=2):
        self.sems = [Sem(es, nc, f"{name}{i}") for i in range(k)]
        self.k = k
        self.n = 0

    def emit(self, q, build):
        s = self.sems[self.n % self.k]
        if s.v > 0:
            q.add(lambda e, h=s.h, t=s.v: e.wait_ge(h, t))
        s.v += 16
        q.add(lambda e, b=build, h=s.h: b(e).then_inc(h, 16))
        self.n += 1

    def wait_upto(self, q, n):
        for j in range(self.k):
            need = (n - j + self.k - 1) // self.k
            if need > 0:
                q.add(lambda e, h=self.sems[j].h, t=16 * need: e.wait_ge(h, t))

    def wait_all(self, q):
        self.wait_upto(q, self.n)


def _op(q, build, sem=None, n=1):
    if sem is None:
        q.add(lambda e, b=build: b(e))
        return None
    sem.v += n
    q.add(lambda e, b=build, s=sem.h, k=n: b(e).then_inc(s, k))
    return sem.v


def _opm(q, build, sems):
    pairs = []
    for sem, n in sems:
        sem.v += n
        pairs.append((sem.h, n))

    def runner(e, b=build, pairs=tuple(pairs)):
        ins = b(e)
        for h, k in pairs:
            ins = ins.then_inc(h, k)
        return ins
    q.add(runner)


def build_nc(dbg=(), ph=9):
    dbg = set(dbg)
    nc = bass.Bass("TRN2", target_bir_lowering=False, debug=False)

    x_in = nc.dram_tensor("x", [T, D], F32, kind="ExternalInput").ap()
    xT_in = nc.dram_tensor("xT", [D, T], F32, kind="ExternalInput").ap()
    xown_in = nc.dram_tensor("x_own", [OWN, D], F32, kind="ExternalInput").ap()
    wq_in = nc.dram_tensor("wq", [D, 512], F32, kind="ExternalInput").ap()
    wk_in = nc.dram_tensor("wk", [D, DQ], F32, kind="ExternalInput").ap()
    wv_in = nc.dram_tensor("wv", [D, DV], F32, kind="ExternalInput").ap()
    wo_in = nc.dram_tensor("wo", [512, D], F32, kind="ExternalInput").ap()
    bo_in = nc.dram_tensor("bo_t", [128, 16], F32, kind="ExternalInput").ap()
    wgu_in = nc.dram_tensor("wgu_sh", [2048, D], F32, kind="ExternalInput").ap()
    wp_in = nc.dram_tensor("wp_sh", [D, 1024], F32, kind="ExternalInput").ap()
    cos_in = nc.dram_tensor("cos_t", [128, T], F32, kind="ExternalInput").ap()
    sin_in = nc.dram_tensor("sin_t", [128, T], F32, kind="ExternalInput").ap()
    rotT_in = nc.dram_tensor("rotT", [128, 128], BF16, kind="ExternalInput").ap()
    identb_in = nc.dram_tensor("ident_bf", [128, 128], BF16, kind="ExternalInput").ap()
    identf_in = nc.dram_tensor("ident_f32", [128, 128], F32, kind="ExternalInput").ap()
    masks_in = nc.dram_tensor("masks", [128, 4 * 512], F32, kind="ExternalInput").ap()
    alib_in = nc.dram_tensor("alibias", [128, 4 * NCOMB], F32, kind="ExternalInput").ap()
    aliq_in = nc.dram_tensor("aliq", [1, 4 * 512], F32R, kind="ExternalInput").ap()
    onesr_in = nc.dram_tensor("ones_r", [1, 128], F32R, kind="ExternalInput").ap()
    onesc_in = nc.dram_tensor("ones_col", [128, 1], BF16, kind="ExternalInput").ap()
    epsln_in = nc.dram_tensor("eps_ln", [128, 1], F32, kind="ExternalInput").ap()
    epsrms_in = nc.dram_tensor("eps_rms", [128, 1], F32, kind="ExternalInput").ap()
    magicb_in = nc.dram_tensor("magicb", [128, 1], F32, kind="ExternalInput").ap()
    magicnb_in = nc.dram_tensor("magicnb", [128, 1], F32, kind="ExternalInput").ap()

    out_ext = nc.dram_tensor("out", [D, OWN], F32, kind="ExternalOutput").ap()
    dbg_t = {}
    for name, shape, dt in dbg:
        dbg_t[name] = nc.dram_tensor("dbg_" + name, list(shape), dt,
                                     kind="ExternalOutput").ap()

    gshT_d = nc.dram_tensor("gshT_d", [2048, 2048], BF16).ap()
    gagT_d = nc.dram_tensor("gagT_d", [8 * 2048, 2048], BF16, addr_space="Shared").ap()
    pshT_d = nc.dram_tensor("pshT_d", [1024, 2048], BF16).ap()
    pagT_d = nc.dram_tensor("pagT_d", [8 * 1024, 2048], BF16, addr_space="Shared").ap()
    attnp_d = nc.dram_tensor("attnp_d", [4 * 4 * 512, 1024], F32).ap()
    attnr_d = nc.dram_tensor("attnr_d", [D, OWN], F32).ap()
    h_d = nc.dram_tensor("h_d", [HFF, OWN], BF16).ap()

    ENGS = ("sync", "vector", "scalar", "tensor", "gpsimd")
    q = {e: EQ() for e in ENGS}
    SY, VE, SC, TE, GP = (q[e] for e in ENGS)

    def _flush(blk):
        @blk.sync
        def _(e):
            for fn in SY.items:
                fn(e)

        @blk.vector
        def _(e):
            for fn in VE.items:
                fn(e)

        @blk.scalar
        def _(e):
            for fn in SC.items:
                fn(e)

        @blk.tensor
        def _(e):
            for fn in TE.items:
                fn(e)

        @blk.gpsimd
        def _(e):
            for fn in GP.items:
                fn(e)

    es = contextlib.ExitStack()
    with es:
        S = {}
        for n_ in ["app", "trp", "tev",
                   "mmq", "qev", "rot", "rope", "vmm", "vev",
                   "lnA", "lnB", "lnC", "lnD", "lnE",
                   "qd1", "qdg", "qrc", "qtt", "qte", "qmg", "ccA", "ccB",
                   "ms", "mt", "mm_", "mp", "pvd", "mce", "mct", "mcf",
                   "wo1", "wo2", "rs",
                   "xoA", "xoB", "xoC", "xoD", "xoE", "xoap", "xotr", "xotv",
                   "ff1", "ffs", "ffh", "ffq", "ffo",
                   "rsA", "rsB", "rsC",
                   "f2e", "f2f", "f2t", "f2a", "f2x", "f2m"]:
            S[n_] = Sem(es, nc, n_)
        ldc = DS(es, nc, "ldc", 4)
        ldc2 = DS(es, nc, "ldc2", 3)
        ldw = DS(es, nc, "ldw", 2)
        ldx = DS(es, nc, "ldx", 2)
        ldcs = DS(es, nc, "ldcs", 2)
        qg1 = DS(es, nc, "qg1", 2)
        qg2 = DS(es, nc, "qg2", 2)
        sast = DS(es, nc, "sast", 4)
        xol = DS(es, nc, "xol", 4)
        tg1 = DS(es, nc, "tg1", 2)
        tg2 = DS(es, nc, "tg2", 2)
        hst = DS(es, nc, "hst", 2)
        hld = DS(es, nc, "hld", 4)
        tp1 = DS(es, nc, "tp1", 4)
        axl = DS(es, nc, "axl", 2)
        fin = DS(es, nc, "fin", 2)
        dbgs = DS(es, nc, "dbgs", 1)

        blk = es.enter_context(nc.Block())

        rotT = es.enter_context(nc.sbuf_tensor("rotT_s", [128, 128], BF16))
        identb = es.enter_context(nc.sbuf_tensor("identb_s", [128, 128], BF16))
        identf = es.enter_context(nc.sbuf_tensor("identf_s", [128, 128], F32))
        onesr = es.enter_context(nc.sbuf_tensor("onesr_s", [1, 128], F32R))
        onesc = es.enter_context(nc.sbuf_tensor("onesc_s", [128, 1], BF16))
        bo_t = es.enter_context(nc.sbuf_tensor("bo_t_s", [128, 16], F32))
        epsln = es.enter_context(nc.sbuf_tensor("epsln_s", [128, 1], F32))
        epsrms = es.enter_context(nc.sbuf_tensor("epsrms_s", [128, 1], F32))
        magicb = es.enter_context(nc.sbuf_tensor("magicb_s", [128, 1], F32))
        magicnb = es.enter_context(nc.sbuf_tensor("magicnb_s", [128, 1], F32))
        esQW = contextlib.ExitStack()
        qw = esQW.enter_context(nc.sbuf_tensor("qw", [128, 4 * D], BF16, side="right"))
        qd = esQW.enter_context(nc.sbuf_tensor("qd", [128, 3 * D], BF16, side="right"))
        qsum = esQW.enter_context(nc.sbuf_tensor("qsum", [128, 32], F32, side="right"))
        qs_bf = esQW.enter_context(nc.sbuf_tensor("qs_bf", [128, 8 * 32], BF16, side="right"))
        qrcp = esQW.enter_context(nc.sbuf_tensor("qrcp", [128, 8 * 32], F32, side="right"))
        qdT = esQW.enter_context(nc.sbuf_tensor("qdT", [128, 2048], BF16, side="right"))
        qmagf = esQW.enter_context(nc.sbuf_tensor("qmagf", [128, 2048], F32, side="right"))

        for dst, srct in [(rotT, rotT_in), (identb, identb_in), (identf, identf_in),
                          (onesr, onesr_in), (onesc, onesc_in), (bo_t, bo_in),
                          (epsln, epsln_in), (epsrms, epsrms_in),
                          (magicb, magicb_in), (magicnb, magicnb_in)]:
            ldc.emit(SY, lambda e, d=dst, s=srct: e.dma_start(out=d[:], in_=s))

        esAtt = contextlib.ExitStack()
        q_fm = esAtt.enter_context(nc.sbuf_tensor("q_fm", [128, 4 * T], BF16))
        k_fm = esAtt.enter_context(nc.sbuf_tensor("k_fm", [128, T], BF16))
        v_tm = esAtt.enter_context(nc.sbuf_tensor("v_tm", [128, 32 * (DV + 1)], BF16))

        esP1 = contextlib.ExitStack()
        xwin = esP1.enter_context(nc.sbuf_tensor("xwin", [128, 4 * D], F32))
        xn_tm = esP1.enter_context(nc.sbuf_tensor("xn_tm", [128, 2 * D], BF16))
        xnw = esP1.enter_context(nc.sbuf_tensor("xnw", [128, 2 * 16 * 512], BF16))
        wq_s = esP1.enter_context(nc.sbuf_tensor("wq_s", [128, 16 * 512], BF16))
        wk_s = esP1.enter_context(nc.sbuf_tensor("wk_s", [128, 16 * 128], BF16))
        wv_s = esP1.enter_context(nc.sbuf_tensor("wv_s", [128, 16 * 128], BF16))
        cosw = esP1.enter_context(nc.sbuf_tensor("cosw", [128, 2 * 512], F32))
        sinw = esP1.enter_context(nc.sbuf_tensor("sinw", [128, 2 * 512], F32))
        lnsum = esP1.enter_context(nc.sbuf_tensor("lnsum", [128, 4], F32))
        lnsq = esP1.enter_context(nc.sbuf_tensor("lnsq", [128, 4], F32))
        lnmean = esP1.enter_context(nc.sbuf_tensor("lnmean", [128, 4], F32))
        lnm2 = esP1.enter_context(nc.sbuf_tensor("lnm2", [128, 4], F32))
        lnvar = esP1.enter_context(nc.sbuf_tensor("lnvar", [128, 4], F32))
        lnrs = esP1.enter_context(nc.sbuf_tensor("lnrs", [128, 4], F32))
        lnnmr = esP1.enter_context(nc.sbuf_tensor("lnnmr", [128, 4], F32))
        lnjunk = esP1.enter_context(nc.sbuf_tensor("lnjunk", [128, D], BF16))
        qst = esP1.enter_context(nc.sbuf_tensor("qst", [128, 512], BF16))
        rt1 = esP1.enter_context(nc.sbuf_tensor("rt1", [128, 512], F32))
        rt2 = esP1.enter_context(nc.sbuf_tensor("rt2", [128, 512], F32))

        esP1p = contextlib.ExitStack()
        ps_tr = [esP1p.enter_context(nc.psum_tensor(f"ps_tr{i}", [128, 1024], BF16))
                 for i in range(2)]
        ps_qkv = esP1p.enter_context(nc.psum_tensor("ps_qkv", [128, 512], F32))
        ps_rot = esP1p.enter_context(nc.psum_tensor("ps_rot", [128, 512], F32))
        ps_v = [esP1p.enter_context(nc.psum_tensor(f"ps_v{i}", [128, 512], F32))
                for i in range(2)]
        ps_qt = [esP1p.enter_context(nc.psum_tensor(f"ps_qt{i}", [128, 1024], BF16))
                 for i in range(2)]

        ldw.emit(GP, lambda e: e.dma_start(
            out=wq_s.ap().rearrange("p (c m) -> p c m", c=16),
            in_=wq_in.rearrange("(c p) m -> p c m", p=128)))
        ldw.emit(GP, lambda e: e.dma_start(
            out=wk_s.ap().rearrange("p (c m) -> p c m", c=16),
            in_=wk_in.rearrange("(c p) m -> p c m", p=128)))
        ldw.emit(GP, lambda e: e.dma_start(
            out=wv_s.ap().rearrange("p (c m) -> p c m", c=16),
            in_=wv_in.rearrange("(c p) m -> p c m", p=128)))
        def FC(t):
            return 2 * (t + 1) if t < 16 else t + 17

        nqf = [0]

        def emit_qtrans(qt):
            nblk = 16 if qt < 16 else 8
            dp = qt % 3
            S["qdg"].w(TE, QDG_MUL(qt))
            for fi in range(nblk // 8):
                bank = nqf[0] % 2
                if nqf[0] >= 2:
                    S["qte"].w(TE, nqf[0] - 1)
                for j in range(8):
                    _op(TE, lambda e, fi=fi, j=j, bank=bank, dp=dp: e.transpose(
                        ps_qt[bank][:, j * 128:(j + 1) * 128],
                        qd[:, dp * D + (fi * 8 + j) * 128:dp * D + (fi * 8 + j + 1) * 128],
                        identb[:]), S["qtt"] if j == 7 else None, 1)
                nqf[0] += 1
            for fi in range(nblk // 8):
                nf = nqf[0] - nblk // 8 + fi
                bank = nf % 2
                S["qtt"].w(SC, nf + 1)
                if fi == 0 and qt >= 1:
                    qg2.wait_upto(SC, qt)
                _op(SC, lambda e, fi=fi, bank=bank: e.copy(
                    out=qdT[:, fi * 1024:(fi + 1) * 1024], in_=ps_qt[bank][:]),
                    S["qte"], 1)
            if qt < 16:
                dstp = gshT_d.rearrange("(j p) c -> p j c", p=128)[
                    :, :, qt * 128:(qt + 1) * 128]
                srcp = qdT.ap().rearrange("p (j c) -> p j c", c=128)
            else:
                dstp = pshT_d.rearrange("(j p) c -> p j c", p=128)[
                    :, :, (qt - 16) * 128:(qt - 16 + 1) * 128]
                srcp = qdT.ap()[:, 0:1024].rearrange("p (j c) -> p j c", c=128)
            S["qte"].w(SC, nqf[0])
            qg2.emit(SC, lambda e, dstp=dstp, srcp=srcp: e.dma_start(
                out=dstp, in_=srcp))

        def qsrc(qt):
            return (wgu_in[qt * 128:(qt + 1) * 128, :] if qt < 16
                    else wp_in[(qt - 16) * 128:(qt - 16 + 1) * 128, :])

        def emit_qred(qt):
            wid = 2048 if qt < 16 else 1024
            qp, s8 = qt % 4, qt % 8
            qg1.wait_upto(VE, qt + 1)
            _op(VE, lambda e, wid=wid, qp=qp: e.tensor_reduce(
                out=qsum[:, 0:wid // 64], in_=qw.ap()[:, qp * D:qp * D + wid].rearrange(
                    "p (g e) -> p g e", e=64),
                axis=AX.X, op=ALU.add, apply_absolute_value=True), S["qd1"], 1)
            S["qd1"].w(VE)
            if qt >= 8:
                S["qdg"].w(VE, QDG_MUL(qt - 8))
            _op(VE, lambda e, ng=wid // 64, s8=s8: e.tensor_scalar(
                out=qs_bf[:, s8 * 32:s8 * 32 + ng], in0=qsum[:, 0:ng], scalar1=1.0 / 64,
                scalar2=None, op0=ALU.mult), S["qd1"], 1)
            S["qd1"].w(VE)
            _op(VE, lambda e, ng=wid // 64, s8=s8: e.tensor_scalar(
                out=qs_bf[:, s8 * 32:s8 * 32 + ng], in0=qs_bf[:, s8 * 32:s8 * 32 + ng],
                scalar1=1e-8, scalar2=None, op0=ALU.max), S["qd1"], 1)
            S["qd1"].w(VE)
            _op(VE, lambda e, ng=wid // 64, s8=s8: e.reciprocal(
                out=qrcp[:, s8 * 32:s8 * 32 + ng], in_=qs_bf[:, s8 * 32:s8 * 32 + ng]),
                S["qrc"], 1)

        def emit_qmul(mq):
            wid = 2048 if mq < 16 else 1024
            ng = wid // 64
            s8, dp = mq % 8, mq % 3
            S["qmg"].w(GP, 2 * mq + 2)
            _op(GP, lambda e, wid=wid, ng=ng, s8=s8, dp=dp: e.tensor_tensor(
                out=qd[:, dp * D:dp * D + wid].rearrange("p (g e) -> p g e", e=64),
                in0=qd[:, dp * D:dp * D + wid].rearrange("p (g e) -> p g e", e=64),
                in1=qs_bf[:, s8 * 32:s8 * 32 + ng].unsqueeze(2).broadcast_to([128, ng, 64]),
                op=ALU.mult), S["qdg"], 1)
            S["qdg"].w(GP)

        def QDG_CLIP(t):
            if t == 0:
                return 2
            return 50 if t == 16 else 3 * t + 1

        def QDG_MUL(t):
            if t == 15:
                return 48
            return 96 if t == 31 else 3 * t + 5

        def emit_qmagic(qt):
            wid = 2048 if qt < 16 else 1024
            dp = qt % 3
            S["qdg"].w(SC, QDG_CLIP(qt))
            _op(SC, lambda e, wid=wid, dp=dp: e.activation(
                out=qmagf[:, 0:wid], in_=qd[:, dp * D:dp * D + wid],
                func=AF.Identity, scale=1.0, bias=magicb[:]), S["qmg"], 1)
            S["qmg"].w(SC)
            _op(SC, lambda e, wid=wid, dp=dp: e.activation(
                out=qd[:, dp * D:dp * D + wid], in_=qmagf[:, 0:wid],
                func=AF.Identity, scale=1.0, bias=magicnb[:]), S["qmg"], 1)

        for qt in range(4):
            qg1.emit(GP, lambda e, s=qsrc(qt), qp=qt % 4: e.dma_start(
                out=qw[:, qp * D:qp * D + 2048], in_=s))
        for qt in range(32):
            wid = 2048 if qt < 16 else 1024
            ng = wid // 64
            qp, s8, dp = qt % 4, qt % 8, qt % 3
            qg1.wait_upto(GP, qt + 1)
            S["qrc"].w(GP, qt + 1)
            if qt >= 3:
                S["qtt"].w(GP, FC(qt - 3))
            _op(GP, lambda e, wid=wid, ng=ng, qp=qp, s8=s8, dp=dp: e.tensor_tensor(
                out=qd[:, dp * D:dp * D + wid].rearrange("p (g e) -> p g e", e=64),
                in0=qw.ap()[:, qp * D:qp * D + wid].rearrange("p (g e) -> p g e", e=64),
                in1=qrcp[:, s8 * 32:s8 * 32 + ng].unsqueeze(2).broadcast_to([128, ng, 64]),
                op=ALU.mult), S["qdg"], 1)
            S["qdg"].w(GP)
            if qt + 4 < 32:
                S["qd1"].w(GP, 3 * qt + 1)
                qg1.emit(GP, lambda e, s=qsrc(qt + 4), qp=qp, w2=(2048 if qt + 4 < 16 else 1024): e.dma_start(
                    out=qw[:, qp * D:qp * D + w2], in_=s))
            _op(GP, lambda e, wid=wid, dp=dp: e.tensor_scalar(
                out=qd[:, dp * D:dp * D + wid], in0=qd[:, dp * D:dp * D + wid],
                scalar1=QCLIP, scalar2=-QCLIP,
                op0=ALU.min, op1=ALU.max), S["qdg"], 1)
            S["qdg"].w(GP)
            if qt >= 1 and qt != 16:
                emit_qmul(qt - 1)
            if qt == 15:
                emit_qmul(15)
            if qt == 31:
                emit_qmul(31)
            if qt == 15:
                qg2.wait_upto(GP, 16)
                _op(GP, lambda e: e.collective_compute(
                    "AllGather", ALU.bypass, replica_groups=[list(range(8))],
                    ins=[gshT_d.opt()], outs=[gagT_d.opt()]), S["ccA"], 1)
        S["qd1"].w(GP, 96)
        ldw.emit(GP, lambda e: e.dma_start(
            out=wo_s.ap().rearrange("p (c m) -> p c m", c=4),
            in_=wo_in.rearrange("(c p) m -> p c m", p=128)))

        ldc.wait_all(VE)
        _op(VE, lambda e: e.memset(v_tm[:], 1.0), S["vev"], 1)

        for w in range(NW):
            wpar = w % 2
            if w >= 1:
                S["app"].w(SY, w * 4)
            for i in range(4):
                t = 4 * w + i
                ldx.emit(SY, lambda e, i=i, t=t: e.dma_start(
                    out=xwin[:, i * D:(i + 1) * D], in_=x_in[t * 128:(t + 1) * 128, :]))
            if w >= 2:
                S["rope"].w(SY, 15 * (w - 1))
            for srct, dstb in ((cos_in, cosw), (sin_in, sinw)):
                ldcs.emit(SY, lambda e, s=srct, d=dstb, w=w, wpar=wpar: e.dma_start(
                    out=d[:, wpar * 512:(wpar + 1) * 512],
                    in_=s[:, w * 512:(w + 1) * 512]))

            ldx.wait_upto(VE, 4 * (w + 1))
            if w >= 1:
                S["lnE"].w(VE, w)
            for i in range(4):
                _op(VE, lambda e, i=i: e.tensor_reduce(
                    out=lnsum[:, i:i + 1], in_=xwin[:, i * D:(i + 1) * D],
                    axis=AX.X, op=ALU.add), S["lnA"] if i == 3 else None, 1)
            S["lnA"].w(VE)
            _op(VE, lambda e: e.tensor_scalar(out=lnmean[:], in0=lnsum[:],
                scalar1=1.0 / D, scalar2=None, op0=ALU.mult), S["lnA"], 1)
            S["lnA"].w(VE)
            _op(VE, lambda e: e.tensor_tensor(out=lnm2[:], in0=lnmean[:],
                in1=lnmean[:], op=ALU.mult), S["lnA"], 1)
            ldx.wait_upto(SC, 4 * (w + 1))
            if w > 0:
                S["lnB"].w(SC)
            for i in range(4):
                if i > 0:
                    S["lnB"].w(SC)
                _op(SC, lambda e, i=i: e.activation(
                    out=lnjunk[:], in_=xwin[:, i * D:(i + 1) * D], func=AF.Square,
                    accum_out=lnsq[:, i:i + 1]), S["lnB"], 1)
            S["lnB"].w(VE)
            S["lnA"].w(VE)
            _op(VE, lambda e: e.tensor_scalar(
                out=lnvar[:], in0=lnsq[:], scalar1=1.0 / D, scalar2=None,
                op0=ALU.mult), S["lnC"], 1)
            S["lnC"].w(VE)
            _op(VE, lambda e: e.tensor_tensor(
                out=lnvar[:], in0=lnvar[:], in1=lnm2[:],
                op=ALU.subtract), S["lnC"], 1)
            S["lnC"].w(SC)
            _op(SC, lambda e: e.activation(out=lnvar[:], in_=lnvar[:], func=AF.Ln,
                                           bias=epsln[:]), S["lnD"], 1)
            S["lnD"].w(SC)
            _op(SC, lambda e: e.activation(out=lnrs[:], in_=lnvar[:], func=AF.Exp,
                                           scale=-0.5), S["lnD"], 1)
            S["lnD"].w(VE)
            _op(VE, lambda e: e.scalar_tensor_tensor(
                out=lnnmr[:], in0=lnmean[:], scalar=-1.0, in1=lnrs[:],
                op0=ALU.mult, op1=ALU.mult), S["lnE"], 1)
            S["lnE"].w(SC)
            for i in range(4):
                ti = 4 * w + i
                if ti >= 2:
                    S["trp"].w(SC, (ti - 1) * 4)
                _opm(SC, lambda e, i=i: e.activation(
                    out=xn_tm[:, (i % 2) * D:(i % 2 + 1) * D],
                    in_=xwin[:, i * D:(i + 1) * D], func=AF.Identity,
                    scale=lnrs[:, i:i + 1], bias=lnnmr[:, i:i + 1]),
                    [(S["app"], 1)])

            if w == 0:
                ldc.wait_all(TE)
                ldw.wait_upto(TE, 3)
            for i in range(4):
                ti = 4 * w + i
                S["app"].w(TE, ti + 1)
                for grp in range(4):
                    gi = ti * 4 + grp
                    gpar = gi % 2
                    if gi >= 2:
                        S["tev"].w(TE, gi - 1)
                    for cc4 in range(4):
                        dc = grp * 4 + cc4
                        _op(TE, lambda e, i=i, dc=dc, gpar=gpar, cc4=cc4: e.transpose(
                            ps_tr[gpar][:, cc4 * 128:(cc4 + 1) * 128],
                            xn_tm[:, (i % 2) * D + dc * 128:(i % 2) * D + (dc + 1) * 128],
                            identb[:]),
                            S["trp"] if cc4 == 3 else None, 1)
            for i in range(4):
                for grp in range(4):
                    gi = (4 * w + i) * 4 + grp
                    gpar = gi % 2
                    S["trp"].w(VE, gi + 1)
                    _op(VE, lambda e, wpar=wpar, i=i, grp=grp, gpar=gpar: e.tensor_copy(
                        out=xnw.ap().rearrange("p (wn c m) -> p wn c m", wn=2, c=16)[
                            :, wpar, grp * 4:(grp + 1) * 4, i * 128:(i + 1) * 128],
                        in_=ps_tr[gpar][:, 0:512].rearrange("p (c m) -> p c m", c=4)),
                        S["tev"], 1)

            def xnw_sl(dc, wpar):
                base = wpar * 16 * 512 + dc * 512
                return xnw[:, base:base + 512]

            S["tev"].w(TE, (4 * w + 4) * 4)
            for ch in range(5):
                cidx = 5 * w + ch
                if cidx > 0:
                    S["qev"].w(TE, cidx)
                for kc in range(16):
                    if ch < 4:
                        lhs = wq_s[:, kc * 512 + ch * 128: kc * 512 + (ch + 1) * 128]
                    else:
                        lhs = wk_s[:, kc * 128:(kc + 1) * 128]
                    _op(TE, lambda e, lhs=lhs, kc=kc, wpar=wpar: e.matmul(
                        ps_qkv[:], lhs, xnw_sl(kc, wpar), start=(kc == 0), stop=(kc == 15)),
                        S["mmq"] if kc == 15 else None, 1)
                S["mmq"].w(SC, cidx + 1)
                if cidx > 0:
                    S["rope"].w(SC, 3 * cidx - 2)
                _op(SC, lambda e: e.copy(out=qst[:], in_=ps_qkv[:]), S["qev"], 1)
                S["qev"].w(TE, cidx + 1)
                if cidx > 0:
                    S["rope"].w(TE, 3 * cidx - 1)
                _op(TE, lambda e: e.matmul(ps_rot[:], rotT[:], qst[:],
                                           start=True, stop=True), S["rot"], 1)
                if ch == 0:
                    ldcs.wait_upto(VE, 2 * (w + 1))
                if cidx > 0:
                    S["rope"].w(VE, 3 * cidx)
                S["qev"].w(VE, cidx + 1)
                _op(VE, lambda e, wpar=wpar: e.tensor_tensor(
                    out=rt1[:], in0=qst[:], in1=cosw[:, wpar * 512:(wpar + 1) * 512],
                    op=ALU.mult), S["rope"], 1)
                S["rot"].w(VE, cidx + 1)
                _op(VE, lambda e, wpar=wpar: e.tensor_tensor(
                    out=rt2[:], in0=ps_rot[:], in1=sinw[:, wpar * 512:(wpar + 1) * 512],
                    op=ALU.mult), S["rope"], 1)
                S["rope"].w(VE)
                dst = (q_fm[:, ch * T + w * 512: ch * T + (w + 1) * 512] if ch < 4
                       else k_fm[:, w * 512:(w + 1) * 512])
                _op(VE, lambda e, dst=dst: e.tensor_tensor(
                    out=dst, in0=rt1[:], in1=rt2[:], op=ALU.add), S["rope"], 1)

            if w >= 1:
                for qq in range(4):
                    emit_qred(4 * (w - 1) + qq)

            for sub in range(4):
                vi = 4 * w + sub
                if vi >= 2:
                    S["vev"].w(TE, vi)
                for kc in range(16):
                    _op(TE, lambda e, sub=sub, kc=kc, wpar=wpar, vp=vi % 2: e.matmul(
                        ps_v[vp][:, 0:128], xnw_sl(kc, wpar)[:, sub * 128:(sub + 1) * 128],
                        wv_s[:, kc * 128:(kc + 1) * 128],
                        start=(kc == 0), stop=(kc == 15)),
                        S["vmm"] if kc == 15 else None, 1)
                S["vmm"].w(SC, vi + 1)
                if vi == 0:
                    S["vev"].w(SC, 1)
                _op(SC, lambda e, vi=vi, vp=vi % 2: e.copy(
                    out=v_tm[:, vi * (DV + 1):vi * (DV + 1) + DV], in_=ps_v[vp][:, 0:128]),
                    S["vev"], 1)

            if w >= 2:
                for qtr in range(4 * (w - 2), 4 * (w - 1)):
                    emit_qmagic(qtr)
                    emit_qtrans(qtr)
        for qt in range(28, 32):
            emit_qred(qt)
        for qtr in range(24, 32):
            emit_qmagic(qtr)
            emit_qtrans(qtr)

        def w_fb1(q):
            S["rope"].w(q, 120)
            S["vmm"].w(q, 32)
            S["vev"].w(q, 33)

        for nm, buf in [("q_fm", q_fm), ("k_fm", k_fm), ("v_tm", v_tm)]:
            if nm in dbg_t:
                w_fb1(SY)
                dbgs.emit(SY, lambda e, nm=nm, buf=buf: e.dma_start(
                    out=dbg_t[nm], in_=buf[:]))

        esP1p.close()
        esP1.close()

        if ph <= 1:
            dbgs.wait_all(SY)
            _flush(blk)
            esAtt.close()
            return nc

        esQW.close()
        esP2 = contextlib.ExitStack()
        tbuf = esP2.enter_context(nc.sbuf_tensor("tbuf", [128, 4 * 512], F32))
        pbuf = esP2.enter_context(nc.sbuf_tensor("pbuf", [128, 4 * 512], BF16))
        ctmp = esP2.enter_context(nc.sbuf_tensor("ctmp", [128, 4 * 128], BF16))
        crcp = esP2.enter_context(nc.sbuf_tensor("crcp", [128, 4], F32))
        ast = esP2.enter_context(nc.sbuf_tensor("ast", [128, 4 * 512], F32))
        xox = esP2.enter_context(nc.sbuf_tensor("xox", [128, 4 * 512], F32))
        ctx_fmT = esP2.enter_context(nc.sbuf_tensor("ctx_fmT", [128, 4 * T], BF16))
        masks = esP2.enter_context(nc.sbuf_tensor("masks_s", [128, 4 * 512], F32))
        alibias = esP2.enter_context(nc.sbuf_tensor("alibias_s", [128, 4 * NCOMB], F32))
        aliq = esP2.enter_context(nc.sbuf_tensor("aliq_s", [1, 4 * 512], F32R))
        qcomb = esP2.enter_context(nc.sbuf_tensor("qcomb", [128, 20 * 512], F32))
        t2buf = esP2.enter_context(nc.sbuf_tensor("t2buf", [128, 2 * 512], F32))
        wo_s = esP2.enter_context(nc.sbuf_tensor("wo_s", [128, 4 * D], BF16, side="right"))

        w_fb1(SY)
        for dst, srct in [(masks, masks_in), (alibias, alib_in), (aliq, aliq_in)]:
            ldc2.emit(SY, lambda e, d=dst, s=srct: e.dma_start(out=d[:], in_=s))

        esP2p = contextlib.ExitStack()
        ps_s = [esP2p.enter_context(nc.psum_tensor(f"ps_s{i}", [128, 512], F32))
                for i in range(3)]
        ps_c = [[esP2p.enter_context(nc.psum_tensor(f"ps_c{i}{j}", [128, 512], F32))
                 for j in range(2)] for i in range(2)]
        ps_ct = esP2p.enter_context(nc.psum_tensor("ps_ct", [128, 1024], BF16))

        ITERS = [(h, qg, kc) for h in range(4) for qg, kc in COMBOS]
        NIT = len(ITERS)
        GEND = {}
        for i_, (h_, qg_, kc_) in enumerate(ITERS):
            GEND[h_ * 8 + qg_] = i_

        def emit_pv(pidx):
            ph, pqg, pkc = ITERS[pidx]
            ppar = pidx % 4
            gi = ph * 8 + pqg
            gpi = gi % 2
            S["mp"].w(TE, pidx + 1)
            if pkc == 0 and gi >= 2:
                S["mcf"].w(TE, gi - 1)
            last = pkc == 4 * pqg + 3
            for sub in range(4):
                _op(TE, lambda e, ppar=ppar, sub=sub, pkc=pkc, gpi=gpi,
                    st=(pkc == 0 and sub % 2 == 0),
                    sp=(last and sub % 2 == 1): e.matmul(
                    ps_c[gpi][sub // 2][:, (sub % 2) * (DV + 1):
                                        (sub % 2) * (DV + 1) + DV + 1],
                    pbuf[:, ppar * 512 + sub * 128: ppar * 512 + (sub + 1) * 128],
                    v_tm[:, pkc * (DV + 1):(pkc + 1) * (DV + 1)],
                    start=st, stop=sp),
                    S["pvd"] if sub == 3 else None, 1)

        def emit_group_finalize(gi):
            h_, qg_ = gi // 8, gi % 8
            gpi = gi % 2
            S["pvd"].w(VE, GEND[gi] + 1)
            S["mce"].w(VE)
            for sub in range(4):
                _op(VE, lambda e, gpi=gpi, sub=sub: e.reciprocal(
                    out=crcp[:, sub:sub + 1],
                    in_=ps_c[gpi][sub // 2][:, (sub % 2) * (DV + 1) + DV:
                                            (sub % 2) * (DV + 1) + DV + 1]),
                    S["mce"] if sub == 3 else None, 1)
            S["mce"].w(VE)
            if gi >= 1:
                S["mct"].w(VE, 4 * gi)
            for sub in range(4):
                _op(VE, lambda e, gpi=gpi, sub=sub: e.tensor_scalar(
                    out=ctmp[:, sub * 128:(sub + 1) * 128],
                    in0=ps_c[gpi][sub // 2][:, (sub % 2) * (DV + 1):
                                            (sub % 2) * (DV + 1) + DV],
                    scalar1=crcp[:, sub:sub + 1], scalar2=None, op0=ALU.mult),
                    S["mce"], 1)
            S["mce"].w(TE)
            if gi >= 1:
                S["mcf"].w(TE, gi)
            for sub in range(4):
                _op(TE, lambda e, sub=sub: e.transpose(
                    ps_ct[:, sub * 128:(sub + 1) * 128],
                    ctmp[:, sub * 128:(sub + 1) * 128], identb[:]), S["mct"], 1)
            S["mct"].w(VE, 4 * (gi + 1))
            _op(VE, lambda e, h_=h_, qg_=qg_: e.tensor_copy(
                out=ctx_fmT[:, h_ * T + qg_ * 512: h_ * T + (qg_ + 1) * 512],
                in_=ps_ct[:, 0:512]), S["mcf"], 1)

        S["rope"].w(TE, 120)
        S["vev"].w(TE, 33)
        ldc2.wait_upto(TE, 3)
        ldc2.wait_upto(SC, 3)
        ldc2.wait_upto(VE, 3)
        for h in range(4):
            _op(TE, lambda e, h=h: e.matmul(
                ps_s[0][:], onesr[:], aliq[:, h * 512:(h + 1) * 512],
                start=True, stop=True), S["ms"], 1)
            S["ms"].w(SC, h + 1)
            _op(SC, lambda e, h=h: e.copy(
                out=qcomb[:, (h * 5 + 4) * 512:(h * 5 + 5) * 512], in_=ps_s[0][:]),
                S["mt"], 1)
            S["mt"].w(TE, h + 1)
        S["mt"].w(VE, 4)
        for h in range(4):
            for r in range(4):
                _op(VE, lambda e, h=h, r=r: e.tensor_tensor(
                    out=qcomb[:, (h * 5 + r) * 512:(h * 5 + r + 1) * 512],
                    in0=qcomb[:, (h * 5 + 4) * 512:(h * 5 + 5) * 512],
                    in1=masks[:, r * 512:(r + 1) * 512], op=ALU.add),
                    S["mm_"] if (h == 3 and r == 3) else None, 1)
        S["mm_"].w(VE, 1)
        MS0, MT0 = S["ms"].v, S["mt"].v
        S["mt"].w(TE, MT0)
        for idx, (h, qg, kc) in enumerate(ITERS):
            par = idx % 3
            pt = idx % 4
            t2 = idx % 2
            pp = idx % 4
            ci = CIDX[(qg, kc)]
            gi = h * 8 + qg
            if idx >= 3:
                S["mm_"].w(TE, idx - 1)
            _op(TE, lambda e, kc=kc, h=h, qg=qg, par=par: e.matmul(
                ps_s[par][:], k_fm[:, kc * 128:(kc + 1) * 128],
                q_fm[:, h * T + qg * 512:h * T + (qg + 1) * 512],
                start=True, stop=True), S["ms"], 1)
            if idx >= 3:
                emit_pv(idx - 3)
            if kc == 3 and gi >= 1:
                emit_group_finalize(gi - 1)
            sel = (kc - 4 * qg) if kc >= 4 * qg else 4
            S["ms"].w(VE, MS0 + idx + 1)
            if idx >= 4:
                S["mt"].w(VE, MT0 + idx - 3)
            _op(VE, lambda e, par=par, pt=pt, h=h, sel=sel: e.scalar_tensor_tensor(
                out=tbuf[:, pt * 512:(pt + 1) * 512], in0=ps_s[par][:],
                scalar=CSC, in1=qcomb[:, (h * 5 + sel) * 512:(h * 5 + sel + 1) * 512],
                op0=ALU.mult, op1=ALU.add), S["mm_"], 1)
            S["mm_"].w(SC, idx + 2)
            _op(SC, lambda e, pt=pt, t2=t2, h=h, ci=ci: e.activation(
                out=t2buf[:, t2 * 512:(t2 + 1) * 512],
                in_=tbuf[:, pt * 512:(pt + 1) * 512],
                func=AF.Tanh, scale=1.0,
                bias=alibias[:, h * NCOMB + ci:h * NCOMB + ci + 1]), S["mt"], 1)
            S["mt"].w(SC, MT0 + idx + 1)
            if idx >= 4:
                S["pvd"].w(SC, idx - 3)
            _op(SC, lambda e, t2=t2, pp=pp: e.activation(
                out=pbuf[:, pp * 512:(pp + 1) * 512],
                in_=t2buf[:, t2 * 512:(t2 + 1) * 512], func=AF.Exp, scale=CAP),
                S["mp"], 1)
        emit_pv(NIT - 3)
        emit_pv(NIT - 2)
        emit_pv(NIT - 1)
        emit_group_finalize(31)

        if "ctx" in dbg_t:
            S["mcf"].w(SY, 32)
            dbgs.emit(SY, lambda e: e.dma_start(out=dbg_t["ctx"], in_=ctx_fmT[:]))

        if ph <= 2:
            S["mcf"].w(SY, 32)
            S["ccA"].w(SY, 1)
            S["ccB"].w(SY, 1)
            dbgs.wait_all(SY)
            _flush(blk)
            esP2p.close(); esP2.close(); esAtt.close()
            return nc

        S["mcf"].w(TE, 32)
        ldw.wait_upto(TE, 4)
        n3 = 0
        for dk in range(16):
            for tg in range(8):
                par = n3 % 2
                sp4 = n3 % 4
                if n3 >= 4:
                    S["wo2"].w(SY, n3 - 3)
                xol.emit(SY, lambda e, dk=dk, tg=tg, sp4=sp4: e.dma_start(
                    out=xox[:, sp4 * 512:(sp4 + 1) * 512],
                    in_=xT_in[dk * 128:(dk + 1) * 128, tg * 512:(tg + 1) * 512]))
                if n3 >= 2:
                    S["wo2"].w(TE, n3 - 1)
                for hh in range(4):
                    _op(TE, lambda e, dk=dk, tg=tg, hh=hh, par=par: e.matmul(
                        ps_s[par][:],
                        wo_s[:, hh * D + dk * 128:hh * D + (dk + 1) * 128],
                        ctx_fmT[:, hh * T + tg * 512:hh * T + (tg + 1) * 512],
                        start=(hh == 0), stop=(hh == 3)),
                        S["wo1"] if hh == 3 else None, 1)
                S["wo1"].w(VE, n3 + 1)
                xol.wait_upto(VE, n3 + 1)
                if n3 >= 4:
                    sast.wait_upto(VE, n3 - 3)
                _op(VE, lambda e, par=par, sp4=sp4: e.scalar_tensor_tensor(
                    out=ast[:, sp4 * 512:(sp4 + 1) * 512],
                    in0=xox[:, sp4 * 512:(sp4 + 1) * 512], scalar=0.25,
                    in1=ps_s[par][:], op0=ALU.mult, op1=ALU.add),
                    S["wo2"], 1)
                S["wo2"].w(SC, n3 + 1)
                prow = (dk // 4) * 2048 + (tg // 2) * 512 + (dk % 4) * 128
                pcol = (tg % 2) * 512
                sast.emit(SC, lambda e, prow=prow, pcol=pcol, sp4=sp4: e.dma_start(
                    out=attnp_d[prow:prow + 128, pcol:pcol + 512],
                    in_=ast[:, sp4 * 512:(sp4 + 1) * 512]))
                n3 += 1
        qg2.wait_upto(GP, 32)
        _op(GP, lambda e: e.collective_compute(
            "AllGather", ALU.bypass, replica_groups=[list(range(8))],
            ins=[pshT_d.opt()], outs=[pagT_d.opt()]), S["ccB"], 1)
        for p in range(4):
            sast.wait_upto(GP, 32 * (p + 1))
            _op(GP, lambda e, p=p: e.collective_compute(
                "ReduceScatter", ALU.add, replica_groups=[[0, 1, 2, 3], [4, 5, 6, 7]],
                ins=[attnp_d[p * 2048:(p + 1) * 2048, :].opt()],
                outs=[attnr_d[p * 512:(p + 1) * 512, :].opt()]), S["rs"], 1)

        def w_fb2(q):
            S["mcf"].w(q, 32)
            S["wo1"].w(q, 128)
            S["wo2"].w(q, 128)

        if "attnr" in dbg_t:
            S["rs"].w(SY, 4)
            dbgs.emit(SY, lambda e: e.dma_start(out=dbg_t["attnr"], in_=attnr_d))

        if ph <= 3:
            S["rs"].w(SY, 4)
            dbgs.wait_all(SY)
            _flush(blk)
            esP2p.close(); esP2.close(); esAtt.close()
            return nc

        esP2p.close()
        esP2.close()
        esAtt.close()
        for q in (SY, SC, VE, GP):
            qg2.wait_all(q)

        esFFa = contextlib.ExitStack()
        xn_own = esFFa.enter_context(nc.sbuf_tensor("xn_own", [128, 16 * 1024], BF16))
        rstd_fm = esFFa.enter_context(nc.sbuf_tensor("rstd_fm", [128, 2 * 512], F32))
        rstt = esFFa.enter_context(nc.sbuf_tensor("rstt", [1, 2 * 512], F32))
        rsttr = esFFa.enter_context(nc.sbuf_tensor("rsttr", [1, 2 * 512], F32R))

        esX = contextlib.ExitStack()
        xw2 = esX.enter_context(nc.sbuf_tensor("xw2", [128, 4 * D], F32))
        xt2 = esX.enter_context(nc.sbuf_tensor("xt2", [128, 2 * D], BF16))
        l2sum = esX.enter_context(nc.sbuf_tensor("l2sum", [128, 4], F32))
        l2sq = esX.enter_context(nc.sbuf_tensor("l2sq", [128, 4], F32))
        l2mean = esX.enter_context(nc.sbuf_tensor("l2mean", [128, 4], F32))
        l2m2 = esX.enter_context(nc.sbuf_tensor("l2m2", [128, 4], F32))
        l2var = esX.enter_context(nc.sbuf_tensor("l2var", [128, 4], F32))
        l2rs = esX.enter_context(nc.sbuf_tensor("l2rs", [128, 4], F32))
        l2nmr = esX.enter_context(nc.sbuf_tensor("l2nmr", [128, 4], F32))
        l2junk = esX.enter_context(nc.sbuf_tensor("l2junk", [128, D], BF16))
        esXp = contextlib.ExitStack()
        ps_x2 = [esXp.enter_context(nc.psum_tensor(f"ps_x2{i}", [128, 1024], BF16))
                 for i in range(2)]

        w_fb2(SY)
        sast.wait_all(SY)
        w_fb2(VE)
        w_fb2(SC)
        for w in range(2):
            if w >= 1:
                S["xoap"].w(SY, w * 4)
            for i in range(4):
                t = 4 * w + i
                ldx.emit(SY, lambda e, i=i, t=t: e.dma_start(
                    out=xw2[:, i * D:(i + 1) * D], in_=xown_in[t * 128:(t + 1) * 128, :]))
            ldx.wait_upto(VE, 32 + 4 * (w + 1))
            if w >= 1:
                S["xoE"].w(VE, w)
            for i in range(4):
                _op(VE, lambda e, i=i: e.tensor_reduce(
                    out=l2sum[:, i:i + 1], in_=xw2[:, i * D:(i + 1) * D],
                    axis=AX.X, op=ALU.add), S["xoA"] if i == 3 else None, 1)
            S["xoA"].w(VE)
            _op(VE, lambda e: e.tensor_scalar(out=l2mean[:], in0=l2sum[:],
                scalar1=1.0 / D, scalar2=None, op0=ALU.mult), S["xoA"], 1)
            S["xoA"].w(VE)
            _op(VE, lambda e: e.tensor_tensor(out=l2m2[:], in0=l2mean[:],
                in1=l2mean[:], op=ALU.mult), S["xoA"], 1)
            ldx.wait_upto(SC, 32 + 4 * (w + 1))
            if w > 0:
                S["xoB"].w(SC)
            for i in range(4):
                if i > 0:
                    S["xoB"].w(SC)
                _op(SC, lambda e, i=i: e.activation(
                    out=l2junk[:], in_=xw2[:, i * D:(i + 1) * D], func=AF.Square,
                    accum_out=l2sq[:, i:i + 1]), S["xoB"], 1)
            S["xoB"].w(VE)
            S["xoA"].w(VE)
            _op(VE, lambda e: e.tensor_scalar(
                out=l2var[:], in0=l2sq[:], scalar1=1.0 / D, scalar2=None,
                op0=ALU.mult), S["xoC"], 1)
            S["xoC"].w(VE)
            _op(VE, lambda e: e.tensor_tensor(
                out=l2var[:], in0=l2var[:], in1=l2m2[:],
                op=ALU.subtract), S["xoC"], 1)
            S["xoC"].w(SC)
            _op(SC, lambda e: e.activation(out=l2var[:], in_=l2var[:], func=AF.Ln,
                                           bias=epsln[:]), S["xoD"], 1)
            S["xoD"].w(SC)
            _op(SC, lambda e: e.activation(out=l2rs[:], in_=l2var[:], func=AF.Exp,
                                           scale=-0.5), S["xoD"], 1)
            S["xoD"].w(VE)
            _op(VE, lambda e: e.scalar_tensor_tensor(
                out=l2nmr[:], in0=l2mean[:], scalar=-1.0, in1=l2rs[:],
                op0=ALU.mult, op1=ALU.mult), S["xoE"], 1)
            S["xoE"].w(SC)
            for i in range(4):
                ti = 4 * w + i
                if ti >= 2:
                    S["xotr"].w(SC, (ti - 1) * 4)
                _op(SC, lambda e, i=i: e.activation(
                    out=xt2[:, (i % 2) * D:(i % 2 + 1) * D],
                    in_=xw2[:, i * D:(i + 1) * D], func=AF.Identity,
                    scale=l2rs[:, i:i + 1], bias=l2nmr[:, i:i + 1]), S["xoap"], 1)
            for i in range(4):
                ti = 4 * w + i
                S["xoap"].w(TE, ti + 1)
                for grp in range(4):
                    gi2 = ti * 4 + grp
                    gpar = gi2 % 2
                    if gi2 >= 2:
                        S["xotv"].w(TE, gi2 - 1)
                    for cc4 in range(4):
                        dc = grp * 4 + cc4
                        _op(TE, lambda e, i=i, dc=dc, gpar=gpar, cc4=cc4: e.transpose(
                            ps_x2[gpar][:, cc4 * 128:(cc4 + 1) * 128],
                            xt2[:, (i % 2) * D + dc * 128:(i % 2) * D + (dc + 1) * 128],
                            identb[:]), S["xotr"] if cc4 == 3 else None, 1)
            for i in range(4):
                ti = 4 * w + i
                for grp in range(4):
                    gi2 = ti * 4 + grp
                    gpar = gi2 % 2
                    S["xotr"].w(VE, gi2 + 1)
                    _op(VE, lambda e, ti=ti, grp=grp, gpar=gpar: e.tensor_copy(
                        out=xn_own.ap().rearrange("p (c m) -> p c m", c=16)[
                            :, grp * 4:(grp + 1) * 4, ti * 128:(ti + 1) * 128],
                        in_=ps_x2[gpar][:, 0:512].rearrange("p (c m) -> p c m", c=4)),
                        S["xotv"], 1)
        XO_TV = S["xotv"].v

        if "xn_own" in dbg_t:
            S["xotv"].w(SY, XO_TV)
            dbgs.emit(SY, lambda e: e.dma_start(out=dbg_t["xn_own"], in_=xn_own[:]))

        def w_fb3(q):
            S["xoap"].w(q, 8)
            S["xotr"].w(q, 32)
            S["xotv"].w(q, 32)

        esXp.close()
        esX.close()

        esF1 = contextlib.ExitStack()
        wgt_g = esF1.enter_context(nc.sbuf_tensor("wgt_g", [128, 2 * 16 * 1024], BF16))
        wgt_u = esF1.enter_context(nc.sbuf_tensor("wgt_u", [128, 2 * 16 * 1024], BF16))
        sgbuf = esF1.enter_context(nc.sbuf_tensor("sgbuf", [128, 2 * 512], F32))
        hstg = esF1.enter_context(nc.sbuf_tensor("hstg", [128, 2 * 512], BF16))
        h2stg = esF1.enter_context(nc.sbuf_tensor("h2stg", [128, 2 * 512], BF16))
        esF1p = contextlib.ExitStack()
        ps_g = [esF1p.enter_context(nc.psum_tensor(f"ps_g{i}", [128, 512], F32))
                for i in range(2)]
        ps_u = [esF1p.enter_context(nc.psum_tensor(f"ps_u{i}", [128, 512], F32))
                for i in range(2)]
        ps_r2 = [esF1p.enter_context(nc.psum_tensor(f"ps_r2{i}", [128, 512], F32))
                 for i in range(2)]

        NBLK = 8

        def emit_treads(bi):
            r, hb = bi // 2, bi % 2
            bpar = bi % 2
            if bi >= 2:
                S["ff1"].w(SY, (bi - 1) * 32)
            for dc in range(16):
                tg1.emit(SY, lambda e, bpar=bpar, dc=dc, r=r, hb=hb: e.dma_start(
                    out=wgt_g[:, (bpar * 16 + dc) * 1024:(bpar * 16 + dc + 1) * 1024],
                    in_=gagT_d[r * 2048 + dc * 128:r * 2048 + (dc + 1) * 128,
                               hb * 1024:(hb + 1) * 1024]))
                tg2.emit(SY, lambda e, bpar=bpar, dc=dc, r=r, hb=hb: e.dma_start(
                    out=wgt_u[:, (bpar * 16 + dc) * 1024:(bpar * 16 + dc + 1) * 1024],
                    in_=gagT_d[(r + 4) * 2048 + dc * 128:(r + 4) * 2048 + (dc + 1) * 128,
                               hb * 1024:(hb + 1) * 1024]))

        w_fb3(SY)
        S["ccA"].w(SY, 1)
        emit_treads(0)
        emit_treads(1)

        def emit_r2mm(mj):
            mjg, mth = mj // 2, mj % 2
            S["ffq"].w(TE, mj + 1)
            _op(TE, lambda e, par=mj % 2, th=mth,
                st=(mjg == 0), sp=(mjg == 63): e.matmul(
                ps_r2[th][0:1, 0:512], onesc[:],
                h2stg[:, par * 512:(par + 1) * 512],
                start=st, stop=sp), S["ffo"], 1)

        nj = 0
        for bi in range(NBLK):
            r, hb = bi // 2, bi % 2
            bpar = bi % 2
            tg1.wait_upto(TE, (bi + 1) * 16)
            tg2.wait_upto(TE, (bi + 1) * 16)
            if bi == 0:
                S["xotv"].w(TE, XO_TV)
            for jj in range(8):
                jg = r * 16 + hb * 8 + jj
                for th in range(2):
                    par = nj % 2
                    if nj >= 2:
                        S["ffs"].w(TE, nj - 1)
                        S["ffh"].w(TE, nj - 1)
                    for dc in range(16):
                        _op(TE, lambda e, bpar=bpar, dc=dc, jj=jj, th=th, par=par: e.matmul(
                            ps_g[par][:],
                            wgt_g[:, (bpar * 16 + dc) * 1024 + jj * 128:
                                  (bpar * 16 + dc) * 1024 + (jj + 1) * 128],
                            xn_own[:, dc * 1024 + th * 512:dc * 1024 + (th + 1) * 512],
                            start=(dc == 0), stop=(dc == 15)),
                            S["ff1"] if dc == 15 else None, 1)
                    for dc in range(16):
                        _op(TE, lambda e, bpar=bpar, dc=dc, jj=jj, th=th, par=par: e.matmul(
                            ps_u[par][:],
                            wgt_u[:, (bpar * 16 + dc) * 1024 + jj * 128:
                                  (bpar * 16 + dc) * 1024 + (jj + 1) * 128],
                            xn_own[:, dc * 1024 + th * 512:dc * 1024 + (th + 1) * 512],
                            start=(dc == 0), stop=(dc == 15)),
                            S["ff1"] if dc == 15 else None, 1)
                    S["ff1"].w(SC, 2 * nj + 1)
                    if nj >= 2:
                        S["ffh"].w(SC, nj - 1)
                    _op(SC, lambda e, par=par: e.activation(
                        out=sgbuf[:, par * 512:(par + 1) * 512], in_=ps_g[par][:],
                        func=AF.Silu), S["ffs"], 1)
                    S["ffs"].w(VE, nj + 1)
                    S["ff1"].w(VE, 2 * nj + 2)
                    if nj >= 2:
                        S["ffq"].w(VE, nj - 1)
                        hst.wait_upto(VE, nj - 1)
                    _op(VE, lambda e, par=par: e.tensor_tensor(
                        out=hstg[:, par * 512:(par + 1) * 512], in0=ps_u[par][:],
                        in1=sgbuf[:, par * 512:(par + 1) * 512], op=ALU.mult),
                        S["ffh"], 1)
                    S["ffh"].w(SC, nj + 1)
                    if nj >= 2:
                        S["ffo"].w(SC, nj - 1)
                    _op(SC, lambda e, par=par: e.activation(
                        out=h2stg[:, par * 512:(par + 1) * 512],
                        in_=hstg[:, par * 512:(par + 1) * 512], func=AF.Square),
                        S["ffq"], 1)
                    if nj >= 1:
                        emit_r2mm(nj - 1)
                    S["ffh"].w(SC, nj + 1)
                    hst.emit(SC, lambda e, jg=jg, th=th, par=par: e.dma_start(
                        out=h_d[jg * 128:(jg + 1) * 128, th * 512:(th + 1) * 512],
                        in_=hstg[:, par * 512:(par + 1) * 512]))
                    nj += 1
            if bi + 2 < NBLK:
                emit_treads(bi + 2)
        emit_r2mm(127)

        S["ffo"].w(VE, 128)
        for th in range(2):
            _op(VE, lambda e, th=th: e.tensor_scalar(
                out=rstt[:, th * 512:(th + 1) * 512], in0=ps_r2[th][0:1, 0:512],
                scalar1=1.0 / HFF, scalar2=None, op0=ALU.mult), S["rsA"], 1)
        S["rsA"].w(SC, 2)
        _op(SC, lambda e: e.activation(out=rstt[:], in_=rstt[:], func=AF.Ln,
                                       bias=epsrms[0:1, 0:1]), S["rsB"], 1)
        S["rsB"].w(SC)
        _op(SC, lambda e: e.activation(out=rstt[:], in_=rstt[:], func=AF.Exp,
                                       scale=-0.5), S["rsB"], 1)
        S["rsB"].w(VE)
        _op(VE, lambda e: e.tensor_copy(out=rsttr[:], in_=rstt[:]), S["rsB"], 1)
        if "h" in dbg_t:
            hst.wait_all(SY)
            dbgs.emit(SY, lambda e: e.dma_start(out=dbg_t["h"], in_=h_d))

        if ph <= 4:
            S["rsB"].w(SY, 3)
            S["rs"].w(SY, 4)
            dbgs.wait_all(SY)
            _flush(blk)
            esF1p.close(); esF1.close(); esFFa.close()
            return nc

        esF1p.close()
        esF1.close()

        esF2 = contextlib.ExitStack()
        h_buf = esF2.enter_context(nc.sbuf_tensor("h_buf", [128, 64 * 1024], BF16))
        wpt = esF2.enter_context(nc.sbuf_tensor("wpt", [128, 6 * 512], BF16))
        fstg = esF2.enter_context(nc.sbuf_tensor("fstg", [128, 2 * 512], F32))
        outstg = esF2.enter_context(nc.sbuf_tensor("outstg", [128, 4 * 512], F32))
        axa = esF2.enter_context(nc.sbuf_tensor("axa", [128, 4 * 512], F32))
        axx = esF2.enter_context(nc.sbuf_tensor("axx", [128, 4 * 512], F32))
        esF2p = contextlib.ExitStack()
        ps_rb = [esF2p.enter_context(nc.psum_tensor(f"ps_rb{i}", [128, 512], F32))
                 for i in range(2)]
        ps_o = [esF2p.enter_context(nc.psum_tensor(f"ps_o{i}", [128, 512], F32))
                for i in range(4)]

        S["rsB"].w(TE, 3)
        for th in range(2):
            _op(TE, lambda e, th=th: e.matmul(
                ps_rb[th][:], onesr[:], rsttr[:, th * 512:(th + 1) * 512],
                start=True, stop=True), S["rsC"], 1)
        S["rsC"].w(SC, 2)
        for th in range(2):
            _op(SC, lambda e, th=th: e.copy(
                out=rstd_fm[:, th * 512:(th + 1) * 512], in_=ps_rb[th][:]), S["rsC"], 1)
        RSC_ALL = S["rsC"].v

        S["ff1"].w(SY, 256)
        S["ffo"].w(SY, 128)
        for j2 in range(64):
            hst.wait_upto(SY, 2 * (j2 + 1))
            hld.emit(SY, lambda e, j2=j2: e.dma_start(
                out=h_buf[:, j2 * 1024:(j2 + 1) * 1024],
                in_=h_d[j2 * 128:(j2 + 1) * 128, :]))
        S["ccB"].w(SY, 1)
        S["rs"].w(SY, 4)
        nsw_l = 0

        def emit_out_stores(si):
            th_, dqi_ = si // 4, si % 4
            for dc in range(4):
                fin.emit(SY, lambda e, th=th_, dqi=dqi_, dc=dc: e.dma_start(
                    out=out_ext[dqi * 512 + dc * 128:dqi * 512 + (dc + 1) * 128,
                                th * 512:(th + 1) * 512],
                    in_=outstg[:, dc * 512:(dc + 1) * 512]))

        for th in range(2):
            for dqi in range(4):
                si = th * 4 + dqi
                for j2 in range(64):
                    par = nsw_l % 6
                    if nsw_l >= 6:
                        S["f2f"].w(SY, nsw_l - 5)
                    tp1.emit(SY, lambda e, j2=j2, dqi=dqi, par=par: e.dma_start(
                        out=wpt[:, par * 512:(par + 1) * 512],
                        in_=pagT_d[j2 * 128:(j2 + 1) * 128, dqi * 512:(dqi + 1) * 512]))
                    nsw_l += 1
                if si >= 1:
                    S["f2a"].w(SY, si * 4)
                    emit_out_stores(si - 1)
                for dc in range(4):
                    axl.emit(SY, lambda e, th=th, dqi=dqi, dc=dc: e.dma_start(
                        out=axa[:, dc * 512:(dc + 1) * 512],
                        in_=attnr_d[dqi * 512 + dc * 128:dqi * 512 + (dc + 1) * 128,
                                    th * 512:(th + 1) * 512]))
        S["f2a"].w(SY, 32)
        emit_out_stores(7)

        nsw = 0
        nev = 0
        ntr = 0
        for th in range(2):
            for dqi in range(4):
                si = th * 4 + dqi
                if si == 0:
                    S["rsC"].w(TE, RSC_ALL)
                for j2 in range(64):
                    par = nsw % 6
                    tp1.wait_upto(TE, nsw + 1)
                    if si == 0:
                        hld.wait_upto(TE, j2 + 1)
                    for dc in range(4):
                        if si > 0 and j2 == 0:
                            S["f2m"].w(TE, (si - 1) * 4 + dc + 1)
                        _op(TE, lambda e, par=par, dc=dc, j2=j2, th=th: e.matmul(
                            ps_o[dc][:],
                            wpt[:, par * 512 + dc * 128:par * 512 + (dc + 1) * 128],
                            h_buf[:, j2 * 1024 + th * 512:j2 * 1024 + (th + 1) * 512],
                            start=(j2 == 0), stop=(j2 == 63)),
                            S["f2f"] if dc == 3 else None, 1)
                    nsw += 1
                for dc in range(4):
                    fpar = dc % 2
                    S["f2f"].w(VE, nsw)
                    if si >= 1 and dc == 0:
                        fin.wait_upto(VE, si * 4)
                    _op(VE, lambda e, dc=dc, th=th, fpar=fpar: e.tensor_tensor(
                        out=fstg[:, fpar * 512:(fpar + 1) * 512], in0=ps_o[dc][:],
                        in1=rstd_fm[:, th * 512:(th + 1) * 512], op=ALU.mult),
                        S["f2m"], 1)
                    S["f2m"].w(VE)
                    _op(VE, lambda e, dqi=dqi, dc=dc, fpar=fpar: e.tensor_scalar(
                        out=fstg[:, fpar * 512:(fpar + 1) * 512],
                        in0=fstg[:, fpar * 512:(fpar + 1) * 512],
                        scalar1=bo_t[:, dqi * 4 + dc:dqi * 4 + dc + 1], scalar2=None,
                        op0=ALU.add), S["f2e"], 1)
                    S["f2e"].w(VE)
                    axl.wait_upto(VE, (si + 1) * 4)
                    _op(VE, lambda e, dc=dc, fpar=fpar: e.tensor_tensor(
                        out=outstg[:, dc * 512:(dc + 1) * 512],
                        in0=fstg[:, fpar * 512:(fpar + 1) * 512],
                        in1=axa[:, dc * 512:(dc + 1) * 512],
                        op=ALU.add), S["f2a"], 1)
                    nev += 1
        fin.wait_all(SY)
        dbgs.wait_all(SY)

        esF2p.close()
        esF2.close()
        esFFa.close()

        _flush(blk)

    return nc


def kernel(L0_attn_hidden_states, L0_attn_Wq, L0_attn_Wk, L0_attn_Wv,
           L0_attn_Wo, L0_attn_bo, L0_ffn_w_gate_up, L0_ffn_w_proj):
    x = np.ascontiguousarray(np.asarray(L0_attn_hidden_states, dtype=np.float32))
    Wq = np.ascontiguousarray(np.asarray(L0_attn_Wq, dtype=np.float32))
    Wk = np.ascontiguousarray(np.asarray(L0_attn_Wk, dtype=np.float32))
    Wv = np.ascontiguousarray(np.asarray(L0_attn_Wv, dtype=np.float32))
    Wo = np.ascontiguousarray(np.asarray(L0_attn_Wo, dtype=np.float32))
    bo = np.ascontiguousarray(np.asarray(L0_attn_bo, dtype=np.float32))
    wgu = np.ascontiguousarray(np.asarray(L0_ffn_w_gate_up, dtype=np.float32))
    wp = np.ascontiguousarray(np.asarray(L0_ffn_w_proj, dtype=np.float32))

    nc = build_nc()
    in_maps = []
    bo_tile = np.ascontiguousarray(bo.reshape(16, 128).T)
    for c in range(8):
        b, g = c // 4, c % 4
        m = dict(_host_consts(g))
        m["x"] = x[b]
        m["xT"] = np.ascontiguousarray(x[b].T)
        m["x_own"] = np.ascontiguousarray(x[b, g * OWN:(g + 1) * OWN])
        m["wq"] = np.ascontiguousarray(Wq[:, g * 512:(g + 1) * 512])
        m["wk"] = np.ascontiguousarray(Wk[:, g * 128:(g + 1) * 128])
        m["wv"] = np.ascontiguousarray(Wv[:, g * 128:(g + 1) * 128])
        m["wo"] = np.ascontiguousarray(Wo[g * 512:(g + 1) * 512, :])
        m["bo_t"] = bo_tile
        m["wgu_sh"] = np.ascontiguousarray(wgu[c * 2048:(c + 1) * 2048, :])
        m["wp_sh"] = np.ascontiguousarray(wp[:, c * 1024:(c + 1) * 1024])
        in_maps.append(m)

    res = run_bass_kernel_spmd(nc, in_maps, core_ids=list(range(8)))
    out = np.empty((B, T, D), np.float32)
    for c in range(8):
        b, g = c // 4, c % 4
        out[b, g * OWN:(g + 1) * OWN] = res.results[c]["out"].T
    return out


if __name__ == "__main__":
    print("building program...")
    nc = build_nc()
    print("instructions:", nc.n_instructions())


# version 30
# speedup vs baseline: 1.0475x; 1.0145x over previous
import math
import contextlib
import numpy as np
import ml_dtypes
import concourse.bass as bass
import concourse.mybir as mybir
from concourse.bass_utils import run_bass_kernel_spmd

F32 = mybir.dt.float32
F32R = mybir.dt.float32r
BF16 = mybir.dt.bfloat16
AF = mybir.ActivationFunctionType
ALU = mybir.AluOpType
AX = mybir.AxisListType
BF = ml_dtypes.bfloat16

B, T, D = 2, 4096, 2048
H, DQ, DV = 16, 128, 128
HFF = 8192
NW = T // 512
OWN = 1024
CAP = 30.0
CSC = float(np.float32(1.0 / (math.sqrt(DQ) * CAP)))
NEG = -1.0e30
LN_EPS = 1e-5
RMS_EPS = float(np.finfo(np.float32).eps)
MAGIC = 12582912.0
QCLIP = 1.4375

COMBOS = [(qg, kc) for qg in range(8) for kc in range(4 * qg + 4)]
NCOMB = len(COMBOS)
CIDX = {c: i for i, c in enumerate(COMBOS)}


def _host_consts(g):
    o = {}
    invf = 1.0 / (10000.0 ** (np.arange(0, DQ, 2) / DQ))
    pos = np.arange(T, dtype=np.float64)
    fr2 = np.repeat(pos[None, :] * invf[:, None], 2, axis=0)
    o["cos_t"] = np.cos(fr2).astype(np.float32)
    o["sin_t"] = np.sin(fr2).astype(np.float32)
    P = np.zeros((DQ, DQ), np.float32)
    for i in range(DQ // 2):
        P[2 * i, 2 * i + 1] = -1.0
        P[2 * i + 1, 2 * i] = 1.0
    o["rotT"] = np.ascontiguousarray(P.T.astype(BF))
    o["ident_bf"] = np.eye(128, dtype=np.float32).astype(BF)
    o["ident_f32"] = np.eye(128, dtype=np.float32)
    m = np.zeros((128, 4 * 512), np.float64)
    pp = np.arange(128)[:, None]
    dqi = np.arange(512)[None, :]
    for r in range(4):
        m[:, r * 512:(r + 1) * 512] = np.where(pp + r * 128 > dqi, NEG, 0.0)
    o["masks"] = m.astype(np.float32)
    slopes = [2.0 ** (-(i + 1) * 8.0 / H) for i in range(H)]
    bias = np.zeros((128, 4 * NCOMB), np.float64)
    rhs = np.zeros((1, 4 * 512), np.float64)
    p64 = np.arange(128, dtype=np.float64)
    d64 = np.arange(512, dtype=np.float64)
    for hl in range(4):
        s = slopes[4 * g + hl]
        for ci, (qg, kc) in enumerate(COMBOS):
            bias[:, hl * NCOMB + ci] = CSC * s * (kc * 128 + p64 - qg * 512 - 256)
        rhs[0, hl * 512:(hl + 1) * 512] = -CSC * s * (d64 - 256)
    o["alibias"] = bias.astype(np.float32)
    o["aliq"] = rhs.astype(np.float32)
    o["ones_r"] = np.ones((1, 128), np.float32)
    o["eps_ln"] = np.full((128, 1), LN_EPS, np.float32)
    o["eps_rms"] = np.full((128, 1), RMS_EPS, np.float32)
    o["magicb"] = np.full((128, 1), MAGIC, np.float32)
    o["magicnb"] = np.full((128, 1), -MAGIC, np.float32)
    o["ones_col"] = np.ones((128, 1), np.float32).astype(BF)
    return o


class EQ:
    def __init__(self):
        self.items = []

    def add(self, fn):
        self.items.append(fn)


class Sem:
    def __init__(self, es, nc, name):
        self.h = es.enter_context(nc.semaphore(name))
        self.v = 0

    def w(self, q, val=None):
        tv = self.v if val is None else val
        q.add(lambda e, s=self.h, t=tv: e.wait_ge(s, t))
        return tv


class DS:

    def __init__(self, es, nc, name, k=2):
        self.sems = [Sem(es, nc, f"{name}{i}") for i in range(k)]
        self.k = k
        self.n = 0

    def emit(self, q, build):
        s = self.sems[self.n % self.k]
        if s.v > 0:
            q.add(lambda e, h=s.h, t=s.v: e.wait_ge(h, t))
        s.v += 16
        q.add(lambda e, b=build, h=s.h: b(e).then_inc(h, 16))
        self.n += 1

    def wait_upto(self, q, n):
        for j in range(self.k):
            need = (n - j + self.k - 1) // self.k
            if need > 0:
                q.add(lambda e, h=self.sems[j].h, t=16 * need: e.wait_ge(h, t))

    def wait_all(self, q):
        self.wait_upto(q, self.n)


def _op(q, build, sem=None, n=1):
    if sem is None:
        q.add(lambda e, b=build: b(e))
        return None
    sem.v += n
    q.add(lambda e, b=build, s=sem.h, k=n: b(e).then_inc(s, k))
    return sem.v


def _opm(q, build, sems):
    pairs = []
    for sem, n in sems:
        sem.v += n
        pairs.append((sem.h, n))

    def runner(e, b=build, pairs=tuple(pairs)):
        ins = b(e)
        for h, k in pairs:
            ins = ins.then_inc(h, k)
        return ins
    q.add(runner)


def build_nc(dbg=(), ph=9):
    dbg = set(dbg)
    nc = bass.Bass("TRN2", target_bir_lowering=False, debug=False)

    x_in = nc.dram_tensor("x", [T, D], F32, kind="ExternalInput").ap()
    xT_in = nc.dram_tensor("xT", [D, T], F32, kind="ExternalInput").ap()
    xown_in = nc.dram_tensor("x_own", [OWN, D], F32, kind="ExternalInput").ap()
    wq_in = nc.dram_tensor("wq", [D, 512], F32, kind="ExternalInput").ap()
    wk_in = nc.dram_tensor("wk", [D, DQ], F32, kind="ExternalInput").ap()
    wv_in = nc.dram_tensor("wv", [D, DV], F32, kind="ExternalInput").ap()
    wo_in = nc.dram_tensor("wo", [512, D], F32, kind="ExternalInput").ap()
    bo_in = nc.dram_tensor("bo_t", [128, 16], F32, kind="ExternalInput").ap()
    wgu_in = nc.dram_tensor("wgu_sh", [2048, D], F32, kind="ExternalInput").ap()
    wp_in = nc.dram_tensor("wp_sh", [D, 1024], F32, kind="ExternalInput").ap()
    cos_in = nc.dram_tensor("cos_t", [128, T], F32, kind="ExternalInput").ap()
    sin_in = nc.dram_tensor("sin_t", [128, T], F32, kind="ExternalInput").ap()
    rotT_in = nc.dram_tensor("rotT", [128, 128], BF16, kind="ExternalInput").ap()
    identb_in = nc.dram_tensor("ident_bf", [128, 128], BF16, kind="ExternalInput").ap()
    identf_in = nc.dram_tensor("ident_f32", [128, 128], F32, kind="ExternalInput").ap()
    masks_in = nc.dram_tensor("masks", [128, 4 * 512], F32, kind="ExternalInput").ap()
    alib_in = nc.dram_tensor("alibias", [128, 4 * NCOMB], F32, kind="ExternalInput").ap()
    aliq_in = nc.dram_tensor("aliq", [1, 4 * 512], F32R, kind="ExternalInput").ap()
    onesr_in = nc.dram_tensor("ones_r", [1, 128], F32R, kind="ExternalInput").ap()
    onesc_in = nc.dram_tensor("ones_col", [128, 1], BF16, kind="ExternalInput").ap()
    epsln_in = nc.dram_tensor("eps_ln", [128, 1], F32, kind="ExternalInput").ap()
    epsrms_in = nc.dram_tensor("eps_rms", [128, 1], F32, kind="ExternalInput").ap()
    magicb_in = nc.dram_tensor("magicb", [128, 1], F32, kind="ExternalInput").ap()
    magicnb_in = nc.dram_tensor("magicnb", [128, 1], F32, kind="ExternalInput").ap()

    out_ext = nc.dram_tensor("out", [D, OWN], F32, kind="ExternalOutput").ap()
    dbg_t = {}
    for name, shape, dt in dbg:
        dbg_t[name] = nc.dram_tensor("dbg_" + name, list(shape), dt,
                                     kind="ExternalOutput").ap()

    gshT_d = nc.dram_tensor("gshT_d", [2048, 2048], BF16).ap()
    gagT_d = nc.dram_tensor("gagT_d", [8 * 2048, 2048], BF16, addr_space="Shared").ap()
    pshT_d = nc.dram_tensor("pshT_d", [1024, 2048], BF16).ap()
    pagT_d = nc.dram_tensor("pagT_d", [8 * 1024, 2048], BF16, addr_space="Shared").ap()
    attnp_d = nc.dram_tensor("attnp_d", [4 * 4 * 512, 1024], F32).ap()
    attnr_d = nc.dram_tensor("attnr_d", [D, OWN], F32).ap()
    h_d = nc.dram_tensor("h_d", [HFF, OWN], BF16).ap()

    ENGS = ("sync", "vector", "scalar", "tensor", "gpsimd")
    q = {e: EQ() for e in ENGS}
    SY, VE, SC, TE, GP = (q[e] for e in ENGS)

    def _flush(blk):
        @blk.sync
        def _(e):
            for fn in SY.items:
                fn(e)

        @blk.vector
        def _(e):
            for fn in VE.items:
                fn(e)

        @blk.scalar
        def _(e):
            for fn in SC.items:
                fn(e)

        @blk.tensor
        def _(e):
            for fn in TE.items:
                fn(e)

        @blk.gpsimd
        def _(e):
            for fn in GP.items:
                fn(e)

    es = contextlib.ExitStack()
    with es:
        S = {}
        for n_ in ["app", "trp", "tev",
                   "mmq", "qev", "rot", "rope", "vmm", "vev",
                   "lnA", "lnB", "lnC", "lnD", "lnE",
                   "qd1", "qdg", "qrc", "qtt", "qte", "qmg", "ccA", "ccB",
                   "ms", "mt", "mm_", "mp", "pvd", "mce", "mct", "mcf",
                   "wo1", "wo2", "rs",
                   "xoA", "xoB", "xoC", "xoD", "xoE", "xoap", "xotr", "xotv",
                   "ff1", "ffs", "ffh", "ffq", "ffo",
                   "rsA", "rsB", "rsC",
                   "f2e", "f2f", "f2t", "f2a", "f2x", "f2m"]:
            S[n_] = Sem(es, nc, n_)
        ldc = DS(es, nc, "ldc", 4)
        ldc2 = DS(es, nc, "ldc2", 3)
        ldw = DS(es, nc, "ldw", 2)
        ldx = DS(es, nc, "ldx", 2)
        ldcs = DS(es, nc, "ldcs", 2)
        qg1 = DS(es, nc, "qg1", 2)
        qg2 = DS(es, nc, "qg2", 2)
        sast = DS(es, nc, "sast", 4)
        xol = DS(es, nc, "xol", 4)
        tg1 = DS(es, nc, "tg1", 2)
        tg2 = DS(es, nc, "tg2", 2)
        hst = DS(es, nc, "hst", 2)
        hld = DS(es, nc, "hld", 4)
        tp1 = DS(es, nc, "tp1", 4)
        axl = DS(es, nc, "axl", 2)
        fin = DS(es, nc, "fin", 2)
        dbgs = DS(es, nc, "dbgs", 1)

        blk = es.enter_context(nc.Block())

        rotT = es.enter_context(nc.sbuf_tensor("rotT_s", [128, 128], BF16))
        identb = es.enter_context(nc.sbuf_tensor("identb_s", [128, 128], BF16))
        identf = es.enter_context(nc.sbuf_tensor("identf_s", [128, 128], F32))
        onesr = es.enter_context(nc.sbuf_tensor("onesr_s", [1, 128], F32R))
        onesc = es.enter_context(nc.sbuf_tensor("onesc_s", [128, 1], BF16))
        bo_t = es.enter_context(nc.sbuf_tensor("bo_t_s", [128, 16], F32))
        epsln = es.enter_context(nc.sbuf_tensor("epsln_s", [128, 1], F32))
        epsrms = es.enter_context(nc.sbuf_tensor("epsrms_s", [128, 1], F32))
        magicb = es.enter_context(nc.sbuf_tensor("magicb_s", [128, 1], F32))
        magicnb = es.enter_context(nc.sbuf_tensor("magicnb_s", [128, 1], F32))
        esQW = contextlib.ExitStack()
        qw = esQW.enter_context(nc.sbuf_tensor("qw", [128, 4 * D], BF16, side="right"))
        qd = esQW.enter_context(nc.sbuf_tensor("qd", [128, 3 * D], BF16, side="right"))
        qsum = esQW.enter_context(nc.sbuf_tensor("qsum", [128, 32], F32, side="right"))
        qs_bf = esQW.enter_context(nc.sbuf_tensor("qs_bf", [128, 8 * 32], BF16, side="right"))
        qrcp = esQW.enter_context(nc.sbuf_tensor("qrcp", [128, 8 * 32], F32, side="right"))
        qdT = esQW.enter_context(nc.sbuf_tensor("qdT", [128, 2048], BF16, side="right"))
        qmagf = esQW.enter_context(nc.sbuf_tensor("qmagf", [128, 2048], F32, side="right"))

        for dst, srct in [(rotT, rotT_in), (identb, identb_in), (identf, identf_in),
                          (onesr, onesr_in), (onesc, onesc_in), (bo_t, bo_in),
                          (epsln, epsln_in), (epsrms, epsrms_in),
                          (magicb, magicb_in), (magicnb, magicnb_in)]:
            ldc.emit(SY, lambda e, d=dst, s=srct: e.dma_start(out=d[:], in_=s))

        esAtt = contextlib.ExitStack()
        q_fm = esAtt.enter_context(nc.sbuf_tensor("q_fm", [128, 4 * T], BF16))
        k_fm = esAtt.enter_context(nc.sbuf_tensor("k_fm", [128, T], BF16))
        v_tm = esAtt.enter_context(nc.sbuf_tensor("v_tm", [128, 32 * (DV + 1)], BF16))

        esP1 = contextlib.ExitStack()
        xwin = esP1.enter_context(nc.sbuf_tensor("xwin", [128, 4 * D], F32))
        xn_tm = esP1.enter_context(nc.sbuf_tensor("xn_tm", [128, 2 * D], BF16))
        xnw = esP1.enter_context(nc.sbuf_tensor("xnw", [128, 2 * 16 * 512], BF16))
        wq_s = esP1.enter_context(nc.sbuf_tensor("wq_s", [128, 16 * 512], BF16))
        wk_s = esP1.enter_context(nc.sbuf_tensor("wk_s", [128, 16 * 128], BF16))
        wv_s = esP1.enter_context(nc.sbuf_tensor("wv_s", [128, 16 * 128], BF16))
        cosw = esP1.enter_context(nc.sbuf_tensor("cosw", [128, 2 * 512], F32))
        sinw = esP1.enter_context(nc.sbuf_tensor("sinw", [128, 2 * 512], F32))
        lnsum = esP1.enter_context(nc.sbuf_tensor("lnsum", [128, 4], F32))
        lnsq = esP1.enter_context(nc.sbuf_tensor("lnsq", [128, 4], F32))
        lnmean = esP1.enter_context(nc.sbuf_tensor("lnmean", [128, 4], F32))
        lnm2 = esP1.enter_context(nc.sbuf_tensor("lnm2", [128, 4], F32))
        lnvar = esP1.enter_context(nc.sbuf_tensor("lnvar", [128, 4], F32))
        lnrs = esP1.enter_context(nc.sbuf_tensor("lnrs", [128, 4], F32))
        lnnmr = esP1.enter_context(nc.sbuf_tensor("lnnmr", [128, 4], F32))
        lnjunk = esP1.enter_context(nc.sbuf_tensor("lnjunk", [128, D], BF16))
        qst = esP1.enter_context(nc.sbuf_tensor("qst", [128, 512], BF16))
        rt1 = esP1.enter_context(nc.sbuf_tensor("rt1", [128, 512], F32))
        rt2 = esP1.enter_context(nc.sbuf_tensor("rt2", [128, 512], F32))

        esP1p = contextlib.ExitStack()
        ps_tr = [esP1p.enter_context(nc.psum_tensor(f"ps_tr{i}", [128, 1024], BF16))
                 for i in range(2)]
        ps_qkv = esP1p.enter_context(nc.psum_tensor("ps_qkv", [128, 512], F32))
        ps_rot = esP1p.enter_context(nc.psum_tensor("ps_rot", [128, 512], F32))
        ps_v = [esP1p.enter_context(nc.psum_tensor(f"ps_v{i}", [128, 512], F32))
                for i in range(2)]
        ps_qt = [esP1p.enter_context(nc.psum_tensor(f"ps_qt{i}", [128, 1024], BF16))
                 for i in range(2)]

        ldw.emit(GP, lambda e: e.dma_start(
            out=wq_s.ap().rearrange("p (c m) -> p c m", c=16),
            in_=wq_in.rearrange("(c p) m -> p c m", p=128)))
        ldw.emit(GP, lambda e: e.dma_start(
            out=wk_s.ap().rearrange("p (c m) -> p c m", c=16),
            in_=wk_in.rearrange("(c p) m -> p c m", p=128)))
        ldw.emit(GP, lambda e: e.dma_start(
            out=wv_s.ap().rearrange("p (c m) -> p c m", c=16),
            in_=wv_in.rearrange("(c p) m -> p c m", p=128)))
        def FC(t):
            return 2 * (t + 1) if t < 16 else t + 17

        nqf = [0]

        def emit_qtrans(qt):
            nblk = 16 if qt < 16 else 8
            dp = qt % 3
            S["qdg"].w(TE, QDG_MUL(qt))
            for fi in range(nblk // 8):
                bank = nqf[0] % 2
                if nqf[0] >= 2:
                    S["qte"].w(TE, nqf[0] - 1)
                for j in range(8):
                    _op(TE, lambda e, fi=fi, j=j, bank=bank, dp=dp: e.transpose(
                        ps_qt[bank][:, j * 128:(j + 1) * 128],
                        qd[:, dp * D + (fi * 8 + j) * 128:dp * D + (fi * 8 + j + 1) * 128],
                        identb[:]), S["qtt"] if j == 7 else None, 1)
                nqf[0] += 1
            for fi in range(nblk // 8):
                nf = nqf[0] - nblk // 8 + fi
                bank = nf % 2
                S["qtt"].w(SC, nf + 1)
                if fi == 0 and qt >= 1:
                    qg2.wait_upto(SC, qt)
                _op(SC, lambda e, fi=fi, bank=bank: e.copy(
                    out=qdT[:, fi * 1024:(fi + 1) * 1024], in_=ps_qt[bank][:]),
                    S["qte"], 1)
            if qt < 16:
                dstp = gshT_d.rearrange("(j p) c -> p j c", p=128)[
                    :, :, qt * 128:(qt + 1) * 128]
                srcp = qdT.ap().rearrange("p (j c) -> p j c", c=128)
            else:
                dstp = pshT_d.rearrange("(j p) c -> p j c", p=128)[
                    :, :, (qt - 16) * 128:(qt - 16 + 1) * 128]
                srcp = qdT.ap()[:, 0:1024].rearrange("p (j c) -> p j c", c=128)
            S["qte"].w(SC, nqf[0])
            qg2.emit(SC, lambda e, dstp=dstp, srcp=srcp: e.dma_start(
                out=dstp, in_=srcp))

        def qsrc(qt):
            return (wgu_in[qt * 128:(qt + 1) * 128, :] if qt < 16
                    else wp_in[(qt - 16) * 128:(qt - 16 + 1) * 128, :])

        def emit_qred(qt):
            wid = 2048 if qt < 16 else 1024
            qp, s8 = qt % 4, qt % 8
            qg1.wait_upto(VE, qt + 1)
            _op(VE, lambda e, wid=wid, qp=qp: e.tensor_reduce(
                out=qsum[:, 0:wid // 64], in_=qw.ap()[:, qp * D:qp * D + wid].rearrange(
                    "p (g e) -> p g e", e=64),
                axis=AX.X, op=ALU.add, apply_absolute_value=True), S["qd1"], 1)
            S["qd1"].w(VE)
            if qt >= 8:
                S["qdg"].w(VE, QDG_MUL(qt - 8))
            _op(VE, lambda e, ng=wid // 64, s8=s8: e.tensor_scalar(
                out=qs_bf[:, s8 * 32:s8 * 32 + ng], in0=qsum[:, 0:ng], scalar1=1.0 / 64,
                scalar2=1e-8, op0=ALU.mult, op1=ALU.max), S["qd1"], 1)
            S["qd1"].w(VE)
            _op(VE, lambda e, ng=wid // 64, s8=s8: e.reciprocal(
                out=qrcp[:, s8 * 32:s8 * 32 + ng], in_=qs_bf[:, s8 * 32:s8 * 32 + ng]),
                S["qrc"], 1)

        def emit_qmul(mq):
            wid = 2048 if mq < 16 else 1024
            ng = wid // 64
            s8, dp = mq % 8, mq % 3
            S["qmg"].w(GP, 2 * mq + 2)
            _op(GP, lambda e, wid=wid, ng=ng, s8=s8, dp=dp: e.tensor_tensor(
                out=qd[:, dp * D:dp * D + wid].rearrange("p (g e) -> p g e", e=64),
                in0=qd[:, dp * D:dp * D + wid].rearrange("p (g e) -> p g e", e=64),
                in1=qs_bf[:, s8 * 32:s8 * 32 + ng].unsqueeze(2).broadcast_to([128, ng, 64]),
                op=ALU.mult), S["qdg"], 1)
            S["qdg"].w(GP)

        def QDG_CLIP(t):
            if t == 0:
                return 2
            return 50 if t == 16 else 3 * t + 1

        def QDG_MUL(t):
            if t == 15:
                return 48
            return 96 if t == 31 else 3 * t + 5

        def emit_qmagic(qt):
            wid = 2048 if qt < 16 else 1024
            dp = qt % 3
            S["qdg"].w(SC, QDG_CLIP(qt))
            _op(SC, lambda e, wid=wid, dp=dp: e.activation(
                out=qmagf[:, 0:wid], in_=qd[:, dp * D:dp * D + wid],
                func=AF.Identity, scale=1.0, bias=magicb[:]), S["qmg"], 1)
            S["qmg"].w(SC)
            _op(SC, lambda e, wid=wid, dp=dp: e.activation(
                out=qd[:, dp * D:dp * D + wid], in_=qmagf[:, 0:wid],
                func=AF.Identity, scale=1.0, bias=magicnb[:]), S["qmg"], 1)

        for qt in range(4):
            qg1.emit(GP, lambda e, s=qsrc(qt), qp=qt % 4: e.dma_start(
                out=qw[:, qp * D:qp * D + 2048], in_=s))
        for qt in range(32):
            wid = 2048 if qt < 16 else 1024
            ng = wid // 64
            qp, s8, dp = qt % 4, qt % 8, qt % 3
            qg1.wait_upto(GP, qt + 1)
            S["qrc"].w(GP, qt + 1)
            if qt >= 3:
                S["qtt"].w(GP, FC(qt - 3))
            _op(GP, lambda e, wid=wid, ng=ng, qp=qp, s8=s8, dp=dp: e.tensor_tensor(
                out=qd[:, dp * D:dp * D + wid].rearrange("p (g e) -> p g e", e=64),
                in0=qw.ap()[:, qp * D:qp * D + wid].rearrange("p (g e) -> p g e", e=64),
                in1=qrcp[:, s8 * 32:s8 * 32 + ng].unsqueeze(2).broadcast_to([128, ng, 64]),
                op=ALU.mult), S["qdg"], 1)
            S["qdg"].w(GP)
            if qt + 4 < 32:
                S["qd1"].w(GP, 2 * qt + 1)
                qg1.emit(GP, lambda e, s=qsrc(qt + 4), qp=qp, w2=(2048 if qt + 4 < 16 else 1024): e.dma_start(
                    out=qw[:, qp * D:qp * D + w2], in_=s))
            _op(GP, lambda e, wid=wid, dp=dp: e.tensor_scalar(
                out=qd[:, dp * D:dp * D + wid], in0=qd[:, dp * D:dp * D + wid],
                scalar1=QCLIP, scalar2=-QCLIP,
                op0=ALU.min, op1=ALU.max), S["qdg"], 1)
            S["qdg"].w(GP)
            if qt >= 1 and qt != 16:
                emit_qmul(qt - 1)
            if qt == 15:
                emit_qmul(15)
            if qt == 31:
                emit_qmul(31)
            if qt == 15:
                qg2.wait_upto(GP, 16)
                _op(GP, lambda e: e.collective_compute(
                    "AllGather", ALU.bypass, replica_groups=[list(range(8))],
                    ins=[gshT_d.opt()], outs=[gagT_d.opt()]), S["ccA"], 1)
        S["qd1"].w(GP, 64)
        ldw.emit(GP, lambda e: e.dma_start(
            out=wo_s.ap().rearrange("p (c m) -> p c m", c=4),
            in_=wo_in.rearrange("(c p) m -> p c m", p=128)))

        ldc.wait_all(VE)
        _op(VE, lambda e: e.memset(v_tm[:], 1.0), S["vev"], 1)

        for w in range(NW):
            wpar = w % 2
            if w >= 1:
                S["app"].w(SY, w * 4)
            for i in range(4):
                t = 4 * w + i
                ldx.emit(SY, lambda e, i=i, t=t: e.dma_start(
                    out=xwin[:, i * D:(i + 1) * D], in_=x_in[t * 128:(t + 1) * 128, :]))
            if w >= 2:
                S["rope"].w(SY, 15 * (w - 1))
            for srct, dstb in ((cos_in, cosw), (sin_in, sinw)):
                ldcs.emit(SY, lambda e, s=srct, d=dstb, w=w, wpar=wpar: e.dma_start(
                    out=d[:, wpar * 512:(wpar + 1) * 512],
                    in_=s[:, w * 512:(w + 1) * 512]))

            ldx.wait_upto(VE, 4 * (w + 1))
            if w >= 1:
                S["lnE"].w(VE, w)
            for i in range(4):
                _op(VE, lambda e, i=i: e.tensor_reduce(
                    out=lnsum[:, i:i + 1], in_=xwin[:, i * D:(i + 1) * D],
                    axis=AX.X, op=ALU.add), S["lnA"] if i == 3 else None, 1)
            S["lnA"].w(VE)
            _op(VE, lambda e: e.tensor_scalar(out=lnmean[:], in0=lnsum[:],
                scalar1=1.0 / D, scalar2=None, op0=ALU.mult), S["lnA"], 1)
            S["lnA"].w(VE)
            _op(VE, lambda e: e.tensor_tensor(out=lnm2[:], in0=lnmean[:],
                in1=lnmean[:], op=ALU.mult), S["lnA"], 1)
            ldx.wait_upto(SC, 4 * (w + 1))
            if w > 0:
                S["lnB"].w(SC)
            for i in range(4):
                if i > 0:
                    S["lnB"].w(SC)
                _op(SC, lambda e, i=i: e.activation(
                    out=lnjunk[:], in_=xwin[:, i * D:(i + 1) * D], func=AF.Square,
                    accum_out=lnsq[:, i:i + 1]), S["lnB"], 1)
            S["lnB"].w(VE)
            S["lnA"].w(VE)
            _op(VE, lambda e: e.scalar_tensor_tensor(
                out=lnvar[:], in0=lnsq[:], scalar=1.0 / D, in1=lnm2[:],
                op0=ALU.mult, op1=ALU.subtract), S["lnC"], 1)
            S["lnC"].w(SC)
            _op(SC, lambda e: e.activation(out=lnvar[:], in_=lnvar[:], func=AF.Ln,
                                           bias=epsln[:]), S["lnD"], 1)
            S["lnD"].w(SC)
            _op(SC, lambda e: e.activation(out=lnrs[:], in_=lnvar[:], func=AF.Exp,
                                           scale=-0.5), S["lnD"], 1)
            S["lnD"].w(VE)
            _op(VE, lambda e: e.scalar_tensor_tensor(
                out=lnnmr[:], in0=lnmean[:], scalar=-1.0, in1=lnrs[:],
                op0=ALU.mult, op1=ALU.mult), S["lnE"], 1)
            S["lnE"].w(SC)
            for i in range(4):
                ti = 4 * w + i
                if ti >= 2:
                    S["trp"].w(SC, (ti - 1) * 4)
                _opm(SC, lambda e, i=i: e.activation(
                    out=xn_tm[:, (i % 2) * D:(i % 2 + 1) * D],
                    in_=xwin[:, i * D:(i + 1) * D], func=AF.Identity,
                    scale=lnrs[:, i:i + 1], bias=lnnmr[:, i:i + 1]),
                    [(S["app"], 1)])

            if w == 0:
                ldc.wait_all(TE)
                ldw.wait_upto(TE, 3)
            for i in range(4):
                ti = 4 * w + i
                S["app"].w(TE, ti + 1)
                for grp in range(4):
                    gi = ti * 4 + grp
                    gpar = gi % 2
                    if gi >= 2:
                        S["tev"].w(TE, gi - 1)
                    for cc4 in range(4):
                        dc = grp * 4 + cc4
                        _op(TE, lambda e, i=i, dc=dc, gpar=gpar, cc4=cc4: e.transpose(
                            ps_tr[gpar][:, cc4 * 128:(cc4 + 1) * 128],
                            xn_tm[:, (i % 2) * D + dc * 128:(i % 2) * D + (dc + 1) * 128],
                            identb[:]),
                            S["trp"] if cc4 == 3 else None, 1)
            for i in range(4):
                for grp in range(4):
                    gi = (4 * w + i) * 4 + grp
                    gpar = gi % 2
                    S["trp"].w(VE, gi + 1)
                    _op(VE, lambda e, wpar=wpar, i=i, grp=grp, gpar=gpar: e.tensor_copy(
                        out=xnw.ap().rearrange("p (wn c m) -> p wn c m", wn=2, c=16)[
                            :, wpar, grp * 4:(grp + 1) * 4, i * 128:(i + 1) * 128],
                        in_=ps_tr[gpar][:, 0:512].rearrange("p (c m) -> p c m", c=4)),
                        S["tev"], 1)

            def xnw_sl(dc, wpar):
                base = wpar * 16 * 512 + dc * 512
                return xnw[:, base:base + 512]

            S["tev"].w(TE, (4 * w + 4) * 4)
            for ch in range(5):
                cidx = 5 * w + ch
                if cidx > 0:
                    S["qev"].w(TE, cidx)
                for kc in range(16):
                    if ch < 4:
                        lhs = wq_s[:, kc * 512 + ch * 128: kc * 512 + (ch + 1) * 128]
                    else:
                        lhs = wk_s[:, kc * 128:(kc + 1) * 128]
                    _op(TE, lambda e, lhs=lhs, kc=kc, wpar=wpar: e.matmul(
                        ps_qkv[:], lhs, xnw_sl(kc, wpar), start=(kc == 0), stop=(kc == 15)),
                        S["mmq"] if kc == 15 else None, 1)
                S["mmq"].w(SC, cidx + 1)
                if cidx > 0:
                    S["rope"].w(SC, 3 * cidx - 2)
                _op(SC, lambda e: e.copy(out=qst[:], in_=ps_qkv[:]), S["qev"], 1)
                S["qev"].w(TE, cidx + 1)
                if cidx > 0:
                    S["rope"].w(TE, 3 * cidx - 1)
                _op(TE, lambda e: e.matmul(ps_rot[:], rotT[:], qst[:],
                                           start=True, stop=True), S["rot"], 1)
                if ch == 0:
                    ldcs.wait_upto(VE, 2 * (w + 1))
                if cidx > 0:
                    S["rope"].w(VE, 3 * cidx)
                S["qev"].w(VE, cidx + 1)
                _op(VE, lambda e, wpar=wpar: e.tensor_tensor(
                    out=rt1[:], in0=qst[:], in1=cosw[:, wpar * 512:(wpar + 1) * 512],
                    op=ALU.mult), S["rope"], 1)
                S["rot"].w(VE, cidx + 1)
                _op(VE, lambda e, wpar=wpar: e.tensor_tensor(
                    out=rt2[:], in0=ps_rot[:], in1=sinw[:, wpar * 512:(wpar + 1) * 512],
                    op=ALU.mult), S["rope"], 1)
                S["rope"].w(VE)
                dst = (q_fm[:, ch * T + w * 512: ch * T + (w + 1) * 512] if ch < 4
                       else k_fm[:, w * 512:(w + 1) * 512])
                _op(VE, lambda e, dst=dst: e.tensor_tensor(
                    out=dst, in0=rt1[:], in1=rt2[:], op=ALU.add), S["rope"], 1)

            if w >= 1:
                for qq in range(4):
                    emit_qred(4 * (w - 1) + qq)

            for sub in range(4):
                vi = 4 * w + sub
                if vi >= 2:
                    S["vev"].w(TE, vi)
                for kc in range(16):
                    _op(TE, lambda e, sub=sub, kc=kc, wpar=wpar, vp=vi % 2: e.matmul(
                        ps_v[vp][:, 0:128], xnw_sl(kc, wpar)[:, sub * 128:(sub + 1) * 128],
                        wv_s[:, kc * 128:(kc + 1) * 128],
                        start=(kc == 0), stop=(kc == 15)),
                        S["vmm"] if kc == 15 else None, 1)
                S["vmm"].w(SC, vi + 1)
                if vi == 0:
                    S["vev"].w(SC, 1)
                _op(SC, lambda e, vi=vi, vp=vi % 2: e.copy(
                    out=v_tm[:, vi * (DV + 1):vi * (DV + 1) + DV], in_=ps_v[vp][:, 0:128]),
                    S["vev"], 1)

            if w >= 2:
                for qtr in range(4 * (w - 2), 4 * (w - 1)):
                    emit_qmagic(qtr)
                    emit_qtrans(qtr)
        for qt in range(28, 32):
            emit_qred(qt)
        for qtr in range(24, 32):
            emit_qmagic(qtr)
            emit_qtrans(qtr)

        def w_fb1(q):
            S["rope"].w(q, 120)
            S["vmm"].w(q, 32)
            S["vev"].w(q, 33)

        for nm, buf in [("q_fm", q_fm), ("k_fm", k_fm), ("v_tm", v_tm)]:
            if nm in dbg_t:
                w_fb1(SY)
                dbgs.emit(SY, lambda e, nm=nm, buf=buf: e.dma_start(
                    out=dbg_t[nm], in_=buf[:]))

        esP1p.close()
        esP1.close()

        if ph <= 1:
            dbgs.wait_all(SY)
            _flush(blk)
            esAtt.close()
            return nc

        esQW.close()
        esP2 = contextlib.ExitStack()
        tbuf = esP2.enter_context(nc.sbuf_tensor("tbuf", [128, 4 * 512], F32))
        pbuf = esP2.enter_context(nc.sbuf_tensor("pbuf", [128, 4 * 512], BF16))
        ctmp = esP2.enter_context(nc.sbuf_tensor("ctmp", [128, 4 * 128], BF16))
        crcp = esP2.enter_context(nc.sbuf_tensor("crcp", [128, 4], F32))
        ast = esP2.enter_context(nc.sbuf_tensor("ast", [128, 4 * 512], F32))
        xox = esP2.enter_context(nc.sbuf_tensor("xox", [128, 4 * 512], F32))
        ctx_fmT = esP2.enter_context(nc.sbuf_tensor("ctx_fmT", [128, 4 * T], BF16))
        masks = esP2.enter_context(nc.sbuf_tensor("masks_s", [128, 4 * 512], F32))
        alibias = esP2.enter_context(nc.sbuf_tensor("alibias_s", [128, 4 * NCOMB], F32))
        aliq = esP2.enter_context(nc.sbuf_tensor("aliq_s", [1, 4 * 512], F32R))
        qcomb = esP2.enter_context(nc.sbuf_tensor("qcomb", [128, 20 * 512], F32))
        t2buf = esP2.enter_context(nc.sbuf_tensor("t2buf", [128, 2 * 512], F32))
        wo_s = esP2.enter_context(nc.sbuf_tensor("wo_s", [128, 4 * D], BF16, side="right"))

        w_fb1(SY)
        for dst, srct in [(masks, masks_in), (alibias, alib_in), (aliq, aliq_in)]:
            ldc2.emit(SY, lambda e, d=dst, s=srct: e.dma_start(out=d[:], in_=s))

        esP2p = contextlib.ExitStack()
        ps_s = [esP2p.enter_context(nc.psum_tensor(f"ps_s{i}", [128, 512], F32))
                for i in range(3)]
        ps_c = [[esP2p.enter_context(nc.psum_tensor(f"ps_c{i}{j}", [128, 512], F32))
                 for j in range(2)] for i in range(2)]
        ps_ct = esP2p.enter_context(nc.psum_tensor("ps_ct", [128, 1024], BF16))

        ITERS = [(h, qg, kc) for h in range(4) for qg, kc in COMBOS]
        NIT = len(ITERS)
        GEND = {}
        for i_, (h_, qg_, kc_) in enumerate(ITERS):
            GEND[h_ * 8 + qg_] = i_

        def emit_pv(pidx):
            ph, pqg, pkc = ITERS[pidx]
            ppar = pidx % 4
            gi = ph * 8 + pqg
            gpi = gi % 2
            S["mp"].w(TE, pidx + 1)
            if pkc == 0 and gi >= 2:
                S["mcf"].w(TE, gi - 1)
            last = pkc == 4 * pqg + 3
            for sub in range(4):
                _op(TE, lambda e, ppar=ppar, sub=sub, pkc=pkc, gpi=gpi,
                    st=(pkc == 0 and sub % 2 == 0),
                    sp=(last and sub % 2 == 1): e.matmul(
                    ps_c[gpi][sub // 2][:, (sub % 2) * (DV + 1):
                                        (sub % 2) * (DV + 1) + DV + 1],
                    pbuf[:, ppar * 512 + sub * 128: ppar * 512 + (sub + 1) * 128],
                    v_tm[:, pkc * (DV + 1):(pkc + 1) * (DV + 1)],
                    start=st, stop=sp),
                    S["pvd"] if sub == 3 else None, 1)

        def emit_group_finalize(gi):
            h_, qg_ = gi // 8, gi % 8
            gpi = gi % 2
            S["pvd"].w(VE, GEND[gi] + 1)
            S["mce"].w(VE)
            for sub in range(4):
                _op(VE, lambda e, gpi=gpi, sub=sub: e.reciprocal(
                    out=crcp[:, sub:sub + 1],
                    in_=ps_c[gpi][sub // 2][:, (sub % 2) * (DV + 1) + DV:
                                            (sub % 2) * (DV + 1) + DV + 1]),
                    S["mce"] if sub == 3 else None, 1)
            S["mce"].w(VE)
            if gi >= 1:
                S["mct"].w(VE, 4 * gi)
            for sub in range(4):
                _op(VE, lambda e, gpi=gpi, sub=sub: e.tensor_scalar(
                    out=ctmp[:, sub * 128:(sub + 1) * 128],
                    in0=ps_c[gpi][sub // 2][:, (sub % 2) * (DV + 1):
                                            (sub % 2) * (DV + 1) + DV],
                    scalar1=crcp[:, sub:sub + 1], scalar2=None, op0=ALU.mult),
                    S["mce"], 1)
            S["mce"].w(TE)
            if gi >= 1:
                S["mcf"].w(TE, gi)
            for sub in range(4):
                _op(TE, lambda e, sub=sub: e.transpose(
                    ps_ct[:, sub * 128:(sub + 1) * 128],
                    ctmp[:, sub * 128:(sub + 1) * 128], identb[:]), S["mct"], 1)
            S["mct"].w(VE, 4 * (gi + 1))
            _op(VE, lambda e, h_=h_, qg_=qg_: e.tensor_copy(
                out=ctx_fmT[:, h_ * T + qg_ * 512: h_ * T + (qg_ + 1) * 512],
                in_=ps_ct[:, 0:512]), S["mcf"], 1)

        S["rope"].w(TE, 120)
        S["vev"].w(TE, 33)
        ldc2.wait_upto(TE, 3)
        ldc2.wait_upto(SC, 3)
        ldc2.wait_upto(VE, 3)
        for h in range(4):
            _op(TE, lambda e, h=h: e.matmul(
                ps_s[0][:], onesr[:], aliq[:, h * 512:(h + 1) * 512],
                start=True, stop=True), S["ms"], 1)
            S["ms"].w(SC, h + 1)
            _op(SC, lambda e, h=h: e.copy(
                out=qcomb[:, (h * 5 + 4) * 512:(h * 5 + 5) * 512], in_=ps_s[0][:]),
                S["mt"], 1)
            S["mt"].w(TE, h + 1)
        S["mt"].w(VE, 4)
        for h in range(4):
            for r in range(4):
                _op(VE, lambda e, h=h, r=r: e.tensor_tensor(
                    out=qcomb[:, (h * 5 + r) * 512:(h * 5 + r + 1) * 512],
                    in0=qcomb[:, (h * 5 + 4) * 512:(h * 5 + 5) * 512],
                    in1=masks[:, r * 512:(r + 1) * 512], op=ALU.add),
                    S["mm_"] if (h == 3 and r == 3) else None, 1)
        S["mm_"].w(VE, 1)
        MS0, MT0 = S["ms"].v, S["mt"].v
        S["mt"].w(TE, MT0)
        for idx, (h, qg, kc) in enumerate(ITERS):
            par = idx % 3
            pt = idx % 4
            t2 = idx % 2
            pp = idx % 4
            ci = CIDX[(qg, kc)]
            gi = h * 8 + qg
            if idx >= 3:
                S["mm_"].w(TE, idx - 1)
            _op(TE, lambda e, kc=kc, h=h, qg=qg, par=par: e.matmul(
                ps_s[par][:], k_fm[:, kc * 128:(kc + 1) * 128],
                q_fm[:, h * T + qg * 512:h * T + (qg + 1) * 512],
                start=True, stop=True), S["ms"], 1)
            if idx >= 3:
                emit_pv(idx - 3)
            if kc == 3 and gi >= 1:
                emit_group_finalize(gi - 1)
            sel = (kc - 4 * qg) if kc >= 4 * qg else 4
            S["ms"].w(VE, MS0 + idx + 1)
            if idx >= 4:
                S["mt"].w(VE, MT0 + idx - 3)
            _op(VE, lambda e, par=par, pt=pt, h=h, sel=sel: e.scalar_tensor_tensor(
                out=tbuf[:, pt * 512:(pt + 1) * 512], in0=ps_s[par][:],
                scalar=CSC, in1=qcomb[:, (h * 5 + sel) * 512:(h * 5 + sel + 1) * 512],
                op0=ALU.mult, op1=ALU.add), S["mm_"], 1)
            S["mm_"].w(SC, idx + 2)
            _op(SC, lambda e, pt=pt, t2=t2, h=h, ci=ci: e.activation(
                out=t2buf[:, t2 * 512:(t2 + 1) * 512],
                in_=tbuf[:, pt * 512:(pt + 1) * 512],
                func=AF.Tanh, scale=1.0,
                bias=alibias[:, h * NCOMB + ci:h * NCOMB + ci + 1]), S["mt"], 1)
            S["mt"].w(SC, MT0 + idx + 1)
            if idx >= 4:
                S["pvd"].w(SC, idx - 3)
            _op(SC, lambda e, t2=t2, pp=pp: e.activation(
                out=pbuf[:, pp * 512:(pp + 1) * 512],
                in_=t2buf[:, t2 * 512:(t2 + 1) * 512], func=AF.Exp, scale=CAP),
                S["mp"], 1)
        emit_pv(NIT - 3)
        emit_pv(NIT - 2)
        emit_pv(NIT - 1)
        emit_group_finalize(31)

        if "ctx" in dbg_t:
            S["mcf"].w(SY, 32)
            dbgs.emit(SY, lambda e: e.dma_start(out=dbg_t["ctx"], in_=ctx_fmT[:]))

        if ph <= 2:
            S["mcf"].w(SY, 32)
            S["ccA"].w(SY, 1)
            S["ccB"].w(SY, 1)
            dbgs.wait_all(SY)
            _flush(blk)
            esP2p.close(); esP2.close(); esAtt.close()
            return nc

        S["mcf"].w(TE, 32)
        ldw.wait_upto(TE, 4)
        n3 = 0
        for dk in range(16):
            for tg in range(8):
                par = n3 % 2
                sp4 = n3 % 4
                if n3 >= 4:
                    S["wo2"].w(SY, n3 - 3)
                xol.emit(SY, lambda e, dk=dk, tg=tg, sp4=sp4: e.dma_start(
                    out=xox[:, sp4 * 512:(sp4 + 1) * 512],
                    in_=xT_in[dk * 128:(dk + 1) * 128, tg * 512:(tg + 1) * 512]))
                if n3 >= 2:
                    S["wo2"].w(TE, n3 - 1)
                for hh in range(4):
                    _op(TE, lambda e, dk=dk, tg=tg, hh=hh, par=par: e.matmul(
                        ps_s[par][:],
                        wo_s[:, hh * D + dk * 128:hh * D + (dk + 1) * 128],
                        ctx_fmT[:, hh * T + tg * 512:hh * T + (tg + 1) * 512],
                        start=(hh == 0), stop=(hh == 3)),
                        S["wo1"] if hh == 3 else None, 1)
                S["wo1"].w(VE, n3 + 1)
                xol.wait_upto(VE, n3 + 1)
                if n3 >= 4:
                    sast.wait_upto(VE, n3 - 3)
                _op(VE, lambda e, par=par, sp4=sp4: e.scalar_tensor_tensor(
                    out=ast[:, sp4 * 512:(sp4 + 1) * 512],
                    in0=xox[:, sp4 * 512:(sp4 + 1) * 512], scalar=0.25,
                    in1=ps_s[par][:], op0=ALU.mult, op1=ALU.add),
                    S["wo2"], 1)
                S["wo2"].w(SC, n3 + 1)
                prow = (dk // 4) * 2048 + (tg // 2) * 512 + (dk % 4) * 128
                pcol = (tg % 2) * 512
                sast.emit(SC, lambda e, prow=prow, pcol=pcol, sp4=sp4: e.dma_start(
                    out=attnp_d[prow:prow + 128, pcol:pcol + 512],
                    in_=ast[:, sp4 * 512:(sp4 + 1) * 512]))
                n3 += 1
        qg2.wait_upto(GP, 32)
        _op(GP, lambda e: e.collective_compute(
            "AllGather", ALU.bypass, replica_groups=[list(range(8))],
            ins=[pshT_d.opt()], outs=[pagT_d.opt()]), S["ccB"], 1)
        for p in range(4):
            sast.wait_upto(GP, 32 * (p + 1))
            _op(GP, lambda e, p=p: e.collective_compute(
                "ReduceScatter", ALU.add, replica_groups=[[0, 1, 2, 3], [4, 5, 6, 7]],
                ins=[attnp_d[p * 2048:(p + 1) * 2048, :].opt()],
                outs=[attnr_d[p * 512:(p + 1) * 512, :].opt()]), S["rs"], 1)

        def w_fb2(q):
            S["mcf"].w(q, 32)
            S["wo1"].w(q, 128)
            S["wo2"].w(q, 128)

        if "attnr" in dbg_t:
            S["rs"].w(SY, 4)
            dbgs.emit(SY, lambda e: e.dma_start(out=dbg_t["attnr"], in_=attnr_d))

        if ph <= 3:
            S["rs"].w(SY, 4)
            dbgs.wait_all(SY)
            _flush(blk)
            esP2p.close(); esP2.close(); esAtt.close()
            return nc

        esP2p.close()
        esP2.close()
        esAtt.close()
        for q in (SY, SC, VE, GP):
            qg2.wait_all(q)

        esFFa = contextlib.ExitStack()
        xn_own = esFFa.enter_context(nc.sbuf_tensor("xn_own", [128, 16 * 1024], BF16))
        rstd_fm = esFFa.enter_context(nc.sbuf_tensor("rstd_fm", [128, 2 * 512], F32))
        rstt = esFFa.enter_context(nc.sbuf_tensor("rstt", [1, 2 * 512], F32))
        rsttr = esFFa.enter_context(nc.sbuf_tensor("rsttr", [1, 2 * 512], F32R))

        esX = contextlib.ExitStack()
        xw2 = esX.enter_context(nc.sbuf_tensor("xw2", [128, 4 * D], F32))
        xt2 = esX.enter_context(nc.sbuf_tensor("xt2", [128, 2 * D], BF16))
        l2sum = esX.enter_context(nc.sbuf_tensor("l2sum", [128, 4], F32))
        l2sq = esX.enter_context(nc.sbuf_tensor("l2sq", [128, 4], F32))
        l2mean = esX.enter_context(nc.sbuf_tensor("l2mean", [128, 4], F32))
        l2m2 = esX.enter_context(nc.sbuf_tensor("l2m2", [128, 4], F32))
        l2var = esX.enter_context(nc.sbuf_tensor("l2var", [128, 4], F32))
        l2rs = esX.enter_context(nc.sbuf_tensor("l2rs", [128, 4], F32))
        l2nmr = esX.enter_context(nc.sbuf_tensor("l2nmr", [128, 4], F32))
        l2junk = esX.enter_context(nc.sbuf_tensor("l2junk", [128, D], BF16))
        esXp = contextlib.ExitStack()
        ps_x2 = [esXp.enter_context(nc.psum_tensor(f"ps_x2{i}", [128, 1024], BF16))
                 for i in range(2)]

        w_fb2(SY)
        sast.wait_all(SY)
        w_fb2(VE)
        w_fb2(SC)
        for w in range(2):
            if w >= 1:
                S["xoap"].w(SY, w * 4)
            for i in range(4):
                t = 4 * w + i
                ldx.emit(SY, lambda e, i=i, t=t: e.dma_start(
                    out=xw2[:, i * D:(i + 1) * D], in_=xown_in[t * 128:(t + 1) * 128, :]))
            ldx.wait_upto(VE, 32 + 4 * (w + 1))
            if w >= 1:
                S["xoE"].w(VE, w)
            for i in range(4):
                _op(VE, lambda e, i=i: e.tensor_reduce(
                    out=l2sum[:, i:i + 1], in_=xw2[:, i * D:(i + 1) * D],
                    axis=AX.X, op=ALU.add), S["xoA"] if i == 3 else None, 1)
            S["xoA"].w(VE)
            _op(VE, lambda e: e.tensor_scalar(out=l2mean[:], in0=l2sum[:],
                scalar1=1.0 / D, scalar2=None, op0=ALU.mult), S["xoA"], 1)
            S["xoA"].w(VE)
            _op(VE, lambda e: e.tensor_tensor(out=l2m2[:], in0=l2mean[:],
                in1=l2mean[:], op=ALU.mult), S["xoA"], 1)
            ldx.wait_upto(SC, 32 + 4 * (w + 1))
            if w > 0:
                S["xoB"].w(SC)
            for i in range(4):
                if i > 0:
                    S["xoB"].w(SC)
                _op(SC, lambda e, i=i: e.activation(
                    out=l2junk[:], in_=xw2[:, i * D:(i + 1) * D], func=AF.Square,
                    accum_out=l2sq[:, i:i + 1]), S["xoB"], 1)
            S["xoB"].w(VE)
            S["xoA"].w(VE)
            _op(VE, lambda e: e.scalar_tensor_tensor(
                out=l2var[:], in0=l2sq[:], scalar=1.0 / D, in1=l2m2[:],
                op0=ALU.mult, op1=ALU.subtract), S["xoC"], 1)
            S["xoC"].w(SC)
            _op(SC, lambda e: e.activation(out=l2var[:], in_=l2var[:], func=AF.Ln,
                                           bias=epsln[:]), S["xoD"], 1)
            S["xoD"].w(SC)
            _op(SC, lambda e: e.activation(out=l2rs[:], in_=l2var[:], func=AF.Exp,
                                           scale=-0.5), S["xoD"], 1)
            S["xoD"].w(VE)
            _op(VE, lambda e: e.scalar_tensor_tensor(
                out=l2nmr[:], in0=l2mean[:], scalar=-1.0, in1=l2rs[:],
                op0=ALU.mult, op1=ALU.mult), S["xoE"], 1)
            S["xoE"].w(SC)
            for i in range(4):
                ti = 4 * w + i
                if ti >= 2:
                    S["xotr"].w(SC, (ti - 1) * 4)
                _op(SC, lambda e, i=i: e.activation(
                    out=xt2[:, (i % 2) * D:(i % 2 + 1) * D],
                    in_=xw2[:, i * D:(i + 1) * D], func=AF.Identity,
                    scale=l2rs[:, i:i + 1], bias=l2nmr[:, i:i + 1]), S["xoap"], 1)
            for i in range(4):
                ti = 4 * w + i
                S["xoap"].w(TE, ti + 1)
                for grp in range(4):
                    gi2 = ti * 4 + grp
                    gpar = gi2 % 2
                    if gi2 >= 2:
                        S["xotv"].w(TE, gi2 - 1)
                    for cc4 in range(4):
                        dc = grp * 4 + cc4
                        _op(TE, lambda e, i=i, dc=dc, gpar=gpar, cc4=cc4: e.transpose(
                            ps_x2[gpar][:, cc4 * 128:(cc4 + 1) * 128],
                            xt2[:, (i % 2) * D + dc * 128:(i % 2) * D + (dc + 1) * 128],
                            identb[:]), S["xotr"] if cc4 == 3 else None, 1)
            for i in range(4):
                ti = 4 * w + i
                for grp in range(4):
                    gi2 = ti * 4 + grp
                    gpar = gi2 % 2
                    S["xotr"].w(VE, gi2 + 1)
                    _op(VE, lambda e, ti=ti, grp=grp, gpar=gpar: e.tensor_copy(
                        out=xn_own.ap().rearrange("p (c m) -> p c m", c=16)[
                            :, grp * 4:(grp + 1) * 4, ti * 128:(ti + 1) * 128],
                        in_=ps_x2[gpar][:, 0:512].rearrange("p (c m) -> p c m", c=4)),
                        S["xotv"], 1)
        XO_TV = S["xotv"].v

        if "xn_own" in dbg_t:
            S["xotv"].w(SY, XO_TV)
            dbgs.emit(SY, lambda e: e.dma_start(out=dbg_t["xn_own"], in_=xn_own[:]))

        def w_fb3(q):
            S["xoap"].w(q, 8)
            S["xotr"].w(q, 32)
            S["xotv"].w(q, 32)

        esXp.close()
        esX.close()

        esF1 = contextlib.ExitStack()
        wgt_g = esF1.enter_context(nc.sbuf_tensor("wgt_g", [128, 2 * 16 * 1024], BF16))
        wgt_u = esF1.enter_context(nc.sbuf_tensor("wgt_u", [128, 2 * 16 * 1024], BF16))
        sgbuf = esF1.enter_context(nc.sbuf_tensor("sgbuf", [128, 2 * 512], F32))
        hstg = esF1.enter_context(nc.sbuf_tensor("hstg", [128, 2 * 512], BF16))
        h2stg = esF1.enter_context(nc.sbuf_tensor("h2stg", [128, 2 * 512], BF16))
        esF1p = contextlib.ExitStack()
        ps_g = [esF1p.enter_context(nc.psum_tensor(f"ps_g{i}", [128, 512], F32))
                for i in range(2)]
        ps_u = [esF1p.enter_context(nc.psum_tensor(f"ps_u{i}", [128, 512], F32))
                for i in range(2)]
        ps_r2 = [esF1p.enter_context(nc.psum_tensor(f"ps_r2{i}", [128, 512], F32))
                 for i in range(2)]

        NBLK = 8

        def emit_treads(bi):
            r, hb = bi // 2, bi % 2
            bpar = bi % 2
            if bi >= 2:
                S["ff1"].w(SY, (bi - 1) * 32)
            for dc in range(16):
                tg1.emit(SY, lambda e, bpar=bpar, dc=dc, r=r, hb=hb: e.dma_start(
                    out=wgt_g[:, (bpar * 16 + dc) * 1024:(bpar * 16 + dc + 1) * 1024],
                    in_=gagT_d[r * 2048 + dc * 128:r * 2048 + (dc + 1) * 128,
                               hb * 1024:(hb + 1) * 1024]))
                tg2.emit(SY, lambda e, bpar=bpar, dc=dc, r=r, hb=hb: e.dma_start(
                    out=wgt_u[:, (bpar * 16 + dc) * 1024:(bpar * 16 + dc + 1) * 1024],
                    in_=gagT_d[(r + 4) * 2048 + dc * 128:(r + 4) * 2048 + (dc + 1) * 128,
                               hb * 1024:(hb + 1) * 1024]))

        w_fb3(SY)
        S["ccA"].w(SY, 1)
        emit_treads(0)
        emit_treads(1)

        def emit_r2mm(mj):
            mjg, mth = mj // 2, mj % 2
            S["ffq"].w(TE, mj + 1)
            _op(TE, lambda e, par=mj % 2, th=mth,
                st=(mjg == 0), sp=(mjg == 63): e.matmul(
                ps_r2[th][0:1, 0:512], onesc[:],
                h2stg[:, par * 512:(par + 1) * 512],
                start=st, stop=sp), S["ffo"], 1)

        nj = 0
        for bi in range(NBLK):
            r, hb = bi // 2, bi % 2
            bpar = bi % 2
            tg1.wait_upto(TE, (bi + 1) * 16)
            tg2.wait_upto(TE, (bi + 1) * 16)
            if bi == 0:
                S["xotv"].w(TE, XO_TV)
            for jj in range(8):
                jg = r * 16 + hb * 8 + jj
                for th in range(2):
                    par = nj % 2
                    if nj >= 2:
                        S["ffs"].w(TE, nj - 1)
                        S["ffh"].w(TE, nj - 1)
                    for dc in range(16):
                        _op(TE, lambda e, bpar=bpar, dc=dc, jj=jj, th=th, par=par: e.matmul(
                            ps_g[par][:],
                            wgt_g[:, (bpar * 16 + dc) * 1024 + jj * 128:
                                  (bpar * 16 + dc) * 1024 + (jj + 1) * 128],
                            xn_own[:, dc * 1024 + th * 512:dc * 1024 + (th + 1) * 512],
                            start=(dc == 0), stop=(dc == 15)),
                            S["ff1"] if dc == 15 else None, 1)
                    for dc in range(16):
                        _op(TE, lambda e, bpar=bpar, dc=dc, jj=jj, th=th, par=par: e.matmul(
                            ps_u[par][:],
                            wgt_u[:, (bpar * 16 + dc) * 1024 + jj * 128:
                                  (bpar * 16 + dc) * 1024 + (jj + 1) * 128],
                            xn_own[:, dc * 1024 + th * 512:dc * 1024 + (th + 1) * 512],
                            start=(dc == 0), stop=(dc == 15)),
                            S["ff1"] if dc == 15 else None, 1)
                    S["ff1"].w(SC, 2 * nj + 1)
                    if nj >= 2:
                        S["ffh"].w(SC, nj - 1)
                    _op(SC, lambda e, par=par: e.activation(
                        out=sgbuf[:, par * 512:(par + 1) * 512], in_=ps_g[par][:],
                        func=AF.Silu), S["ffs"], 1)
                    S["ffs"].w(VE, nj + 1)
                    S["ff1"].w(VE, 2 * nj + 2)
                    if nj >= 2:
                        S["ffq"].w(VE, nj - 1)
                        hst.wait_upto(VE, nj - 1)
                    _op(VE, lambda e, par=par: e.tensor_tensor(
                        out=hstg[:, par * 512:(par + 1) * 512], in0=ps_u[par][:],
                        in1=sgbuf[:, par * 512:(par + 1) * 512], op=ALU.mult),
                        S["ffh"], 1)
                    S["ffh"].w(SC, nj + 1)
                    if nj >= 2:
                        S["ffo"].w(SC, nj - 1)
                    _op(SC, lambda e, par=par: e.activation(
                        out=h2stg[:, par * 512:(par + 1) * 512],
                        in_=hstg[:, par * 512:(par + 1) * 512], func=AF.Square),
                        S["ffq"], 1)
                    if nj >= 1:
                        emit_r2mm(nj - 1)
                    S["ffh"].w(SC, nj + 1)
                    hst.emit(SC, lambda e, jg=jg, th=th, par=par: e.dma_start(
                        out=h_d[jg * 128:(jg + 1) * 128, th * 512:(th + 1) * 512],
                        in_=hstg[:, par * 512:(par + 1) * 512]))
                    nj += 1
            if bi + 2 < NBLK:
                emit_treads(bi + 2)
        emit_r2mm(127)

        S["ffo"].w(VE, 128)
        for th in range(2):
            _op(VE, lambda e, th=th: e.tensor_scalar(
                out=rstt[:, th * 512:(th + 1) * 512], in0=ps_r2[th][0:1, 0:512],
                scalar1=1.0 / HFF, scalar2=None, op0=ALU.mult), S["rsA"], 1)
        S["rsA"].w(SC, 2)
        _op(SC, lambda e: e.activation(out=rstt[:], in_=rstt[:], func=AF.Ln,
                                       bias=epsrms[0:1, 0:1]), S["rsB"], 1)
        S["rsB"].w(SC)
        _op(SC, lambda e: e.activation(out=rstt[:], in_=rstt[:], func=AF.Exp,
                                       scale=-0.5), S["rsB"], 1)
        S["rsB"].w(VE)
        _op(VE, lambda e: e.tensor_copy(out=rsttr[:], in_=rstt[:]), S["rsB"], 1)
        if "h" in dbg_t:
            hst.wait_all(SY)
            dbgs.emit(SY, lambda e: e.dma_start(out=dbg_t["h"], in_=h_d))

        if ph <= 4:
            S["rsB"].w(SY, 3)
            S["rs"].w(SY, 4)
            dbgs.wait_all(SY)
            _flush(blk)
            esF1p.close(); esF1.close(); esFFa.close()
            return nc

        esF1p.close()
        esF1.close()

        esF2 = contextlib.ExitStack()
        h_buf = esF2.enter_context(nc.sbuf_tensor("h_buf", [128, 64 * 1024], BF16))
        wpt = esF2.enter_context(nc.sbuf_tensor("wpt", [128, 6 * 512], BF16))
        fstg = esF2.enter_context(nc.sbuf_tensor("fstg", [128, 2 * 512], F32))
        outstg = esF2.enter_context(nc.sbuf_tensor("outstg", [128, 4 * 512], F32))
        axa = esF2.enter_context(nc.sbuf_tensor("axa", [128, 4 * 512], F32))
        axx = esF2.enter_context(nc.sbuf_tensor("axx", [128, 4 * 512], F32))
        esF2p = contextlib.ExitStack()
        ps_rb = [esF2p.enter_context(nc.psum_tensor(f"ps_rb{i}", [128, 512], F32))
                 for i in range(2)]
        ps_o = [esF2p.enter_context(nc.psum_tensor(f"ps_o{i}", [128, 512], F32))
                for i in range(6)]

        S["rsB"].w(TE, 3)
        for th in range(2):
            _op(TE, lambda e, th=th: e.matmul(
                ps_rb[th][:], onesr[:], rsttr[:, th * 512:(th + 1) * 512],
                start=True, stop=True), S["rsC"], 1)
        S["rsC"].w(SC, 2)
        for th in range(2):
            _op(SC, lambda e, th=th: e.copy(
                out=rstd_fm[:, th * 512:(th + 1) * 512], in_=ps_rb[th][:]), S["rsC"], 1)
        RSC_ALL = S["rsC"].v

        S["ff1"].w(SY, 256)
        S["ffo"].w(SY, 128)
        for j2 in range(64):
            hst.wait_upto(SY, 2 * (j2 + 1))
            hld.emit(SY, lambda e, j2=j2: e.dma_start(
                out=h_buf[:, j2 * 1024:(j2 + 1) * 1024],
                in_=h_d[j2 * 128:(j2 + 1) * 128, :]))
        S["ccB"].w(SY, 1)
        S["rs"].w(SY, 4)
        nsw_l = 0

        def emit_out_stores(si):
            th_, dqi_ = si // 4, si % 4
            for dc in range(4):
                fin.emit(SY, lambda e, th=th_, dqi=dqi_, dc=dc: e.dma_start(
                    out=out_ext[dqi * 512 + dc * 128:dqi * 512 + (dc + 1) * 128,
                                th * 512:(th + 1) * 512],
                    in_=outstg[:, dc * 512:(dc + 1) * 512]))

        for th in range(2):
            for dqi in range(4):
                si = th * 4 + dqi
                for j2 in range(64):
                    par = nsw_l % 6
                    if nsw_l >= 6:
                        S["f2f"].w(SY, nsw_l - 5)
                    tp1.emit(SY, lambda e, j2=j2, dqi=dqi, par=par: e.dma_start(
                        out=wpt[:, par * 512:(par + 1) * 512],
                        in_=pagT_d[j2 * 128:(j2 + 1) * 128, dqi * 512:(dqi + 1) * 512]))
                    nsw_l += 1
                if si >= 1:
                    S["f2a"].w(SY, si * 4)
                    emit_out_stores(si - 1)
                for dc in range(4):
                    axl.emit(SY, lambda e, th=th, dqi=dqi, dc=dc: e.dma_start(
                        out=axa[:, dc * 512:(dc + 1) * 512],
                        in_=attnr_d[dqi * 512 + dc * 128:dqi * 512 + (dc + 1) * 128,
                                    th * 512:(th + 1) * 512]))
        S["f2a"].w(SY, 32)
        emit_out_stores(7)

        nsw = 0
        nev = 0
        ntr = 0
        for th in range(2):
            for dqi in range(4):
                si = th * 4 + dqi
                if si == 0:
                    S["rsC"].w(TE, RSC_ALL)
                for j2 in range(64):
                    par = nsw % 6
                    tp1.wait_upto(TE, nsw + 1)
                    if si == 0:
                        hld.wait_upto(TE, j2 + 1)
                    for dc in range(4):
                        if j2 == 0:
                            if dc >= 2 and si >= 1:
                                S["f2m"].w(TE, (si - 1) * 4 + dc - 1)
                            elif dc < 2 and si >= 2:
                                S["f2m"].w(TE, (si - 2) * 4 + dc + 3)
                        _op(TE, lambda e, par=par, bk=(si * 4 + dc) % 6, j2=j2, th=th, dc=dc: e.matmul(
                            ps_o[bk][:],
                            wpt[:, par * 512 + dc * 128:par * 512 + (dc + 1) * 128],
                            h_buf[:, j2 * 1024 + th * 512:j2 * 1024 + (th + 1) * 512],
                            start=(j2 == 0), stop=(j2 == 63)),
                            S["f2f"] if dc == 3 else None, 1)
                    nsw += 1
                for dc in range(4):
                    fpar = dc % 2
                    S["f2f"].w(VE, nsw)
                    if si >= 1 and dc == 0:
                        fin.wait_upto(VE, si * 4)
                    _op(VE, lambda e, bk=(si * 4 + dc) % 6, th=th, fpar=fpar: e.tensor_tensor(
                        out=fstg[:, fpar * 512:(fpar + 1) * 512], in0=ps_o[bk][:],
                        in1=rstd_fm[:, th * 512:(th + 1) * 512], op=ALU.mult),
                        S["f2m"], 1)
                    S["f2m"].w(VE)
                    _op(VE, lambda e, dqi=dqi, dc=dc, fpar=fpar: e.tensor_scalar(
                        out=fstg[:, fpar * 512:(fpar + 1) * 512],
                        in0=fstg[:, fpar * 512:(fpar + 1) * 512],
                        scalar1=bo_t[:, dqi * 4 + dc:dqi * 4 + dc + 1], scalar2=None,
                        op0=ALU.add), S["f2e"], 1)
                    S["f2e"].w(VE)
                    axl.wait_upto(VE, (si + 1) * 4)
                    _op(VE, lambda e, dc=dc, fpar=fpar: e.tensor_tensor(
                        out=outstg[:, dc * 512:(dc + 1) * 512],
                        in0=fstg[:, fpar * 512:(fpar + 1) * 512],
                        in1=axa[:, dc * 512:(dc + 1) * 512],
                        op=ALU.add), S["f2a"], 1)
                    nev += 1
        fin.wait_all(SY)
        dbgs.wait_all(SY)

        esF2p.close()
        esF2.close()
        esFFa.close()

        _flush(blk)

    return nc


def kernel(L0_attn_hidden_states, L0_attn_Wq, L0_attn_Wk, L0_attn_Wv,
           L0_attn_Wo, L0_attn_bo, L0_ffn_w_gate_up, L0_ffn_w_proj):
    x = np.ascontiguousarray(np.asarray(L0_attn_hidden_states, dtype=np.float32))
    Wq = np.ascontiguousarray(np.asarray(L0_attn_Wq, dtype=np.float32))
    Wk = np.ascontiguousarray(np.asarray(L0_attn_Wk, dtype=np.float32))
    Wv = np.ascontiguousarray(np.asarray(L0_attn_Wv, dtype=np.float32))
    Wo = np.ascontiguousarray(np.asarray(L0_attn_Wo, dtype=np.float32))
    bo = np.ascontiguousarray(np.asarray(L0_attn_bo, dtype=np.float32))
    wgu = np.ascontiguousarray(np.asarray(L0_ffn_w_gate_up, dtype=np.float32))
    wp = np.ascontiguousarray(np.asarray(L0_ffn_w_proj, dtype=np.float32))

    nc = build_nc()
    in_maps = []
    bo_tile = np.ascontiguousarray(bo.reshape(16, 128).T)
    for c in range(8):
        b, g = c // 4, c % 4
        m = dict(_host_consts(g))
        m["x"] = x[b]
        m["xT"] = np.ascontiguousarray(x[b].T)
        m["x_own"] = np.ascontiguousarray(x[b, g * OWN:(g + 1) * OWN])
        m["wq"] = np.ascontiguousarray(Wq[:, g * 512:(g + 1) * 512])
        m["wk"] = np.ascontiguousarray(Wk[:, g * 128:(g + 1) * 128])
        m["wv"] = np.ascontiguousarray(Wv[:, g * 128:(g + 1) * 128])
        m["wo"] = np.ascontiguousarray(Wo[g * 512:(g + 1) * 512, :])
        m["bo_t"] = bo_tile
        m["wgu_sh"] = np.ascontiguousarray(wgu[c * 2048:(c + 1) * 2048, :])
        m["wp_sh"] = np.ascontiguousarray(wp[:, c * 1024:(c + 1) * 1024])
        in_maps.append(m)

    res = run_bass_kernel_spmd(nc, in_maps, core_ids=list(range(8)))
    out = np.empty((B, T, D), np.float32)
    for c in range(8):
        b, g = c // 4, c % 4
        out[b, g * OWN:(g + 1) * OWN] = res.results[c]["out"].T
    return out


if __name__ == "__main__":
    print("building program...")
    nc = build_nc()
    print("instructions:", nc.n_instructions())
